# version 02
# baseline (speedup 1.0000x reference)
import math
from collections import defaultdict
from contextlib import ExitStack

import numpy as np
import ml_dtypes
import concourse.bass as bass
import concourse.mybir as mybir
from concourse.bass_utils import run_bass_kernel_spmd

F32 = mybir.dt.float32
BF16 = mybir.dt.bfloat16
AF = mybir.ActivationFunctionType
ALU = mybir.AluOpType
AX = mybir.AxisListType

T = 2048
D = 1024
NCORES = 8
NEG = -1.0e30

def _tobf(x):
    return np.asarray(x, dtype=ml_dtypes.bfloat16)

def host_constants():
    c = {}
    ar = np.arange(128)
    c["cst_id_bf"] = _tobf(np.eye(128, dtype=np.float32))
    c["cst_id_f32"] = np.eye(128, dtype=np.float32)
    c["cst_ones_col_bf"] = _tobf(np.ones((128, 1), np.float32))
    c["cst_ones_col_f32"] = np.ones((128, 1), np.float32)
    c["cst_ones_row128_f32"] = np.ones((1, 128), np.float32)
    c["cst_triu_bf"] = _tobf((ar[:, None] <= ar[None, :]).astype(np.float32))
    c["cst_slow_f32"] = (ar[:, None] > ar[None, :]).astype(np.float32)
    c["cst_amask_bf"] = _tobf(np.where(ar[None, :] <= ar[:, None], 0.0, NEG).astype(np.float32))
    c["cst_iota8_f32"] = np.broadcast_to(np.arange(8, dtype=np.float32), (128, 8)).copy()
    sw = np.zeros((128, 128), np.float32)
    sw[np.arange(128), np.arange(128) ^ 32] = 1.0
    c["cst_swap64_bf"] = _tobf(sw)
    inv = 1.0 / (1.0e5 ** (np.arange(0, 64, 2, dtype=np.float32) / 64.0))
    f = np.arange(T, dtype=np.float32)[None, :] * inv[:, None]
    sgn2 = np.where(np.arange(128)[:, None] % 64 < 32, -1.0, 1.0).astype(np.float32)
    c["tab_u2_cos"] = _tobf(np.tile(np.cos(f), (4, 1)))
    c["tab_u2_sins"] = _tobf(np.tile(np.sin(f), (4, 1)) * sgn2)
    inv7 = 1.0 / (1.0e4 ** (np.arange(32, dtype=np.float32) / 32.0))
    f7 = np.arange(T, dtype=np.float32)[None, :] * inv7[:, None]
    c["tab_u7_cos"] = _tobf(np.tile(np.cos(f7), (4, 1)))
    sgn = np.where(np.arange(128)[:, None] % 64 < 32, -1.0, 1.0).astype(np.float32)
    c["tab_u7_sins"] = _tobf(np.tile(np.sin(f7), (4, 1)) * sgn)
    inv3 = 1.0 / (1.0e4 ** (np.arange(0, 32, 2, dtype=np.float32) / 32.0))
    ang3 = np.arange(T, dtype=np.float32)[None, :] * inv3[:, None]
    r = np.arange(128)
    rr = r % 64
    cos_rows = np.ones((128, T), np.float32)
    sin_rows = np.zeros((128, T), np.float32)
    rot = rr < 32
    cos_rows[rot] = np.cos(ang3)[(rr[rot] // 2)]
    sgn3 = np.where(r % 2 == 0, -1.0, 1.0)
    sin_rows[rot] = np.sin(ang3)[(rr[rot] // 2)] * sgn3[rot, None]
    c["tab_u3_cosfm"] = _tobf(cos_rows)
    c["tab_u3_sinsfm"] = _tobf(sin_rows)
    swx = np.zeros((128, 128), np.float32)
    dst = np.where(rr < 32, r ^ 1, r)
    swx[r, dst] = 1.0
    c["cst_swapx_bf"] = _tobf(swx)
    delta = np.arange(-2047, 2048)
    n = -delta
    ret = (n < 0).astype(np.int64) * 16
    na = np.abs(n)
    vil = 8 + (np.log(np.maximum(na, 1).astype(np.float32) / 8.0) / math.log(16.0) * 8.0).astype(np.int32)
    vil = np.minimum(vil, 15)
    bucket = ret + np.where(na < 8, na, vil)
    R = np.zeros((32, 4095), np.float32)
    R[bucket, np.arange(4095)] = 1.0
    c["tab_u5_R"] = _tobf(R)
    return c

NORM_DBG = None
_HOSTC = None
def hostc():
    global _HOSTC
    if _HOSTC is None:
        _HOSTC = host_constants()
    return _HOSTC

ENGINES = ("tensor", "vector", "scalar", "gpsimd", "sync")

class B:
    def __init__(self, nc):
        self.nc = nc
        self.q = {e: [] for e in ENGINES}
        self.semh = {}
        self.cnt = defaultdict(int)
        self.lastwait = {e: defaultdict(int) for e in ENGINES}

    def on(self, eng, fn):
        self.q[eng].append(fn)

    def tok(self, sem):
        return (sem, self.cnt[sem])

    def wait(self, eng, tok):
        if tok is None:
            return
        sem, val = tok
        if val <= 0 or self.lastwait[eng][sem] >= val:
            return
        assert val <= self.cnt[sem], f"wait on future token {sem}={val}>{self.cnt[sem]}"
        self.lastwait[eng][sem] = val
        self.on(eng, lambda e, s=sem, v=val: e.wait_ge(self.semh[s], v))

    def emit(self, eng, fn, waits=(), inc=None, inc_n=1):
        for t in waits:
            self.wait(eng, t)
        if inc is None:
            self.on(eng, fn)
            return None
        self.cnt[inc] += inc_n
        tok = (inc, self.cnt[inc])
        self.on(eng, lambda e, f=fn, s=inc, n=inc_n: f(e).then_inc(self.semh[s], n))
        return tok

    def dma(self, queue, out_ap, in_ap, sem=None, waits=(), **kw):
        sem = sem or ("dS" if queue == "sync" else "dG")
        return self.emit(queue, lambda e: e.dma_start(out_ap, in_ap, **kw),
                         waits=waits, inc=sem, inc_n=16)

    def finalize(self, stack):
        for name in list(self.cnt):
            self.semh[name] = stack.enter_context(self.nc.semaphore(name))
        block = stack.enter_context(self.nc.Block())
        for eng in ENGINES:
            if not self.q[eng]:
                continue
            def body(E, eng=eng):
                for f in self.q[eng]:
                    f(E)
            getattr(block, eng)(body)


class PsumPool:
    def __init__(self, b, stack, n_f32=6):
        self.b = b
        self.banks = []
        for i in range(n_f32):
            t = stack.enter_context(b.nc.psum_tensor(f"psum{i}", [128, 512], F32))
            self.banks.append({"t": t, "free": None})
        self.idx = 0
        self.stats = stack.enter_context(b.nc.psum_tensor("psum_stats", [128, 512], F32))
        self.stats_free = None
        self.tr = stack.enter_context(b.nc.psum_tensor("psum_tr", [128, 512], BF16))
        self.tr_free = None

    def acquire(self):
        bk = self.banks[self.idx]
        self.idx = (self.idx + 1) % len(self.banks)
        self.b.wait("tensor", bk["free"])
        return bk

    def hold(self):
        bk = self.banks.pop(self.idx % len(self.banks))
        self.idx = self.idx % max(1, len(self.banks))
        self.b.wait("tensor", bk["free"])
        return bk

    def release(self, bk, tok):
        bk["free"] = tok
        self.banks.insert(self.idx, bk)


class Ctx:
    pass

def fms(buf, kt, t0, tw, Tb=T):
    return buf[:, kt * Tb + t0: kt * Tb + t0 + tw]

def unit_barrier(b, C):
    toks = [b.tok(s) for s in ("pe", "dve", "act", "pool") if b.cnt[s]]
    dtoks = [b.tok(s) for s in list(b.cnt)
             if s.startswith("d") and s != "dCv" and b.cnt[s]]
    for eng in ("tensor", "vector", "scalar", "gpsimd", "sync"):
        for t in toks + dtoks:
            b.wait(eng, t)
    mk = {}
    mk["pe"] = b.emit("tensor", lambda e: e.nop(), inc="pe")
    mk["dve"] = b.emit("vector", lambda e: e.engine_nop(), inc="dve")
    mk["act"] = b.emit("scalar", lambda e: e.nop(), inc="act")
    mk["pool"] = b.emit("gpsimd", lambda e: e.engine_nop(), inc="pool")
    C.ps.stats_free = None
    C.ps.tr_free = None
    for bk in C.ps.banks:
        bk["free"] = None
    C.wslot_tok = [None, None]
    C.gel_free = [None, None]
    C.rowA_free = None
    C.rowB_free = None
    C.tmp32_free = None
    C.xres_pool_tok = None
    C.cb_guard = None
    for nm in ("convtmp_free", "dts_free", "E_free", "eL_free", "mstd_free",
               "xstok_free", "cd_free", "bw_free", "zs_free", "t64_free",
               "rhsh_free", "G_wtok", "vstage_free", "rope_sw_free", "csb_free",
               "kp_free", "vp_free", "S_free", "q_tok"):
        setattr(C, nm, None)
    return mk["dve"]

def bcast_row(b, C, row_ap, bank, waits=()):
    tw = row_ap.shape[-1]
    return b.emit("tensor",
                  lambda e: e.matmul(bank["t"][0:128, 0:tw], C.ones_row128[0:1, :],
                                     row_ap, start=True, stop=True),
                  waits=waits, inc="pe")

def gemm(b, C, *, w, Cout, rhs, evac, jblock=512, tchunks=None, wsem="dW",
         wqueue="sync", wwait=None, extra_stop=None):
    Cin = w.shape[0]
    Kt = Cin // 128
    assert Cin % 128 == 0 and Kt * jblock <= 4096
    tchunks = tchunks or [(t0, 512) for t0 in range(0, T, 512)]
    nblocks = (Cout + jblock - 1) // jblock
    for ib in range(nblocks):
        j0 = ib * jblock
        jw = min(jblock, Cout - j0)
        si = C.wslot_idx
        C.wslot_idx = 1 - C.wslot_idx
        dst = C.wslot[si]
        in_ap = w[:, j0:j0 + jw].rearrange("(kt p) j -> p kt j", p=128)
        out_ap = dst[:, 0:Kt * jw].rearrange("p (kt j) -> p kt j", kt=Kt)
        wtok = b.dma(wqueue, out_ap, in_ap, wsem,
                     waits=[x for x in (C.wslot_tok[si], wwait) if x])
        last_pe = None
        for jj in range(0, jw, 128):
            j = (j0 + jj) // 128
            mw = min(128, jw - jj)
            for (t0, tw) in tchunks:
                bk = C.ps.acquire()
                b.wait("tensor", wtok)
                for kt in range(Kt):
                    r, rtok = rhs(kt, t0, tw)
                    if rtok is not None:
                        b.wait("tensor", rtok)
                    stop = (kt == Kt - 1) and extra_stop is None
                    tokm = b.emit("tensor",
                                  lambda e, o=bk["t"][0:mw, 0:tw],
                                         l=dst[:, kt * jw + jj: kt * jw + jj + mw],
                                         r=r, s=(kt == 0), p=stop:
                                      e.matmul(o, l, r, start=s, stop=p),
                                  inc="pe" if stop else None)
                if extra_stop is not None:
                    tokm = extra_stop(j, mw, bk, t0, tw)
                last_pe = tokm
                bk["free"] = evac(j, t0, tw, bk, tokm, mw)
        C.wslot_tok[si] = last_pe


def emit_norm(b, C, *, src, dst, nkt, eps, mean, gamma_col=None, gamma_off=0,
              Tb=T, t_range=None, src_tok=None, src_is_bf=False):
    nF = nkt * 128
    cp = [src_tok] * nkt
    if not src_is_bf:
        for kt in range(nkt):
            cp[kt] = b.emit("vector",
                            lambda e, s=fms(src, kt, 0, Tb, Tb), d=fms(dst, kt, 0, Tb, Tb):
                                e.tensor_copy(d, s),
                            waits=[src_tok] if src_tok else [], inc="dve")
        stat_src = dst
    else:
        stat_src = src
    lastdve = None
    for t0, tw in (t_range or [(x, 512) for x in range(0, Tb, 512)]):
        b.wait("tensor", C.ps.stats_free)
        st_tok = None
        for kt in range(nkt):
            si = kt % 2
            sqt = b.emit("scalar",
                         lambda e, s=fms(stat_src, kt, t0, tw, Tb),
                                d=C.gel[:, si * 512: si * 512 + tw]:
                             e.activation(d, s, AF.Square),
                         waits=[cp[kt], C.gel_free[si]], inc="act")
            if mean:
                b.emit("tensor",
                       lambda e, r=fms(stat_src, kt, t0, tw, Tb),
                              o=C.ps.stats[0:1, 0:tw], s=(kt == 0), p=(kt == nkt - 1):
                           e.matmul(o, C.ones_col_bf[:, :], r, start=s, stop=p),
                       waits=[cp[kt]], inc="pe")
            st_tok = b.emit("tensor",
                            lambda e, r=C.gel[:, si * 512: si * 512 + tw],
                                   o=C.ps.stats[32:33, 0:tw], s=(kt == 0), p=(kt == nkt - 1):
                                e.matmul(o, C.ones_col_bf[:, :], r, start=s, stop=p),
                            waits=[sqt], inc="pe")
            C.gel_free[si] = st_tok
        if mean:
            t_nm = b.emit("vector",
                          lambda e, w=tw: e.tensor_scalar(C.rowA[0:1, 0:w], C.ps.stats[0:1, 0:w],
                                                          -1.0 / nF, None, op0=ALU.mult),
                          waits=[st_tok, C.rowA_free], inc="dve")
            t_ms = b.emit("scalar",
                          lambda e, w=tw: e.activation(C.rowB[0:1, 0:w], C.rowA[0:1, 0:w], AF.Square),
                          waits=[t_nm, C.rowB_free], inc="act")
            t_var = b.emit("vector",
                           lambda e, w=tw: e.scalar_tensor_tensor(C.rowB[0:1, 0:w], C.ps.stats[32:33, 0:w],
                                                                  1.0 / nF, C.rowB[0:1, 0:w],
                                                                  op0=ALU.mult, op1=ALU.subtract),
                           waits=[st_tok, t_ms], inc="dve")
        else:
            t_var = b.emit("vector",
                           lambda e, w=tw: e.tensor_scalar(C.rowB[0:1, 0:w], C.ps.stats[32:33, 0:w],
                                                           1.0 / nF, None, op0=ALU.mult),
                           waits=[st_tok, C.rowB_free], inc="dve")
        C.ps.stats_free = t_var
        eap, etok = eps_ap(b, C, eps)
        t_ln = b.emit("scalar",
                      lambda e, w=tw, a=eap[0:1, :]: e.activation(C.rowB[0:1, 0:w], C.rowB[0:1, 0:w], AF.Ln, bias=a),
                      waits=[t_var, etok], inc="act")
        t_rs = b.emit("scalar",
                      lambda e, w=tw: e.activation(C.rowB[0:1, 0:w], C.rowB[0:1, 0:w], AF.Exp, scale=-0.5),
                      waits=[t_ln], inc="act")
        bkM = None
        if mean:
            bkM = C.ps.acquire()
            tM = bcast_row(b, C, C.rowA[0:1, 0:tw], bkM, waits=[t_nm])
            C.rowA_free = tM
        bkR = C.ps.acquire()
        tR = bcast_row(b, C, C.rowB[0:1, 0:tw], bkR, waits=[t_rs])
        C.rowB_free = tR
        if NORM_DBG is not None:
            ch = t0 // 512
            if mean:
                C.rowA_free = b.dma("sync", NORM_DBG[0:1, ch * 1024: ch * 1024 + tw],
                                    C.rowA[0:1, 0:tw], "dO", waits=[t_nm, tM])
            C.rowB_free = b.dma("sync", NORM_DBG[0:1, ch * 1024 + 512: ch * 1024 + 512 + tw],
                                C.rowB[0:1, 0:tw], "dO", waits=[t_rs, tR])
        for kt in range(nkt):
            if mean:
                ta = b.emit("vector",
                            lambda e, s=fms(src, kt, t0, tw, Tb), d=fms(dst, kt, t0, tw, Tb), w=tw,
                                   bm=bkM["t"]:
                                e.tensor_tensor(d, s, bm[0:128, 0:w], op=ALU.add),
                            waits=[tM], inc="dve")
                lastdve = b.emit("vector",
                                 lambda e, d=fms(dst, kt, t0, tw, Tb), w=tw, br=bkR["t"]:
                                     e.tensor_tensor(d, d, br[0:128, 0:w], op=ALU.mult),
                                 waits=[tR, ta], inc="dve")
            elif gamma_col is not None:
                lastdve = b.emit("vector",
                                 lambda e, s=fms(src, kt, t0, tw, Tb), d=fms(dst, kt, t0, tw, Tb),
                                        g=gamma_col[:, gamma_off + kt: gamma_off + kt + 1], w=tw,
                                        br=bkR["t"]:
                                     e.scalar_tensor_tensor(d, s, g, br[0:128, 0:w],
                                                            op0=ALU.mult, op1=ALU.mult),
                                 waits=[tR], inc="dve")
            else:
                lastdve = b.emit("vector",
                                 lambda e, s=fms(src, kt, t0, tw, Tb), d=fms(dst, kt, t0, tw, Tb), w=tw,
                                        br=bkR["t"]:
                                     e.tensor_tensor(d, s, br[0:128, 0:w], op=ALU.mult),
                                 waits=[tR], inc="dve")
        if mean:
            bkM["free"] = lastdve
        bkR["free"] = lastdve
    return lastdve

def convert_weight(b, C, src_ap, dst_t, transpose=False, wtmp_t=None):
    rows, cols = src_ap.shape
    if not transpose:
        tok = None
        step = max(128, ((4 << 20) // max(cols, 1)) & ~127)
        r = 0
        while r < rows:
            rr = min(step, rows - r)
            tok = b.dma("gpsimd", dst_t.ap()[r:r + rr, :], src_ap[r:r + rr, :], "dCv")
            r += rr
        return tok
    rows128 = (rows // 128) * 128
    ctok = None
    step = max(128, ((4 << 20) // max(cols, 1)) & ~127)
    r = 0
    while r < rows:
        rr = min(step, rows - r)
        ctok = b.dma("gpsimd", wtmp_t.ap()[r:r + rr, :], src_ap[r:r + rr, :], "dCv")
        r += rr
    tok = None
    for c0 in range(0, cols, 128):
        si = C.wslot_idx
        C.wslot_idx = 1 - C.wslot_idx
        dst_sb = C.wslot[si]
        assert rows128 <= 4096
        ttok = b.dma("sync", dst_sb[:, 0:rows128], wtmp_t.ap()[0:rows128, c0:c0 + 128],
                     "dTr", waits=[x for x in (ctok, C.wslot_tok[si]) if x], transpose=True)
        tok = b.dma("sync", dst_t.ap()[c0:c0 + 128, 0:rows128], dst_sb[:, 0:rows128],
                    "dTr", waits=[ttok])
        C.wslot_tok[si] = ("dTr", tok[1])
    return tok

def make_ctx(b, stack):
    nc = b.nc
    C = Ctx()
    C.ps = PsumPool(b, stack)
    A = lambda name, shape, dt: stack.enter_context(nc.sbuf_tensor(name, shape, dt))
    C.xres = A("xres", [128, 8 * T], F32)
    C.xn = A("xn", [128, 8 * T], BF16)
    C.ab2 = A("ab2", [128, 8 * T], BF16)
    C.ab3 = A("ab3", [128, 8 * T], BF16)
    C.cb = A("cb", [128, 4608], BF16)
    C.cbf = A("cbf", [128, 2048], F32)
    C.wslot = [A("wslot0", [128, 4096], BF16), A("wslot1", [128, 4096], BF16)]
    C.wslot_idx = 0
    C.wslot_tok = [None, None]
    C.gel = A("gel", [128, 1024], BF16)
    C.gel_free = [None, None]
    C.rowA = A("rowA", [1, 512], F32)
    C.rowB = A("rowB", [1, 512], F32)
    C.rowA_free = None
    C.rowB_free = None
    C.tmp32 = A("tmp32", [128, 512], F32)
    C.tmp32_free = None
    C.colbuf = A("colbuf", [128, 64], F32)
    C.router = A("router", [128, 256], F32)
    C.gcols = A("gcols", [128, 32], F32)
    C.bcols = A("bcols", [128, 576], F32)
    C.wrsb = A("wrsb", [128, 64], BF16)
    C.xres_pool_tok = None
    C.cb_guard = None
    for nm in ("convtmp_free", "dts_free", "E_free", "eL_free", "mstd_free",
               "xstok_free", "cd_free", "bw_free", "zs_free", "t64_free",
               "rhsh_free", "G_wtok", "vstage_free", "ghid_tok", "rope_sw_free", "csb_free",
               "kp_free", "vp_free", "S_free", "q_tok"):
        setattr(C, nm, None)
    C.epsmap = {}
    C.id_bf = A("id_bf", [128, 128], BF16)
    C.id_f32 = A("id_f32", [128, 128], F32)
    C.ones_col_bf = A("ones_col_bf", [128, 1], BF16)
    C.ones_col_f32 = A("ones_col_f32", [128, 1], F32)
    C.ones_row128 = A("ones_row128", [1, 128], F32)
    C.triu = A("triu", [128, 128], BF16)
    C.slow = A("slow", [128, 128], F32)
    C.amask = A("amask", [128, 128], BF16)
    C.iota8 = A("iota8", [128, 8], F32)
    C.swap64 = A("swap64", [128, 128], BF16)
    C.swapx = A("swapx", [128, 128], BF16)
    return C

CONST_SPECS = [
    ("cst_id_bf", "id_bf"), ("cst_id_f32", "id_f32"),
    ("cst_ones_col_bf", "ones_col_bf"), ("cst_ones_col_f32", "ones_col_f32"),
    ("cst_ones_row128_f32", "ones_row128"),
    ("cst_triu_bf", "triu"), ("cst_slow_f32", "slow"),
    ("cst_amask_bf", "amask"), ("cst_iota8_f32", "iota8"),
    ("cst_swap64_bf", "swap64"), ("cst_swapx_bf", "swapx"),
]

def load_cols(b, C, src1d, njt, dst, waits=()):
    assert njt <= 64
    s1 = b.dma("sync", C.tmp32[0:njt, 0:128], src1d.rearrange("(a p) -> a p", p=128),
               "dC", waits=[C.tmp32_free] + list(waits) if C.tmp32_free else list(waits))
    bk = C.ps.acquire()
    tt = b.emit("tensor",
                lambda e, o=bk["t"][0:128, 0:njt], s=C.tmp32[0:njt, 0:128], n=njt:
                    e.transpose(o, s, C.id_f32[0:n, 0:n]),
                waits=[s1], inc="pe")
    C.tmp32_free = tt
    ev = b.emit("vector", lambda e, o=dst, s=bk["t"][0:128, 0:njt]: e.tensor_copy(o, s),
                waits=[tt], inc="dve")
    bk["free"] = ev
    return ev

def eps_ap(b, C, eps):
    if eps not in C.epsmap:
        slot = 56 + len(C.epsmap)
        tok = b.emit("gpsimd", lambda e, s=slot, v=eps: e.memset(C.colbuf[:, s:s + 1], v),
                     inc="pool")
        C.epsmap[eps] = (C.colbuf[:, slot:slot + 1], tok)
    return C.epsmap[eps]

def load_consts(b, C, inp):
    for name, attr in CONST_SPECS:
        b.dma("sync", getattr(C, attr)[:, :], inp[name], "dC")
    return b.tok("dC")

GELU_C = 0.044715
GELU_SQ = math.sqrt(GELU_C)
GELU_S = 2.0 * math.sqrt(2.0 / math.pi)

def ghid_ap(C, j, toff, tw):
    g = j * 1024 + toff
    if j < 16:
        return C.ab2[:, g: g + tw]
    return C.ab3[:, g - 16384: g - 16384 + tw]

def emit_u6(b, C, W, in_tok=None):
    ntok = emit_norm(b, C, src=C.xres, dst=C.xn, nkt=8, eps=1e-5, mean=True,
                     src_tok=in_tok)
    wrc = b.dma("gpsimd", W["wrbf"].ap()[:, :], W["wr_f32"], "dCv")
    wr_tok = b.dma("sync", C.wrsb[:, 0:64].rearrange("p (kt j) -> p kt j", kt=8),
                   W["wrbf"].ap().rearrange("(kt p) j -> p kt j", p=128), "dW",
                   waits=[wrc])
    lgt = C.router[:, 0:128]
    wf = C.router[:, 128:256]
    for tt4 in range(0, 16, 4):
        bk = C.ps.acquire()
        for ti in range(tt4, tt4 + 4):
            mt = None
            for kt in range(8):
                mt = b.emit("tensor",
                            lambda e, o=bk["t"][0:128, (ti % 4) * 8:(ti % 4) * 8 + 8],
                                   l=fms(C.xn, kt, ti * 128, 128),
                                   r=C.wrsb[:, kt * 8:(kt + 1) * 8],
                                   s=(kt == 0), p=(kt == 7):
                                e.matmul(o, l, r, start=s, stop=p),
                            waits=[wr_tok, ntok], inc="pe" if kt == 7 else None)
        ev = b.emit("vector",
                    lambda e, o=lgt[:, tt4 * 8: tt4 * 8 + 32], s=bk["t"][0:128, 0:32]:
                        e.tensor_copy(o, s),
                    waits=[mt], inc="dve")
        bk["free"] = ev
    lgt_tok = b.tok("dve")
    if DEBUG_DUMPS:
        b.emit("vector", lambda e: e.tensor_copy(C.gcols[:, 0:32], lgt[:, 0:32]),
               waits=[lgt_tok], inc="dve")
        b.emit("vector", lambda e: e.tensor_copy(C.wrsb[:, 0:64], lgt[:, 64:128]),
               inc="dve")
    for tt in range(16):
        lg = lgt[:, tt * 8:(tt + 1) * 8]
        nm = b.emit("vector", lambda e, s=lg: e.reduce_max(C.colbuf[:, 0:1], s, axis=AX.X, negate=True),
                    waits=[lgt_tok], inc="dve")
        ex = b.emit("scalar",
                    lambda e, s=lg: e.activation(s, s, AF.Exp, bias=C.colbuf[:, 0:1],
                                                 accum_out=C.colbuf[:, 1:2]),
                    waits=[nm], inc="act")
        rc = b.emit("vector", lambda e: e.reciprocal(C.colbuf[:, 2:3], C.colbuf[:, 1:2]),
                    waits=[ex], inc="dve")
        pn = b.emit("vector",
               lambda e, s=lg: e.tensor_scalar(s, s, C.colbuf[:, 2:3], None, op0=ALU.mult),
               waits=[rc], inc="dve")
        m1 = b.emit("vector", lambda e, s=lg: e.reduce_max(C.colbuf[:, 3:4], s, axis=AX.X),
                    waits=[pn], inc="dve")
        s1 = b.emit("vector",
                    lambda e, s=lg: e.tensor_scalar(C.colbuf[:, 16:24], s, C.colbuf[:, 3:4], None,
                                                    op0=ALU.is_lt),
                    waits=[m1], inc="dve")
        s2 = b.emit("vector",
                    lambda e, s=lg: e.tensor_tensor(C.colbuf[:, 24:32], C.colbuf[:, 16:24], s, op=ALU.mult),
                    waits=[s1], inc="dve")
        v2 = b.emit("vector", lambda e: e.reduce_max(C.colbuf[:, 4:5], C.colbuf[:, 24:32], axis=AX.X),
                    waits=[s2], inc="dve")
        s3 = b.emit("vector",
                    lambda e, s=lg: e.tensor_scalar(C.colbuf[:, 16:24], s, C.colbuf[:, 4:5], None,
                                                    op0=ALU.is_ge),
                    waits=[v2], inc="dve")
        b.emit("vector",
               lambda e, s=lg, o=wf[:, tt * 8:(tt + 1) * 8]:
                   e.tensor_tensor(o, C.colbuf[:, 16:24], s, op=ALU.mult),
               waits=[s3], inc="dve")
    stage_free = None
    for tt in range(16):
        bk2 = C.ps.acquire()
        tw_ = b.emit("tensor",
                     lambda e, o=bk2["t"][0:8, 0:128], s=wf[:, tt * 8:(tt + 1) * 8]:
                         e.transpose(o, s, C.id_f32[:, :]),
                     waits=[b.tok("dve")], inc="pe")
        ev = b.emit("vector",
                    lambda e, s=bk2["t"][0:8, 0:128]: e.tensor_copy(C.tmp32[32:40, 0:128], s),
                    waits=[tw_, stage_free], inc="dve")
        bk2["free"] = ev
        stage_free = b.dma("sync", W["wrows"].ap()[0:8, tt * 128:(tt + 1) * 128],
                           C.tmp32[32:40, 0:128], "dRW", waits=[ev])
    C.tmp32_free = stage_free
    t_b = None
    for e_ in range(8):
        t_b = load_cols(b, C, W["b1_f32"][e_, :], 32, C.bcols[:, e_ * 32:(e_ + 1) * 32])
        t_b = load_cols(b, C, W["b2_f32"][e_, :], 8, C.bcols[:, 512 + e_ * 8: 512 + (e_ + 1) * 8])
    b.emit("vector",
           lambda e: e.tensor_scalar(C.bcols[:, 256:512], C.bcols[:, 0:256],
                                     GELU_SQ, None, op0=ALU.mult),
           waits=[t_b], inc="dve")
    t_bias = b.tok("dve")
    wbsb = C.cb[:, 0:2048]
    wb_tok = None
    for e_ in range(8):
        for tc in range(0, T, 512):
            rtok = b.dma("gpsimd", C.rowA[0:1, 0:512], W["wrows"].ap()[e_:e_ + 1, tc:tc + 512],
                         "dRr", waits=[b.tok("dRW"), C.rowA_free])
            bkw = C.ps.acquire()
            tb = bcast_row(b, C, C.rowA[0:1, 0:512], bkw, waits=[rtok])
            C.rowA_free = tb
            wb_tok = b.emit("vector",
                            lambda e, o=wbsb[:, tc:tc + 512], s=bkw["t"][0:128, 0:512]:
                                e.tensor_copy(o, s),
                            waits=[tb, wb_tok], inc="dve")
            bkw["free"] = wb_tok
        wb_ready = wb_tok
        for tc in (0, 1024):
            def gelu_evac(j, t0, tw, bank, mmtok, mw, e_=e_, tc=tc):
                ps = bank["t"][0:128, 0:tw]
                b1c = C.bcols[:, e_ * 32 + j: e_ * 32 + j + 1]
                b1s = C.bcols[:, 256 + e_ * 32 + j: 256 + e_ * 32 + j + 1]
                gs = C.gel[:, 0:512]
                gu = C.gel[:, 512:1024]
                a_ = b.emit("scalar",
                            lambda e: e.activation(gs[:, 0:tw], ps, AF.Square, bias=b1s, scale=GELU_SQ),
                            waits=[mmtok, t_bias, C.gel_free[0]], inc="act")
                v_ = b.emit("vector",
                            lambda e: e.tensor_scalar(gs[:, 0:tw], gs[:, 0:tw], 1.0, None, op0=ALU.add),
                            waits=[a_], inc="dve")
                u_ = b.emit("vector",
                            lambda e: e.scalar_tensor_tensor(gu[:, 0:tw], ps, b1c, gs[:, 0:tw],
                                                             op0=ALU.add, op1=ALU.mult),
                            waits=[mmtok, v_, C.gel_free[1]], inc="dve")
                C.gel_free[0] = u_
                g_ = b.emit("scalar",
                            lambda e: e.activation(gu[:, 0:tw], gu[:, 0:tw], AF.Sigmoid, scale=GELU_S),
                            waits=[u_], inc="act")
                o_ = b.emit("vector",
                            lambda e, d=ghid_ap(C, j, t0 - tc, tw):
                                e.scalar_tensor_tensor(d, ps, b1c, gu[:, 0:tw], op0=ALU.add, op1=ALU.mult),
                            waits=[g_], inc="dve")
                C.gel_free[1] = o_
                C.ghid_tok = o_
                return o_
            gemm(b, C, w=W["w1bf"].ap()[e_ * 1024:(e_ + 1) * 1024, :], Cout=4096,
                 rhs=lambda kt, t0, tw: (fms(C.xn, kt, t0, tw), None),
                 evac=gelu_evac, jblock=512,
                 tchunks=[(tc, 512), (tc + 512, 512)], wwait=W.get("tok_w1"))
            def out_evac(jo, t0, tw, bank, mmtok, mw, e_=e_):
                ps = bank["t"][0:mw, 0:tw]
                b2c = C.bcols[:, 512 + e_ * 8 + jo: 512 + e_ * 8 + jo + 1]
                tv = b.emit("vector",
                            lambda e: e.scalar_tensor_tensor(C.tmp32[0:mw, 0:tw], ps, b2c,
                                                             wbsb[0:mw, t0:t0 + tw],
                                                             op0=ALU.add, op1=ALU.mult),
                            waits=[mmtok, wb_ready, t_bias, C.tmp32_free], inc="dve")
                ax = b.emit("gpsimd",
                            lambda e, d=fms(C.xres, jo, t0, tw):
                                e.tensor_tensor(d, d, C.tmp32[0:mw, 0:tw], op=ALU.add),
                            waits=[tv, C.xres_pool_tok], inc="pool")
                C.tmp32_free = ax
                C.xres_pool_tok = ax
                return tv
            gemm(b, C, w=W["w2bf"].ap()[e_ * 4096:(e_ + 1) * 4096, :], Cout=1024,
                 rhs=lambda kt, t0, tw, tc=tc: (ghid_ap(C, kt, t0 - tc, tw), C.ghid_tok),
                 evac=out_evac, jblock=128,
                 tchunks=[(tc, 512), (tc + 512, 512)], wwait=W.get("tok_w2"))
    return b.tok("pool")

def setup_u6(b, C, nc, inp, newt):
    W = {}
    W["wr_f32"] = inp["L1_ffn_w_router"]
    W["b1_f32"] = inp["L1_ffn_b1"]
    W["b2_f32"] = inp["L1_ffn_b2"]
    W["wrbf"] = newt("u6_wrbf", [1024, 8], BF16)
    W["w1bf"] = newt("u6_w1bf", [8192, 4096], BF16)
    W["w2bf"] = newt("u6_w2bf", [32768, 1024], BF16)
    W["wrows"] = newt("u6_wrows", [8, T], F32)
    W["tok_w1"] = convert_weight(b, C, inp["L1_ffn_w1"].rearrange("e c h -> (e c) h"), W["w1bf"])
    W["tok_w2"] = convert_weight(b, C, inp["L1_ffn_w2"].rearrange("e h c -> (e h) c"), W["w2bf"])
    return W

UNIT_TABS = {}
UNIT_INPUTS = {
    "u6": ["L1_ffn_w_router", "L1_ffn_w1", "L1_ffn_b1", "L1_ffn_w2", "L1_ffn_b2"],
    "u8": ["L1_ffn_w_router", "L1_ffn_w1", "L1_ffn_b1", "L1_ffn_w2", "L1_ffn_b2"],
}
UNIT_SETUP = {"u6": setup_u6, "u8": setup_u6}
UNIT_EMIT = {"u6": emit_u6, "u8": emit_u6}

def build_program(units):
    nc = bass.Bass("TRN2", target_bir_lowering=False)
    stack = ExitStack()
    b = B(nc)
    inp = {}
    needed = sorted({n for u in units for n in UNIT_INPUTS[u]})
    x_in = nc.dram_tensor("x_in", [8 * T], F32, kind="ExternalInput").ap()
    x_out = nc.dram_tensor("x_out", [8 * T * 128], F32, kind="ExternalOutput").ap()
    x_in2 = nc.dram_tensor("x_fm", [128, 8 * T], F32, kind="ExternalInput").ap()
    for name in needed:
        from_spec = INPUT_SHAPES[name]
        inp[name] = nc.dram_tensor(name, list(from_spec), F32, kind="ExternalInput").ap()
    HC = hostc()
    for u in units:
        for tname in UNIT_TABS.get(u, []):
            if tname not in inp:
                inp[tname] = nc.dram_tensor(tname, list(HC[tname].shape),
                                            BF16 if HC[tname].dtype == ml_dtypes.bfloat16 else F32,
                                            kind="ExternalInput").ap()
    for cname, _ in CONST_SPECS:
        inp[cname] = nc.dram_tensor(cname, list(HC[cname].shape),
                                    BF16 if HC[cname].dtype == ml_dtypes.bfloat16 else F32,
                                    kind="ExternalInput").ap()
    with stack:
        C = make_ctx(b, stack)
        ctok = load_consts(b, C, inp)
        newt = lambda name, shape, dt: nc.dram_tensor(name, shape, dt)
        Ws = {}
        seen = {}
        for u in units:
            key = UNIT_SETUP[u]
            if key in seen:
                Ws[u] = seen[key]
            else:
                Ws[u] = UNIT_SETUP[u](b, C, nc, inp, newt)
                seen[key] = Ws[u]
        xtok = b.dma("sync", C.xres[:, :], x_in2, "dX")
        b.wait("vector", ctok)
        tok = xtok
        t0 = b.emit("vector", lambda e: e.tensor_copy(C.colbuf[:, 0:1], C.xres[:, 0:1]),
                    waits=[xtok, ctok], inc="dve")
        tok = t0
        for iu, u in enumerate(units):
            if iu > 0:
                tok = unit_barrier(b, C)
            tok = UNIT_EMIT[u](b, C, Ws[u], in_tok=tok)
        st = b.dma("sync", x_out.rearrange("(p c) -> p c", p=128), C.xres[:, :], "dO",
                   waits=[tok])
        for dname, dap in DEBUG_DUMPS:
            dt_ = nc.dram_tensor("dbg_" + dname, list(dap(C).shape), dap(C).dtype,
                                 kind="ExternalOutput").ap()
            st = b.dma("sync", dt_, dap(C), "dO", waits=[tok])
        b.emit("sync", lambda e: e.nop(), waits=[st])
        b.finalize(stack)
    return nc

DEBUG_DUMPS = []

INPUT_SHAPES = {
    "L1_ffn_w_router": [1024, 8],
    "L1_ffn_w1": [8, 1024, 4096],
    "L1_ffn_b1": [8, 4096],
    "L1_ffn_w2": [8, 4096, 1024],
    "L1_ffn_b2": [8, 1024],
}

ALL_UNITS = ["u1", "u2", "u3", "u4", "u5", "u6", "u7", "u8"]

def _fm_layout(x):
    return np.ascontiguousarray(x.reshape(T, 8, 128).transpose(2, 1, 0).reshape(128, 8 * T))

def _fm_unlayout(y):
    return np.ascontiguousarray(y.reshape(128, 8, T).transpose(2, 1, 0).reshape(T, D))

def run_units(units, x, inputs):
    nc = build_program(units)
    HC = hostc()
    needed = sorted({n for u in units for n in UNIT_INPUTS[u]})
    in_maps = []
    for core in range(NCORES):
        m = {"x_fm": _fm_layout(x[core]), "x_in": np.zeros([8 * T], np.float32)}
        for n in needed:
            if n in HOST_LAYOUT:
                srcn, fn = HOST_LAYOUT[n]
                m[n] = fn(np.asarray(inputs[srcn], dtype=np.float32))
            else:
                m[n] = np.ascontiguousarray(inputs[n], dtype=np.float32)
        for cname, _ in CONST_SPECS:
            m[cname] = HC[cname]
        for u in units:
            for tname in UNIT_TABS.get(u, []):
                m[tname] = HC[tname]
        in_maps.append(m)
    res = run_bass_kernel_spmd(nc, in_maps, core_ids=list(range(NCORES)))
    out = np.stack([_fm_unlayout(res.results[i]["x_out"]) for i in range(NCORES)])
    return out

def kernel(**inputs):
    x = np.ascontiguousarray(inputs["L0_attn_a_x"], dtype=np.float32)
    out = run_units(ALL_UNITS, x, inputs)
    return np.ascontiguousarray(out, dtype=np.float32)

def rope_half(b, C, buf, j, cos_sb, sins_sb, in_tok, swap=None):
    sw = C.cb[:, 4096:4608]
    last = None
    for t0 in range(0, T, 512):
        bk = C.ps.acquire()
        tm = b.emit("tensor",
                    lambda e, o=bk["t"][0:128, 0:512], r=fms(buf, j, t0, 512),
                           sm=(swap if swap is not None else C.swap64):
                        e.matmul(o, sm[:, :], r, start=True, stop=True),
                    waits=[in_tok], inc="pe")
        t1 = b.emit("vector",
                    lambda e, o=sw, s=bk["t"][0:128, 0:512],
                           sn=sins_sb[:, t0:t0 + 512]:
                        e.tensor_tensor(o, s, sn, op=ALU.mult),
                    waits=[tm, C.rope_sw_free], inc="dve")
        bk["free"] = t1
        t2 = b.emit("vector",
                    lambda e, d=fms(buf, j, t0, 512), cs=cos_sb[:, t0:t0 + 512]:
                        e.tensor_tensor(d, d, cs, op=ALU.mult),
                    waits=[tm], inc="dve")
        last = b.emit("vector",
                      lambda e, d=fms(buf, j, t0, 512), s=sw:
                          e.tensor_tensor(d, d, s, op=ALU.add),
                      waits=[t1, t2], inc="dve")
        C.rope_sw_free = last
    return last

def vpass_gemm(b, C, *, wcols, vdst_dram, rhs_src, rhs_tok, wwait=None, Cout=1024):
    Kt = wcols.shape[0] // 128
    for jb0 in range(0, Cout, 512):
        jw = min(512, Cout - jb0)
        si = C.wslot_idx
        C.wslot_idx = 1 - C.wslot_idx
        dst = C.wslot[si]
        in_ap = wcols[:, jb0:jb0 + jw].rearrange("(kt p) j -> p kt j", p=128)
        wtok = b.dma("sync", dst[:, 0:Kt * jw].rearrange("p (kt j) -> p kt j", kt=Kt), in_ap,
                     "dW", waits=[x for x in (C.wslot_tok[si], wwait) if x])
        last_pe = None
        for tt in range(16):
            bk = C.ps.acquire()
            b.wait("tensor", wtok)
            if rhs_tok is not None:
                b.wait("tensor", rhs_tok)
            for kt in range(Kt):
                stop = kt == Kt - 1
                tokm = b.emit("tensor",
                              lambda e, o=bk["t"][0:128, 0:jw],
                                     l=fms(rhs_src, kt, tt * 128, 128),
                                     r=dst[:, kt * jw: kt * jw + jw],
                                     s=(kt == 0), p=stop:
                                  e.matmul(o, l, r, start=s, stop=p),
                              inc="pe" if stop else None)
            last_pe = tokm
            ev = b.emit("vector",
                        lambda e, s=bk["t"][0:128, 0:jw], o=C.cb[:, 4096:4096 + jw]:
                            e.tensor_copy(o, s),
                        waits=[tokm, C.vstage_free], inc="dve")
            bk["free"] = ev
            C.vstage_free = b.dma("sync", vdst_dram.ap()[tt * 128:(tt + 1) * 128, jb0:jb0 + jw],
                                  C.cb[:, 4096:4096 + jw], "dS", waits=[ev])
        C.wslot_tok[si] = last_pe

def attn_core(b, C, *, ksp, vsp, scale, qbuf, nheads=16, bias_fn=None, causal=True,
              pre_qt=None):
    kp = C.cb[:, 0:2048]
    vp = C.cb[:, 2048:4096]
    Ssb = C.cbf
    kv_tok = None
    for hp in range(nheads // 2):
        ktok = b.dma("sync", kp, ksp.ap()[hp * 128:(hp + 1) * 128, :], "dS",
                     waits=[C.kp_free])
        vtok = None
        for h2_ in range(2):
            vtok = b.dma("sync",
                         vp[:, h2_ * 1024:(h2_ + 1) * 1024].rearrange("p (kt d) -> p kt d", d=64),
                         vsp.ap().rearrange("(kt p) (hh d) -> p hh kt d", p=128, d=64)[:, 2 * hp + h2_, :, :],
                         "dS", waits=[C.vp_free])
        last_pv_ev = None
        for h2 in range(2):
            h = 2 * hp + h2
            jt, rbase = h // 2, (h % 2) * 64
            for qt in range(16):
                if pre_qt is not None:
                    pre_qt(h, qt)
                nvis = (qt * 128 + 128) if causal else T
                kfull = qt * 128 if causal else T
                chunks = []
                c0 = 0
                while c0 < kfull:
                    w = min(512, kfull - c0)
                    chunks.append((c0, w))
                    c0 += w
                cbanks = []
                for (c0, w) in chunks:
                    bk = C.ps.acquire()
                    b.emit("tensor",
                           lambda e, o=bk["t"][0:128, 0:w],
                                  l=fms(qbuf, jt, qt * 128, 128)[rbase:rbase + 64, :],
                                  r=kp[rbase:rbase + 64, c0:c0 + w]:
                               e.matmul(o, l, r, start=True, stop=True),
                           waits=[ktok, C.q_tok], inc="pe")
                    cbanks.append(bk)
                if causal:
                    dbk = C.ps.acquire()
                    dtok = b.emit("tensor",
                                  lambda e, o=dbk["t"][0:128, 0:128],
                                         l=fms(qbuf, jt, qt * 128, 128)[rbase:rbase + 64, :],
                                         r=kp[rbase:rbase + 64, qt * 128:qt * 128 + 128]:
                                      e.matmul(o, l, r, start=True, stop=True),
                                  waits=[ktok, C.q_tok], inc="pe")
                nch = len(chunks)
                evs = []
                for ci, (c0, w) in enumerate(chunks):
                    bk = cbanks[ci]
                    if bias_fn is None:
                        ev = b.emit("vector",
                                    lambda e, o=Ssb[:, c0:c0 + w], s=bk["t"][0:128, 0:w],
                                           mcol=C.colbuf[:, 8 + ci:9 + ci]:
                                        e.tensor_scalar(o, s, scale, None, op0=ALU.mult,
                                                        op1=ALU.max, accum_out=mcol),
                                    waits=[b.tok("pe"), C.S_free], inc="dve")
                    else:
                        ev = bias_fn(h, qt, c0, w, bk, ci)
                    evs.append(ev)
                    bk["free"] = ev
                if causal:
                    dev = b.emit("vector",
                                 lambda e, o=Ssb[:, qt * 128:qt * 128 + 128],
                                        s=dbk["t"][0:128, 0:128]:
                                     e.scalar_tensor_tensor(o, s, scale, C.amask[:, :],
                                                            op0=ALU.mult, op1=ALU.add),
                                 waits=[dtok, C.S_free], inc="dve")
                    dbk["free"] = dev
                    dmx = b.emit("vector",
                                 lambda e, s=Ssb[:, qt * 128:qt * 128 + 128],
                                        mcol=C.colbuf[:, 8 + nch:9 + nch]:
                                     e.reduce_max(mcol, s, axis=AX.X),
                                 waits=[dev], inc="dve")
                    nmx = nch + 1
                else:
                    dmx = evs[-1]
                    nmx = nch
                cm = b.emit("vector",
                            lambda e, n=nmx: e.reduce_max(C.colbuf[:, 1:2], C.colbuf[:, 8:8 + n], axis=AX.X),
                            waits=[dmx] + evs, inc="dve")
                ng = b.emit("vector",
                            lambda e: e.tensor_scalar(C.colbuf[:, 0:1], C.colbuf[:, 1:2], -1.0, None,
                                                      op0=ALU.mult),
                            waits=[cm], inc="dve")
                ex = b.emit("scalar",
                            lambda e, w=nvis: e.activation(Ssb[:, 0:w], Ssb[:, 0:w], AF.Exp,
                                                           bias=C.colbuf[:, 0:1],
                                                           accum_out=C.colbuf[:, 2:3]),
                            waits=[ng], inc="act")
                rc = b.emit("vector", lambda e: e.reciprocal(C.colbuf[:, 3:4], C.colbuf[:, 2:3]),
                            waits=[ex], inc="dve")
                sc = b.emit("vector",
                            lambda e, w=nvis: e.tensor_scalar(Ssb[:, 0:w], Ssb[:, 0:w],
                                                              C.colbuf[:, 3:4], None, op0=ALU.mult),
                            waits=[rc], inc="dve")
                obk = C.ps.hold()
                nkt = nvis // 128
                ptok = None
                last_tr = None
                for kt in range(nkt):
                    tbk = C.ps.acquire()
                    tr = b.emit("tensor",
                                lambda e, o=tbk["t"][0:128, 0:128], s=Ssb[:, kt * 128:kt * 128 + 128]:
                                    e.transpose(o, s, C.id_f32[:, :]),
                                waits=[sc], inc="pe")
                    last_tr = tr
                    gsl = C.gel[:, (kt % 2) * 128:(kt % 2) * 128 + 128]
                    pev = b.emit("vector",
                                 lambda e, o=gsl, s=tbk["t"][0:128, 0:128]:
                                     e.tensor_copy(o, s),
                                 waits=[tr, C.gel_free[kt % 2]], inc="dve")
                    tbk["free"] = pev
                    ptok = b.emit("tensor",
                                  lambda e, o=obk["t"][0:64, 0:128],
                                         l=vp[:, h2 * 1024 + kt * 64: h2 * 1024 + kt * 64 + 64],
                                         r=gsl, s=(kt == 0), p=(kt == nkt - 1):
                                      e.matmul(o, l, r, start=s, stop=p),
                                  waits=[pev, vtok], inc="pe" if kt == nkt - 1 else None)
                    C.gel_free[kt % 2] = ptok
                last_pv_ev = b.emit("vector",
                                    lambda e, o=fms(qbuf, jt, qt * 128, 128)[rbase:rbase + 64, :],
                                           s=obk["t"][0:64, 0:128]:
                                        e.tensor_copy(o, s),
                                    waits=[ptok], inc="dve")
                C.ps.release(obk, last_pv_ev)
                C.S_free = last_tr
        C.kp_free = b.tok("pe")
        C.vp_free = b.tok("pe")
    return b.tok("dve")

def emit_rope_attn(b, C, W, in_tok=None):
    ntok = emit_norm(b, C, src=C.xres, dst=C.xn, nkt=8, eps=1e-5, mean=True,
                     src_tok=in_tok)
    cos_sb = C.cb[:, 0:2048]
    sins_sb = C.cb[:, 2048:4096]
    ttok1 = b.dma("sync", cos_sb, W["tab_cos"], "dC")
    ttok2 = b.dma("sync", sins_sb, W["tab_sins"], "dC")
    def evac_qk(j, t0, tw, bank, mmtok, mw):
        dst = fms(C.ab2, j, t0, tw) if j < 8 else fms(C.ab3, j - 8, t0, tw)
        return b.emit("vector",
                      lambda e, o=dst, s=bank["t"][0:mw, 0:tw]: e.tensor_copy(o, s),
                      waits=[mmtok], inc="dve")
    gemm(b, C, w=W["qkvbf"].ap()[:, 0:2048], Cout=2048,
         rhs=lambda kt, t0, tw: (fms(C.xn, kt, t0, tw), ntok),
         evac=evac_qk, jblock=512, wwait=W.get("tok_qkv"))
    qk_done = b.tok("dve")
    b.wait("vector", ttok1)
    b.wait("vector", ttok2)
    C.rope_sw_free = None
    swp = C.swapx if W.get("swapx") else C.swap64
    for j in range(8):
        rope_half(b, C, C.ab2, j, cos_sb, sins_sb, qk_done, swap=swp)
    for j in range(8):
        rtk = rope_half(b, C, C.ab3, j, cos_sb, sins_sb, qk_done, swap=swp)
        b.dma("sync", W["ksp"].ap()[j * 128:(j + 1) * 128, :], fms(C.ab3, j, 0, T),
              "dS", waits=[rtk])
    k_spilled = b.tok("dS")
    C.vstage_free = None
    vpass_gemm(b, C, wcols=W["qkvbf"].ap()[:, 2048:3072], vdst_dram=W["vsp"],
               rhs_src=C.xn, rhs_tok=ntok, wwait=W.get("tok_qkv"))
    v_spilled = b.tok("dS")
    cb_done = b.tok("dve")
    C.kp_free = cb_done
    C.vp_free = cb_done
    C.S_free = None
    C.q_tok = b.tok("dve")
    b.wait("sync", k_spilled)
    b.wait("sync", v_spilled)
    at = attn_core(b, C, ksp=W["ksp"], vsp=W["vsp"], scale=0.125, qbuf=C.ab2)
    def evac_out(j, t0, tw, bank, mmtok, mw):
        tk = b.emit("vector",
                    lambda e, d=fms(C.xres, j, t0, tw), s=bank["t"][0:mw, 0:tw]:
                        e.tensor_tensor(d, d, s, op=ALU.add),
                    waits=[mmtok], inc="dve")
        return tk
    gemm(b, C, w=W["projbf"].ap(), Cout=1024,
         rhs=lambda kt, t0, tw: (fms(C.ab2, kt, t0, tw), at),
         evac=evac_out, jblock=512, wwait=W.get("tok_proj"))
    return b.tok("dve")

def setup_u7(b, C, nc, inp, newt):
    W = {}
    W["qkvbf"] = newt("u7_qkvbf", [1024, 3072], BF16)
    W["projbf"] = newt("u7_projbf", [1024, 1024], BF16)
    W["ksp"] = newt("u7_ksp", [1024, T], BF16)
    W["vsp"] = newt("u7_vsp", [T, 1024], BF16)
    W["tok_qkv"] = convert_weight(b, C, inp["L1_attn_b_W_qkv"], W["qkvbf"])
    W["tok_proj"] = convert_weight(b, C, inp["L1_attn_b_W_proj"], W["projbf"])
    W["tab_cos"] = inp["tab_u7_cos"]
    W["tab_sins"] = inp["tab_u7_sins"]
    return W

def setup_u3(b, C, nc, inp, newt):
    W = {}
    W["qkvbf"] = newt("u3_qkvbf", [1024, 3072], BF16)
    W["projbf"] = newt("u3_projbf", [1024, 1024], BF16)
    W["ksp"] = newt("u3_ksp", [1024, T], BF16)
    W["vsp"] = newt("u3_vsp", [T, 1024], BF16)
    src3 = inp["L0_attn_b_w_qkv"]
    tok = None
    for part in range(3):
        in_ap = bass.AP(src3.tensor, src3.offset + part * 128,
                        [[3072, 1024], [384, 8], [1, 128]])
        tok = b.dma("gpsimd", W["qkvbf"].ap()[:, part * 1024:(part + 1) * 1024]
                    .rearrange("r (mp j) -> r mp j", j=128), in_ap, "dCv")
    W["tok_qkv"] = tok
    W["tok_proj"] = convert_weight(b, C, inp["L0_attn_b_w_out"], W["projbf"])
    W["tab_cos"] = inp["tab_u3_cosfm"]
    W["tab_sins"] = inp["tab_u3_sinsfm"]
    W["swapx"] = True
    return W

UNIT_INPUTS["u7"] = ["L1_attn_b_W_qkv", "L1_attn_b_W_proj"]
UNIT_SETUP["u7"] = setup_u7
UNIT_EMIT["u7"] = emit_rope_attn
UNIT_TABS["u7"] = ["tab_u7_cos", "tab_u7_sins"]
INPUT_SHAPES["L1_attn_b_W_qkv"] = [1024, 3072]
INPUT_SHAPES["L1_attn_b_W_proj"] = [1024, 1024]
UNIT_INPUTS["u3"] = ["L0_attn_b_w_qkv", "L0_attn_b_w_out"]
UNIT_SETUP["u3"] = setup_u3
UNIT_EMIT["u3"] = emit_rope_attn
UNIT_TABS["u3"] = ["tab_u3_cosfm", "tab_u3_sinsfm"]
INPUT_SHAPES["L0_attn_b_w_qkv"] = [1024, 3072]
INPUT_SHAPES["L0_attn_b_w_out"] = [1024, 1024]

def emit_u5(b, C, W, in_tok=None):
    ntok = emit_norm(b, C, src=C.xres, dst=C.xn, nkt=8, eps=1e-5, mean=True,
                     src_tok=in_tok)
    Rtok = b.dma("sync", C.cb[0:32, 0:4095], W["tab_R"], "dC")
    rbl = b.dma("sync", C.tmp32[0:32, 0:16], W["rel_bias_f32"], "dC",
                waits=[C.tmp32_free])
    rbc = b.emit("vector", lambda e: e.tensor_copy(C.gel[0:32, 0:16], C.tmp32[0:32, 0:16]),
                 waits=[rbl], inc="dve")
    C.tmp32_free = rbc
    tstage_free = None
    for c0 in range(0, 4095, 512):
        w = min(512, 4095 - c0)
        bk = C.ps.acquire()
        mt = b.emit("tensor",
                    lambda e, o=bk["t"][0:16, 0:w], r=C.cb[0:32, c0:c0 + w]:
                        e.matmul(o, C.gel[0:32, 0:16], r, start=True, stop=True),
                    waits=[Rtok, rbc], inc="pe")
        ev = b.emit("vector",
                    lambda e, o=C.gel[0:16, 512:512 + w], s=bk["t"][0:16, 0:w]:
                        e.tensor_copy(o, s),
                    waits=[mt, tstage_free], inc="dve")
        bk["free"] = ev
        tstage_free = b.dma("sync", W["tbl"].ap()[0:16, c0:c0 + w], C.gel[0:16, 512:512 + w],
                            "dS", waits=[ev])
    C.gel_free = [b.tok("dS"), b.tok("dS")]
    tbl_done = b.tok("dS")
    for h in range(16):
        b.dma("gpsimd",
              bass.AP(W["tblX"].ap().tensor, h * 128 * 4224, [[4225, 128], [1, 4095]]),
              bass.AP(W["tbl"].ap().tensor, h * 4095, [[0, 128], [1, 4095]]),
              "dG", waits=[tbl_done])
    tblX_done = b.tok("dG")
    def mkevac(dstbuf):
        def evac(j, t0, tw, bank, mmtok, mw):
            return b.emit("vector",
                          lambda e, o=fms(dstbuf, j, t0, tw), s=bank["t"][0:mw, 0:tw]:
                              e.tensor_copy(o, s),
                          waits=[mmtok], inc="dve")
        return evac
    gemm(b, C, w=W["wqbf"].ap(), Cout=1024,
         rhs=lambda kt, t0, tw: (fms(C.xn, kt, t0, tw), ntok),
         evac=mkevac(C.ab2), jblock=512, wwait=W.get("tok_wq"))
    gemm(b, C, w=W["wkbf"].ap(), Cout=1024,
         rhs=lambda kt, t0, tw: (fms(C.xn, kt, t0, tw), ntok),
         evac=mkevac(C.ab3), jblock=512, wwait=W.get("tok_wk"))
    kq_done = b.tok("dve")
    for j in range(8):
        b.dma("sync", W["ksp"].ap()[j * 128:(j + 1) * 128, :], fms(C.ab3, j, 0, T),
              "dS", waits=[kq_done])
    k_spilled = b.tok("dS")
    C.vstage_free = None
    vpass_gemm(b, C, wcols=W["wvbf"].ap(), vdst_dram=W["vsp"],
               rhs_src=C.xn, rhs_tok=ntok, wwait=W.get("tok_wv"))
    v_spilled = b.tok("dS")
    C.kp_free = b.tok("dve")
    C.vp_free = C.kp_free
    C.S_free = None
    C.q_tok = kq_done
    b.wait("sync", k_spilled)
    b.wait("sync", v_spilled)
    bias_state = {"tok": [None, None], "loaded": k_spilled}
    tblxt = W["tblX"].ap()
    def pre_qt(h, qt):
        si = qt % 2
        off = h * 128 * 4224 + 2047 - 128 * qt
        in_ap = bass.AP(tblxt.tensor, off, [[4224, 128], [1, 2048]])
        bias_state["tok"][si] = b.dma("sync", C.ab3[:, si * 2048:(si + 1) * 2048], in_ap,
                                      "dS", waits=[tblX_done, bias_state["loaded"]])
    def bias_fn(h, qt, c0, w, bk, ci):
        si = qt % 2
        ev = b.emit("vector",
                    lambda e, o=C.cbf[:, c0:c0 + w], s=bk["t"][0:128, 0:w],
                           bi=C.ab3[:, si * 2048 + c0: si * 2048 + c0 + w]:
                        e.tensor_tensor(o, s, bi, op=ALU.add),
                    waits=[b.tok("pe"), bias_state["tok"][si], C.S_free], inc="dve")
        mx = b.emit("vector",
                    lambda e, s=C.cbf[:, c0:c0 + w], mcol=C.colbuf[:, 8 + ci:9 + ci]:
                        e.reduce_max(mcol, s, axis=AX.X),
                    waits=[ev], inc="dve")
        bias_state["loaded"] = mx
        return mx
    at = attn_core(b, C, ksp=W["ksp"], vsp=W["vsp"], scale=1.0, qbuf=C.ab2,
                   causal=False, bias_fn=bias_fn, pre_qt=pre_qt)
    def evac_out(j, t0, tw, bank, mmtok, mw):
        return b.emit("vector",
                      lambda e, d=fms(C.xres, j, t0, tw), s=bank["t"][0:mw, 0:tw]:
                          e.tensor_tensor(d, d, s, op=ALU.add),
                      waits=[mmtok], inc="dve")
    gemm(b, C, w=W["wobf"].ap(), Cout=1024,
         rhs=lambda kt, t0, tw: (fms(C.ab2, kt, t0, tw), at),
         evac=evac_out, jblock=512, wwait=W.get("tok_wo"))
    return b.tok("dve")

def setup_u5(b, C, nc, inp, newt):
    W = {}
    for nm, key in (("wq", "L1_attn_a_wq"), ("wk", "L1_attn_a_wk"),
                    ("wv", "L1_attn_a_wv"), ("wo", "L1_attn_a_wo")):
        W[nm + "bf"] = newt("u5_" + nm, [1024, 1024], BF16)
        W["tok_" + nm] = convert_weight(b, C, inp[key], W[nm + "bf"])
    W["ksp"] = newt("u5_ksp", [1024, T], BF16)
    W["vsp"] = newt("u5_vsp", [T, 1024], BF16)
    W["tbl"] = newt("u5_tbl", [16, 4095], BF16)
    W["tblX"] = newt("u5_tblX", [16 * 128 * 4224], BF16)
    W["rel_bias_f32"] = inp["L1_attn_a_rel_bias"]
    W["tab_R"] = inp["tab_u5_R"]
    return W

UNIT_INPUTS["u5"] = ["L1_attn_a_wq", "L1_attn_a_wk", "L1_attn_a_wv", "L1_attn_a_wo",
                     "L1_attn_a_rel_bias"]
UNIT_SETUP["u5"] = setup_u5
UNIT_EMIT["u5"] = emit_u5
UNIT_TABS["u5"] = ["tab_u5_R"]
INPUT_SHAPES["L1_attn_a_wq"] = [1024, 1024]
INPUT_SHAPES["L1_attn_a_wk"] = [1024, 1024]
INPUT_SHAPES["L1_attn_a_wv"] = [1024, 1024]
INPUT_SHAPES["L1_attn_a_wo"] = [1024, 1024]
INPUT_SHAPES["L1_attn_a_rel_bias"] = [32, 16]

def emit_u2(b, C, W, in_tok=None):
    ntok = emit_norm(b, C, src=C.xres, dst=C.xn, nkt=8, eps=1e-5, mean=True,
                     src_tok=in_tok)
    rtoks = []
    for kt in range(8):
        rtoks.append(b.emit("vector",
                            lambda e, d=fms(C.xres, kt, 0, T), s=fms(C.xn, kt, 0, T):
                                e.tensor_tensor(d, d, s, op=ALU.add),
                            waits=[ntok], inc="dve"))
    for j in range(8):
        b.dma("sync", W["xnsp"].ap()[j * 128:(j + 1) * 128, :], fms(C.xn, j, 0, T),
              "dS", waits=[ntok])
    xn_spilled = b.tok("dS")
    g1 = load_cols(b, C, W["ln1_f32"], 8, C.gcols[:, 0:8])
    g2 = load_cols(b, C, W["ln2_f32"], 8, C.gcols[:, 8:16])
    b.wait("vector", xn_spilled)
    y1tok = emit_norm(b, C, src=C.xn, dst=C.xn, nkt=8, eps=1e-6, mean=False,
                      gamma_col=C.gcols, gamma_off=0, src_is_bf=True, src_tok=g1)
    def evac_q(j, t0, tw, bank, mmtok, mw):
        return b.emit("vector",
                      lambda e, o=fms(C.ab2, j, t0, tw), s=bank["t"][0:mw, 0:tw]:
                          e.tensor_copy(o, s),
                      waits=[mmtok], inc="dve")
    gemm(b, C, w=W["wqbf"].ap(), Cout=1024,
         rhs=lambda kt, t0, tw: (fms(C.xn, kt, t0, tw), y1tok),
         evac=evac_q, jblock=512, wwait=W.get("tok_wq"))
    def evac_kv(j, t0, tw, bank, mmtok, mw):
        return b.emit("vector",
                      lambda e, o=C.cb[:, j * T + t0: j * T + t0 + tw], s=bank["t"][0:mw, 0:tw]:
                          e.tensor_scalar(o, s, 0.25, None, op0=ALU.mult),
                      waits=[mmtok, C.cb_guard], inc="dve")
    gemm(b, C, w=W["wkvbf"].ap(), Cout=256,
         rhs=lambda kt, t0, tw: (fms(C.xn, kt, t0, tw), y1tok),
         evac=evac_kv, jblock=256, wwait=W.get("tok_wkv"))
    kv_tok = b.tok("dve")
    kup_tok = b.dma("sync", C.wslot[0][:, 0:2048].rearrange("p (lt j) -> p lt j", lt=2),
                    W["kupbf"].ap().rearrange("(lt p) j -> p lt j", p=128)[:, :, 0:1024],
                    "dW", waits=[C.wslot_tok[0], W.get("tok_kup")])
    C.wslot_tok[0] = None
    lastk = None
    for h in range(16):
        jt, rbase = h // 2, (h % 2) * 64
        for t0 in range(0, T, 512):
            bk = C.ps.acquire()
            mt = None
            for lt in range(2):
                mt = b.emit("tensor",
                            lambda e, o=bk["t"][0:64, 0:512],
                                   l=C.wslot[0][:, lt * 1024 + h * 64: lt * 1024 + h * 64 + 64],
                                   r=C.cb[:, lt * T + t0: lt * T + t0 + 512],
                                   s=(lt == 0), p=(lt == 1):
                                e.matmul(o, l, r, start=s, stop=p),
                            waits=[kup_tok, kv_tok], inc="pe" if lt else None)
            lastk = b.emit("vector",
                           lambda e, o=fms(C.ab3, jt, t0, 512)[rbase:rbase + 64, :],
                                  s=bk["t"][0:64, 0:512]: e.tensor_copy(o, s),
                           waits=[mt], inc="dve")
            bk["free"] = lastk
    C.wslot_tok[0] = b.tok("pe")
    C.vstage_free = None
    vpass_gemm(b, C, wcols=W["vupbf"].ap(), vdst_dram=W["vsp"],
               rhs_src=C.cb, rhs_tok=kv_tok, wwait=W.get("tok_vup"))
    v_spilled = b.tok("dS")
    cos_sb = C.cb[:, 0:2048]
    sins_sb = C.cb[:, 2048:4096]
    cbg = b.tok("pe")
    ttok1 = b.dma("sync", cos_sb, W["tab_cos"], "dC", waits=[cbg])
    ttok2 = b.dma("sync", sins_sb, W["tab_sins"], "dC")
    b.wait("vector", ttok1)
    b.wait("vector", ttok2)
    qk_done = b.tok("dve")
    C.rope_sw_free = None
    for j in range(8):
        rope_half(b, C, C.ab2, j, cos_sb, sins_sb, qk_done)
    for j in range(8):
        rtk = rope_half(b, C, C.ab3, j, cos_sb, sins_sb, qk_done)
        b.dma("sync", W["ksp"].ap()[j * 128:(j + 1) * 128, :], fms(C.ab3, j, 0, T),
              "dS", waits=[rtk])
    k_spilled = b.tok("dS")
    cb_done = b.tok("dve")
    C.kp_free = cb_done
    C.vp_free = cb_done
    C.S_free = None
    C.q_tok = b.tok("dve")
    b.wait("sync", k_spilled)
    b.wait("sync", v_spilled)
    at = attn_core(b, C, ksp=W["ksp"], vsp=W["vsp"], scale=0.125, qbuf=C.ab2)
    xn_state = {"tok": [None, None]}
    def evac_wo(j, t0, tw, bank, mmtok, mw):
        si = (j * 4 + t0 // 512) % 2
        xt = b.dma("sync", C.cb[:, 4096 + si * 256: 4096 + si * 256 + tw // 2]
                   if False else C.cb[:, 4096:4096 + tw], W["xnsp"].ap()[j * 128:(j + 1) * 128, t0:t0 + tw],
                   "dS", waits=[xn_state["tok"][0]])
        t1 = b.emit("vector",
                    lambda e, d=fms(C.xres, j, t0, tw), s=bank["t"][0:mw, 0:tw]:
                        e.tensor_tensor(d, d, s, op=ALU.add),
                    waits=[mmtok], inc="dve")
        t2 = b.emit("vector",
                    lambda e, o=fms(C.ab3, j, t0, tw), s=bank["t"][0:mw, 0:tw],
                           xs=C.cb[:, 4096:4096 + tw]:
                        e.tensor_tensor(o, xs, s, op=ALU.add),
                    waits=[mmtok, xt], inc="dve")
        xn_state["tok"][0] = t2
        return t2
    gemm(b, C, w=W["wobf"].ap(), Cout=1024,
         rhs=lambda kt, t0, tw: (fms(C.ab2, kt, t0, tw), at),
         evac=evac_wo, jblock=512, wwait=W.get("tok_wo"))
    u_tok = b.tok("dve")
    y2tok = emit_norm(b, C, src=C.ab3, dst=C.ab3, nkt=8, eps=1e-6, mean=False,
                      gamma_col=C.gcols, gamma_off=8, src_is_bf=True, src_tok=u_tok)
    wrc = b.dma("gpsimd", W["wrbf"].ap()[:, :], W["wr_f32"], "dCv")
    wr_tok = b.dma("sync", C.wrsb[:, 0:64].rearrange("p (kt j) -> p kt j", kt=8),
                   W["wrbf"].ap().rearrange("(kt p) j -> p kt j", p=128), "dW",
                   waits=[wrc])
    rb_tok = b.dma("sync", C.colbuf[0:1, 32:40], W["rbias_f32"], "dC")
    lgt = C.router[:, 0:128]
    wf = C.router[:, 128:256]
    for tt4 in range(0, 16, 4):
        bk = C.ps.acquire()
        for ti in range(tt4, tt4 + 4):
            mt = None
            for kt in range(8):
                mt = b.emit("tensor",
                            lambda e, o=bk["t"][0:128, (ti % 4) * 8:(ti % 4) * 8 + 8],
                                   l=fms(C.ab3, kt, ti * 128, 128),
                                   r=C.wrsb[:, kt * 8:(kt + 1) * 8],
                                   s=(kt == 0), p=False:
                                e.matmul(o, l, r, start=s, stop=p),
                            waits=[wr_tok, y2tok], inc=None)
            mt = b.emit("tensor",
                        lambda e, o=bk["t"][0:128, (ti % 4) * 8:(ti % 4) * 8 + 8]:
                            e.matmul(o, C.ones_row128[0:1, :], C.colbuf[0:1, 32:40],
                                     start=False, stop=True),
                        waits=[rb_tok], inc="pe")
        ev = b.emit("vector",
                    lambda e, o=lgt[:, tt4 * 8: tt4 * 8 + 32], s=bk["t"][0:128, 0:32]:
                        e.tensor_copy(o, s),
                    waits=[mt], inc="dve")
        bk["free"] = ev
    lgt_tok = b.tok("dve")
    for tt in range(16):
        lg = lgt[:, tt * 8:(tt + 1) * 8]
        cl = b.emit("vector",
                    lambda e, s=lg: e.tensor_scalar(s, s, 50.0, -50.0, op0=ALU.min, op1=ALU.max),
                    waits=[lgt_tok], inc="dve")
        sg = b.emit("scalar", lambda e, s=lg: e.activation(s, s, AF.Sigmoid),
                    waits=[cl], inc="act")
        m1 = b.emit("vector", lambda e, s=lg: e.reduce_max(C.colbuf[:, 0:1], s, axis=AX.X),
                    waits=[sg], inc="dve")
        pc = b.emit("vector",
                    lambda e: e.tensor_scalar(C.colbuf[:, 1:2], C.colbuf[:, 0:1], 1e-8,
                                              1.0 - 1e-8, op0=ALU.max, op1=ALU.min),
                    waits=[m1], inc="dve")
        tw_ = b.emit("vector",
                     lambda e: e.tensor_scalar(C.colbuf[:, 2:3], C.colbuf[:, 1:2],
                                               0.5 + 1e-8, 2.0, op0=ALU.add, op1=ALU.min),
                     waits=[pc], inc="dve")
        tw2 = b.emit("vector",
                     lambda e: e.tensor_scalar(C.colbuf[:, 2:3], C.colbuf[:, 2:3], 0.5, None,
                                               op0=ALU.max),
                     waits=[tw_], inc="dve")
        rr = b.emit("vector", lambda e: e.reciprocal(C.colbuf[:, 3:4], C.colbuf[:, 2:3]),
                    waits=[tw2], inc="dve")
        pr_ = b.emit("vector",
                     lambda e: e.tensor_tensor(C.colbuf[:, 4:5], C.colbuf[:, 1:2],
                                               C.colbuf[:, 3:4], op=ALU.mult),
                     waits=[rr], inc="dve")
        ms = b.emit("vector",
                    lambda e, s=lg: e.tensor_scalar(C.colbuf[:, 16:24], s, C.colbuf[:, 0:1],
                                                    None, op0=ALU.is_ge),
                    waits=[pr_], inc="dve")
        b.emit("vector",
               lambda e, o=wf[:, tt * 8:(tt + 1) * 8]:
                   e.tensor_scalar(o, C.colbuf[:, 16:24], C.colbuf[:, 4:5], None, op0=ALU.mult),
               waits=[ms], inc="dve")
        hr = b.emit("vector",
                    lambda e: e.tensor_scalar(C.colbuf[:, 5:6], C.colbuf[:, 3:4], 0.5, None,
                                              op0=ALU.mult),
                    waits=[rr], inc="dve")
        b.emit("vector",
               lambda e, o=C.gcols[:, 16 + tt:17 + tt]: e.tensor_copy(o, C.colbuf[:, 5:6]),
               waits=[hr], inc="dve")
    wf_tok = b.tok("dve")
    stage_free = None
    for tt in range(16):
        bk2 = C.ps.acquire()
        tw_t = b.emit("tensor",
                      lambda e, o=bk2["t"][0:8, 0:128], s=wf[:, tt * 8:(tt + 1) * 8]:
                          e.transpose(o, s, C.id_f32[:, :]),
                      waits=[wf_tok], inc="pe")
        bk3r = C.ps.acquire()
        tw_s = b.emit("tensor",
                      lambda e, o=bk3r["t"][0:1, 0:128], s=C.gcols[:, 16 + tt:17 + tt]:
                          e.transpose(o, s, C.id_f32[:, :]),
                      waits=[wf_tok], inc="pe")
        ev = b.emit("vector",
                    lambda e, s=bk2["t"][0:8, 0:128]: e.tensor_copy(C.tmp32[32:40, 0:128], s),
                    waits=[tw_s, stage_free], inc="dve")
        ev2 = b.emit("vector",
                     lambda e, s=bk3r["t"][0:1, 0:128]: e.tensor_copy(C.tmp32[64:65, 0:128], s),
                     waits=[tw_s], inc="dve")
        bk2["free"] = ev2
        bk3r["free"] = ev2
        b.dma("sync", W["wrows"].ap()[0:8, tt * 128:(tt + 1) * 128], C.tmp32[32:40, 0:128],
              "dRW", waits=[ev, ev2])
        stage_free = b.dma("sync", W["wrows"].ap()[8:9, tt * 128:(tt + 1) * 128],
                           C.tmp32[64:65, 0:128], "dRW", waits=[ev2])
    wbsb = C.cb[:, 0:2048]
    wb_tok = b.tok("pe")
    wb_dve = None
    for e_ in range(9):
        for tci, tc in enumerate(range(0, T, 512)):
            rtok = b.dma("gpsimd", C.rowA[0:1, 0:512], W["wrows"].ap()[e_:e_ + 1, tc:tc + 512],
                         "dRr", waits=[b.tok("dRW"), C.rowA_free])
            bkw = C.ps.acquire()
            tb = bcast_row(b, C, C.rowA[0:1, 0:512], bkw, waits=[rtok])
            C.rowA_free = tb
            wb_dve = b.emit("vector",
                            lambda e, o=wbsb[:, tc:tc + 512], s=bkw["t"][0:128, 0:512]:
                                e.tensor_copy(o, s),
                            waits=[tb, wb_dve], inc="dve")
            bkw["free"] = wb_dve
        wb_ready = wb_dve
        w1ap = W["sw1bf"].ap() if e_ == 8 else W["ew1bf"].ap()[e_ * 1024:(e_ + 1) * 1024, :]
        w3ap = W["sw3bf"].ap() if e_ == 8 else W["ew3bf"].ap()[e_ * 1024:(e_ + 1) * 1024, :]
        w2ap = W["sw2bf"].ap() if e_ == 8 else W["ew2bf"].ap()[e_ * 2048:(e_ + 1) * 2048, :]
        wt1 = W.get("tok_sw") if e_ == 8 else W.get("tok_ew")
        def out_evac(jo, t0, tw, bank, mmtok, mw, wb_ready=wb_ready):
            tv = b.emit("vector",
                        lambda e: e.scalar_tensor_tensor(C.tmp32[0:mw, 0:tw],
                                                         bank["t"][0:mw, 0:tw], 1.0,
                                                         wbsb[0:mw, t0:t0 + tw],
                                                         op0=ALU.mult, op1=ALU.mult),
                        waits=[mmtok, wb_ready, C.tmp32_free], inc="dve")
            ax = b.emit("gpsimd",
                        lambda e, d=fms(C.xres, jo, t0, tw):
                            e.tensor_tensor(d, d, C.tmp32[0:mw, 0:tw], op=ALU.add),
                        waits=[tv, C.xres_pool_tok], inc="pool")
            C.tmp32_free = ax
            C.xres_pool_tok = ax
            return tv
        for tc2 in (0, 1024):
            for jb0 in range(0, 2048, 512):
                dstA, dstB = C.wslot[0], C.wslot[1]
                wtokA = b.dma("sync", dstA[:, 0:4096].rearrange("p (kt j) -> p kt j", kt=8),
                              w1ap[:, jb0:jb0 + 512].rearrange("(kt p) j -> p kt j", p=128),
                              "dW", waits=[x for x in (C.wslot_tok[0], wt1) if x])
                wtokB = b.dma("sync", dstB[:, 0:4096].rearrange("p (kt j) -> p kt j", kt=8),
                              w3ap[:, jb0:jb0 + 512].rearrange("(kt p) j -> p kt j", p=128),
                              "dW", waits=[x for x in (C.wslot_tok[1], wt1) if x])
                lastpe = None
                for jj in range(0, 512, 128):
                    j = (jb0 + jj) // 128
                    for t0 in (tc2, tc2 + 512):
                        bk1 = C.ps.acquire()
                        b.wait("tensor", wtokA)
                        b.wait("tensor", wtokB)
                        for kt in range(8):
                            b.emit("tensor",
                                   lambda e, o=bk1["t"][0:128, 0:512],
                                          l=dstA[:, kt * 512 + jj: kt * 512 + jj + 128],
                                          r=fms(C.ab3, kt, t0, 512), s=(kt == 0), p=(kt == 7):
                                       e.matmul(o, l, r, start=s, stop=p),
                                   inc="pe" if kt == 7 else None)
                        bk3 = C.ps.acquire()
                        mt3 = None
                        for kt in range(8):
                            mt3 = b.emit("tensor",
                                         lambda e, o=bk3["t"][0:128, 0:512],
                                                l=dstB[:, kt * 512 + jj: kt * 512 + jj + 128],
                                                r=fms(C.ab3, kt, t0, 512), s=(kt == 0), p=(kt == 7):
                                             e.matmul(o, l, r, start=s, stop=p),
                                         inc="pe" if kt == 7 else None)
                        lastpe = mt3
                        sl = b.emit("scalar",
                                    lambda e, o=C.gel[:, 0:512], s=bk1["t"][0:128, 0:512]:
                                        e.activation(o, s, AF.Silu),
                                    waits=[mt3, C.gel_free[0]], inc="act")
                        hv = b.emit("vector",
                                    lambda e, o=C.ab2[:, j * 1024 + (t0 - tc2): j * 1024 + (t0 - tc2) + 512],
                                           s=C.gel[:, 0:512], p3=bk3["t"][0:128, 0:512]:
                                        e.tensor_tensor(o, s, p3, op=ALU.mult),
                                    waits=[sl], inc="dve")
                        C.gel_free[0] = hv
                        bk1["free"] = hv
                        bk3["free"] = hv
                        C.ghid_tok = hv
                C.wslot_tok[0] = lastpe
                C.wslot_tok[1] = lastpe
            gemm(b, C, w=w2ap, Cout=1024,
                 rhs=lambda kt, t0, tw, tc2=tc2: (C.ab2[:, kt * 1024 + (t0 - tc2): kt * 1024 + (t0 - tc2) + tw], C.ghid_tok),
                 evac=out_evac, jblock=128,
                 tchunks=[(tc2, 512), (tc2 + 512, 512)],
                 wwait=W.get("tok_sw2") if e_ == 8 else W.get("tok_ew2"))
    return b.tok("pool")

def setup_u2(b, C, nc, inp, newt):
    W = {}
    W["wqbf"] = newt("u2_wq", [1024, 1024], BF16)
    W["tok_wq"] = convert_weight(b, C, inp["L0_ffn_Wq"], W["wqbf"])
    W["wkvbf"] = newt("u2_wkv", [1024, 256], BF16)
    W["tok_wkv"] = convert_weight(b, C, inp["L0_ffn_Wkv"], W["wkvbf"])
    W["wobf"] = newt("u2_wo", [1024, 1024], BF16)
    W["tok_wo"] = convert_weight(b, C, inp["L0_ffn_Wo"], W["wobf"])
    for nm, key in (("kup", "L0_ffn_Wk_up"), ("vup", "L0_ffn_Wv_up")):
        W[nm + "bf"] = newt("u2_" + nm, [256, 1024], BF16)
        src = inp[key]
        tok = b.dma("gpsimd",
                    W[nm + "bf"].ap().rearrange("l (h d) -> l h d", d=64),
                    bass.AP(src.tensor, src.offset, [[64, 256], [256 * 64, 16], [1, 64]]),
                    "dCv")
        W["tok_" + nm] = tok
    W["wrbf"] = newt("u2_wr", [1024, 8], BF16)
    wrc0 = b.dma("gpsimd", W["wrbf"].ap()[:, :], inp["L0_ffn_Wr"], "dCv")
    W["wr_f32"] = inp["L0_ffn_Wr"]
    W["rbias_f32"] = inp["L0_ffn_r_bias"][None, :]
    W["ln1_f32"] = inp["L0_ffn_ln1_w"]
    W["ln2_f32"] = inp["L0_ffn_ln2_w"]
    W["sw1bf"] = newt("u2_sw1", [1024, 2048], BF16)
    W["sw3bf"] = newt("u2_sw3", [1024, 2048], BF16)
    W["sw2bf"] = newt("u2_sw2", [2048, 1024], BF16)
    t1 = convert_weight(b, C, inp["L0_ffn_sw1"], W["sw1bf"])
    t2 = convert_weight(b, C, inp["L0_ffn_sw3"], W["sw3bf"])
    W["tok_sw"] = t2
    W["tok_sw2"] = convert_weight(b, C, inp["L0_ffn_sw2"], W["sw2bf"])
    W["ew1bf"] = newt("u2_ew1", [8192, 2048], BF16)
    W["ew3bf"] = newt("u2_ew3", [8192, 2048], BF16)
    W["ew2bf"] = newt("u2_ew2", [16384, 1024], BF16)
    t3 = convert_weight(b, C, inp["L0_ffn_ew1"].rearrange("e c h -> (e c) h"), W["ew1bf"])
    t4 = convert_weight(b, C, inp["L0_ffn_ew3"].rearrange("e c h -> (e c) h"), W["ew3bf"])
    W["tok_ew"] = t4
    W["tok_ew2"] = convert_weight(b, C, inp["L0_ffn_ew2"].rearrange("e h c -> (e h) c"), W["ew2bf"])
    W["ksp"] = newt("u2_ksp", [1024, T], BF16)
    W["vsp"] = newt("u2_vsp", [T, 1024], BF16)
    W["xnsp"] = newt("u2_xnsp", [1024, T], BF16)
    W["wrows"] = newt("u2_wrows", [9, T], F32)
    W["tab_cos"] = inp["tab_u2_cos"]
    W["tab_sins"] = inp["tab_u2_sins"]
    return W

UNIT_INPUTS["u2"] = ["L0_ffn_ln1_w", "L0_ffn_ln2_w", "L0_ffn_Wq", "L0_ffn_Wkv",
                     "L0_ffn_Wk_up", "L0_ffn_Wv_up", "L0_ffn_Wo", "L0_ffn_Wr",
                     "L0_ffn_r_bias", "L0_ffn_sw1", "L0_ffn_sw2", "L0_ffn_sw3",
                     "L0_ffn_ew1", "L0_ffn_ew2", "L0_ffn_ew3"]
UNIT_INPUTS["u4"] = UNIT_INPUTS["u2"]
UNIT_SETUP["u2"] = setup_u2
UNIT_SETUP["u4"] = setup_u2
UNIT_EMIT["u2"] = emit_u2
UNIT_EMIT["u4"] = emit_u2
UNIT_TABS["u2"] = ["tab_u2_cos", "tab_u2_sins"]
UNIT_TABS["u4"] = UNIT_TABS["u2"]
INPUT_SHAPES.update({
    "L0_ffn_ln1_w": [1024], "L0_ffn_ln2_w": [1024], "L0_ffn_Wq": [1024, 1024],
    "L0_ffn_Wkv": [1024, 256], "L0_ffn_Wk_up": [16, 256, 64], "L0_ffn_Wv_up": [16, 256, 64],
    "L0_ffn_Wo": [1024, 1024], "L0_ffn_Wr": [1024, 8], "L0_ffn_r_bias": [8],
    "L0_ffn_sw1": [1024, 2048], "L0_ffn_sw2": [2048, 1024], "L0_ffn_sw3": [1024, 2048],
    "L0_ffn_ew1": [8, 1024, 2048], "L0_ffn_ew2": [8, 2048, 1024], "L0_ffn_ew3": [8, 1024, 2048],
})

def xbc_ap(C, t, c0, cw):
    if t < 15:
        return C.ab2[:, 8192 + t * 515 + c0: 8192 + t * 515 + c0 + cw]
    return C.cb[:, (t - 15) * 515 + c0: (t - 15) * 515 + c0 + cw]

def emit_u1(b, C, W, in_tok=None):
    ntok = emit_norm(b, C, src=C.xres, dst=C.xn, nkt=8, eps=1e-5, mean=True,
                     src_tok=in_tok)
    for kt in range(8):
        b.emit("vector",
               lambda e, d=fms(C.xres, kt, 0, T), s=fms(C.xn, kt, 0, T):
                   e.tensor_tensor(d, d, s, op=ALU.add),
               waits=[ntok], inc="dve")
    al = b.dma("sync", C.tmp32[0:1, 0:32], W["A_log_f32"][None, :], "dC",
               waits=[C.tmp32_free])
    a1 = b.emit("scalar", lambda e: e.activation(C.tmp32[0:1, 0:32], C.tmp32[0:1, 0:32], AF.Exp),
                waits=[al], inc="act")
    a2 = b.emit("vector",
                lambda e: e.tensor_scalar(C.tmp32[0:1, 32:64], C.tmp32[0:1, 0:32], -1.0, None,
                                          op0=ALU.mult),
                waits=[a1], inc="dve")
    bkA = C.ps.acquire()
    a3 = bcast_row(b, C, C.tmp32[0:1, 32:64], bkA, waits=[a2])
    a4 = b.emit("vector", lambda e, s=bkA["t"][0:128, 0:32]: e.tensor_copy(C.bcols[:, 0:32], s),
                waits=[a3], inc="dve")
    bkA["free"] = a4
    db = b.dma("sync", C.bcols[0:32, 32:33], W["dt_bias_f32"][:, None], "dC")
    dl = b.dma("sync", C.tmp32[0:1, 0:32], W["D_f32"][None, :], "dC", waits=[a2])
    bkD = C.ps.acquire()
    d3 = b.emit("tensor",
                lambda e: e.matmul(bkD["t"][0:128, 0:32], C.ones_row128[0:1, :],
                                   C.tmp32[0:1, 0:32], start=True, stop=True),
                waits=[dl], inc="pe")
    d4 = b.emit("vector", lambda e, s=bkD["t"][0:128, 0:32]: e.tensor_copy(C.bcols[:, 64:96], s),
                waits=[d3], inc="dve")
    bkD["free"] = d4
    C.tmp32_free = d3
    ctoks = []
    for r in range(4):
        ctoks.append(load_cols(b, C, W["convwT_f32"][r, :], 17, C.bcols[:, 96 + r * 17: 113 + r * 17]))
    ctoks.append(load_cols(b, C, W["conv_b_f32"], 17, C.bcols[:, 164:181]))
    g_ssm = load_cols(b, C, W["ssm_norm_f32"], 16, C.gcols[:, 0:16])
    g_rms = load_cols(b, C, W["rms_w_f32"], 8, C.gcols[:, 16:24])
    cst_tok = b.tok("dve")
    G = C.cb[0:64, 2048:4096]
    gz = b.emit("gpsimd", lambda e: e.memset(G, 0.0), inc="pool")
    hz = b.emit("gpsimd", lambda e: e.memset(C.cb[:, 1030:1081], 0.0), inc="pool")
    z_guard = None
    ssd_guard = None
    for mc in range(4):
        def evac_ip(j, t0, tw, bank, mmtok, mw, mc=mc):
            if j < 16:
                return b.emit("vector",
                              lambda e, o=C.ab2[:, j * 512: j * 512 + 512], s=bank["t"][0:mw, 0:tw]:
                                  e.tensor_copy(o, s),
                              waits=[mmtok, z_guard], inc="dve")
            if j < 33:
                return b.emit("vector",
                              lambda e, o=xbc_ap(C, j - 16, 3, 512), s=bank["t"][0:mw, 0:tw]:
                                  e.tensor_copy(o, s),
                              waits=[mmtok, z_guard], inc="dve")
            return b.emit("vector",
                          lambda e, s=bank["t"][0:mw, 0:tw]:
                              e.tensor_copy(C.tmp32[0:32, 0:512], s),
                          waits=[mmtok, C.tmp32_free], inc="dve")
        gemm(b, C, w=W["inprojbf"].ap(), Cout=4256,
             rhs=lambda kt, t0, tw, mc=mc: (fms(C.xn, kt, mc * 512, 512), ntok),
             evac=evac_ip, jblock=512, tchunks=[(0, 512)], wwait=W.get("tok_inproj"))
        ip_done = b.tok("dve")
        conv_done = []
        for ct in range(17):
            hcp = b.emit("vector",
                         lambda e, o=xbc_ap(C, ct, 0, 3), s=C.cb[:, 1030 + ct * 3: 1033 + ct * 3]:
                             e.tensor_copy(o, s),
                         waits=[ip_done, b.tok("pool") if mc == 0 else None], inc="dve")
            t0_ = b.emit("vector",
                         lambda e, o=C.cbf[:, 0:512], s=xbc_ap(C, ct, 0, 512),
                                w0=C.bcols[:, 96 + ct:97 + ct]:
                             e.tensor_scalar(o, s, w0, None, op0=ALU.mult),
                         waits=[hcp, cst_tok, C.convtmp_free], inc="dve")
            for r in range(1, 4):
                t0_ = b.emit("vector",
                             lambda e, o=C.cbf[:, 0:512], s=xbc_ap(C, ct, r, 512),
                                    wr=C.bcols[:, 96 + r * 17 + ct: 97 + r * 17 + ct]:
                                 e.scalar_tensor_tensor(o, s, wr, C.cbf[:, 0:512],
                                                        op0=ALU.mult, op1=ALU.add),
                             waits=[t0_], inc="dve")
            hsv = b.emit("vector",
                         lambda e, o=C.cb[:, 1030 + ct * 3: 1033 + ct * 3], s=xbc_ap(C, ct, 512, 3):
                             e.tensor_copy(o, s),
                         waits=[t0_], inc="dve")
            sv = b.emit("scalar",
                        lambda e, o=xbc_ap(C, ct, 3, 512), s=C.cbf[:, 0:512],
                               cb_=C.bcols[:, 164 + ct:165 + ct]:
                            e.activation(o, s, AF.Silu, bias=cb_),
                        waits=[t0_, hsv], inc="act")
            C.convtmp_free = sv
            conv_done.append(sv)
        xbc_tok = b.tok("act")
        dts = C.cbf[0:32, 512:1024]
        s1 = b.emit("vector",
                    lambda e: e.tensor_scalar(dts, C.tmp32[0:32, 0:512], C.bcols[0:32, 32:33],
                                              None, op0=ALU.add),
                    waits=[ip_done, db, C.dts_free], inc="dve")
        s2 = b.emit("vector",
                    lambda e: e.scalar_tensor_tensor(C.cbf[0:32, 1024:1536], dts, -1.0, dts,
                                                     op0=ALU.mult, op1=ALU.max),
                    waits=[s1], inc="dve")
        s3 = b.emit("scalar",
                    lambda e: e.activation(C.cbf[0:32, 1024:1536], C.cbf[0:32, 1024:1536],
                                           AF.Exp, scale=-1.0),
                    waits=[s2], inc="act")
        ep1, etok1 = eps_ap(b, C, 1.0)
        s4 = b.emit("scalar",
                    lambda e: e.activation(C.cbf[0:32, 1024:1536], C.cbf[0:32, 1024:1536],
                                           AF.Ln, bias=ep1[0:32, :]),
                    waits=[s3, etok1], inc="act")
        s5 = b.emit("vector",
                    lambda e: e.tensor_scalar(dts, dts, 0.0, None, op0=ALU.max),
                    waits=[s4], inc="dve")
        dt_tok = b.emit("vector",
                        lambda e: e.tensor_tensor(dts, dts, C.cbf[0:32, 1024:1536], op=ALU.add),
                        waits=[s5], inc="dve")
        C.tmp32_free = dt_tok
        C.dts_free = None
        yg_toks = []
        for c in range(4):
            cc = c * 128
            bk = C.ps.acquire()
            tt_ = b.emit("tensor",
                         lambda e, o=bk["t"][0:128, 0:32], s=dts[:, cc:cc + 128]:
                             e.transpose(o, s, C.id_f32[0:32, 0:32]),
                         waits=[dt_tok], inc="pe")
            dtk = b.emit("vector",
                         lambda e, s=bk["t"][0:128, 0:32]: e.tensor_copy(C.cbf[:, 1536:1568], s),
                         waits=[tt_, ssd_guard], inc="dve")
            bk["free"] = dtk
            b.wait("tensor", C.ps.tr_free)
            tb_ = b.emit("tensor",
                         lambda e, o=C.ps.tr[0:128, 0:64], s=xbc_ap(C, 16, 3 + cc, 128)[0:64, :]:
                             e.transpose(o, s, C.id_bf[0:64, 0:64]),
                         waits=[xbc_tok], inc="pe")
            btk = b.emit("vector",
                         lambda e: e.tensor_copy(C.gel[:, 384:448], C.ps.tr[0:128, 0:64]),
                         waits=[tb_, ssd_guard], inc="dve")
            C.ps.tr_free = btk
            b.wait("tensor", C.ps.tr_free)
            tc1_ = b.emit("tensor",
                          lambda e, o=C.ps.tr[0:128, 128:192], s=xbc_ap(C, 16, 3 + cc, 128)[64:128, :]:
                              e.transpose(o, s, C.id_bf[64:128, 64:128]),
                          waits=[xbc_tok], inc="pe")
            ce1 = b.emit("vector",
                         lambda e: e.tensor_copy(C.cb[:, 1081:1145], C.ps.tr[0:128, 128:192]),
                         waits=[tc1_, C.csb_free], inc="dve")
            tc2_ = b.emit("tensor",
                          lambda e, o=C.ps.tr[0:64, 192:320], s=C.cb[:, 1081:1145]:
                              e.transpose(o, s, C.id_bf[:, :]),
                          waits=[ce1], inc="pe")
            ce2 = b.emit("vector",
                         lambda e: e.tensor_copy(C.cb[0:64, 1145:1273], C.ps.tr[0:64, 192:320]),
                         waits=[tc2_], inc="dve")
            C.ps.tr_free = ce2
            Csb = C.cb[0:64, 1145:1273]
            bkS = C.ps.acquire()
            ts_ = b.emit("tensor",
                         lambda e, o=bkS["t"][0:128, 0:128], l=xbc_ap(C, 16, 3 + cc, 128)[0:64, :],
                                r=Csb:
                             e.matmul(o, l, r, start=True, stop=True),
                         waits=[xbc_tok, ce2], inc="pe")
            smk = b.emit("vector",
                         lambda e, s=bkS["t"][0:128, 0:128]:
                             e.tensor_tensor(C.cbf[:, 1280:1408], s, C.triu[:, :], op=ALU.mult),
                         waits=[ts_, ssd_guard], inc="dve")
            bkS["free"] = smk
            C.csb_free = ts_
            for h in range(32):
                jt, rbase = h // 4, (h % 4) * 32
                jt = h // 2
                rbase = (h % 2) * 64
                dtcol = C.cbf[:, 1536 + h: 1537 + h]
                r1 = b.emit("vector",
                            lambda e, dc=dtcol: e.tensor_scalar(C.cbf[:, 1568:1696], C.triu[:, :],
                                                                dc, None, op0=ALU.mult),
                            waits=[dtk, C.rhsh_free], inc="dve")
                bkDm = C.ps.acquire()
                md = b.emit("tensor",
                            lambda e, o=bkDm["t"][0:128, 0:128]:
                                e.matmul(o, C.slow[:, :], C.cbf[:, 1568:1696], start=True, stop=True),
                            waits=[r1], inc=None)
                b.wait("tensor", C.ps.stats_free)
                mc_ = b.emit("tensor",
                             lambda e: e.matmul(C.ps.stats[0:1, 0:128], C.ones_col_f32[:, :],
                                                C.cbf[:, 1568:1696], start=True, stop=True),
                             inc="pe")
                eE = b.emit("scalar",
                            lambda e, o=C.cbf[:, 1024:1152], s=bkDm["t"][0:128, 0:128],
                                   sc=C.bcols[:, h:h + 1]:
                                e.activation(o, s, AF.Exp, scale=sc),
                            waits=[md if False else mc_, C.E_free], inc="act")
                bkDm["free"] = eE
                eL = b.emit("scalar",
                            lambda e, o=C.cbf[0:1, 1696:1824], s=C.ps.stats[0:1, 0:128],
                                   sc=C.bcols[0:1, h:h + 1]:
                                e.activation(o, s, AF.Exp, scale=sc),
                            waits=[mc_, C.eL_free], inc="act")
                C.ps.stats_free = eL
                m1 = b.emit("vector",
                            lambda e: e.tensor_tensor(C.cbf[:, 1152:1280], C.cbf[:, 1024:1152],
                                                      C.cbf[:, 1280:1408], op=ALU.mult),
                            waits=[eE, smk], inc="dve")
                m2 = b.emit("vector",
                            lambda e, dc=dtcol: e.tensor_scalar(C.gel[:, 0:128], C.cbf[:, 1152:1280],
                                                                dc, None, op0=ALU.mult),
                            waits=[m1, C.mstd_free], inc="dve")
                C.E_free = m1
                b.wait("tensor", C.ps.tr_free)
                tx_ = b.emit("tensor",
                             lambda e, o=C.ps.tr[0:128, 64:128],
                                    s=xbc_ap(C, jt, 3 + cc, 128)[rbase:rbase + 64, :],
                                    idn=C.id_bf[rbase:rbase + 64, rbase:rbase + 64]:
                                 e.transpose(o, s, idn),
                             waits=[xbc_tok], inc="pe")
                xtk = b.emit("vector",
                             lambda e: e.tensor_copy(C.gel[:, 128:192], C.ps.tr[0:128, 64:128]),
                             waits=[tx_, C.xstok_free], inc="dve")
                C.ps.tr_free = xtk
                bkEb = C.ps.acquire()
                eb = b.emit("tensor",
                            lambda e, o=bkEb["t"][0:64, 0:128]:
                                e.matmul(o, C.ones_row128[0:1, 0:64], C.cbf[0:1, 1696:1824],
                                         start=True, stop=True),
                            waits=[eL], inc="pe")
                C.eL_free = eb
                cd = b.emit("vector",
                            lambda e, o=C.gel[0:64, 192:320], s=C.cb[0:64, 1145:1273],
                                   bb=bkEb["t"][0:64, 0:128]:
                                e.tensor_tensor(o, s, bb, op=ALU.mult),
                            waits=[eb, ce2, C.cd_free], inc="dve")
                bkEb["free"] = cd
                bkY = C.ps.hold()
                y1_ = b.emit("tensor",
                             lambda e, o=bkY["t"][rbase:rbase + 64, 0:128]:
                                 e.matmul(o, C.gel[:, 128:192], C.gel[:, 0:128], start=True, stop=False),
                             waits=[m2, xtk], inc=None)
                y2_ = b.emit("tensor",
                             lambda e, o=bkY["t"][rbase:rbase + 64, 0:128], l=G[0:64, h * 64:(h + 1) * 64]:
                                 e.matmul(o, l, C.gel[0:64, 192:320], start=False, stop=True),
                             waits=[cd, C.G_wtok], inc="pe")
                C.mstd_free = y2_
                C.cd_free = y2_
                w1_ = b.emit("vector",
                             lambda e, dc=dtcol: e.scalar_tensor_tensor(
                                 C.colbuf[:, 40:41], C.cbf[:, 1151:1152], 1.0, C.cbf[:, 1536 + h:1537 + h],
                                 op0=ALU.mult, op1=ALU.mult) if False else
                                 e.tensor_tensor(C.colbuf[:, 40:41], C.cbf[:, 1151:1152], dc, op=ALU.mult),
                             waits=[eE, dtk], inc="dve")
                bw = b.emit("vector",
                            lambda e: e.tensor_scalar(C.gel[:, 320:384], C.gel[:, 384:448],
                                                      C.colbuf[:, 40:41], None, op0=ALU.mult),
                            waits=[w1_, btk, C.bw_free], inc="dve")
                bkU = C.ps.acquire()
                up = b.emit("tensor",
                            lambda e, o=bkU["t"][0:64, 0:64]:
                                e.matmul(o, C.gel[:, 320:384], C.gel[:, 128:192], start=True, stop=True),
                            waits=[bw, xtk], inc="pe")
                C.bw_free = up
                C.xstok_free = up
                bkSc = C.ps.acquire()
                sb_ = b.emit("tensor",
                             lambda e, o=bkSc["t"][0:64, 0:1]:
                                 e.matmul(o, C.ones_row128[0:1, 0:64], C.cbf[0:1, 1823:1824],
                                          start=True, stop=True),
                             waits=[eL], inc="pe")
                sc_ = b.emit("vector",
                             lambda e, s=bkSc["t"][0:64, 0:1]: e.tensor_copy(C.colbuf[0:64, 41:42], s),
                             waits=[sb_], inc="dve")
                bkSc["free"] = sc_
                gu = b.emit("vector",
                            lambda e, g=G[0:64, h * 64:(h + 1) * 64], u=bkU["t"][0:64, 0:64]:
                                e.scalar_tensor_tensor(g, g, C.colbuf[0:64, 41:42], u,
                                                       op0=ALU.mult, op1=ALU.add),
                            waits=[sc_, up, y2_], inc="dve")
                bkU["free"] = gu
                C.G_wtok = gu
                zs = b.emit("scalar",
                            lambda e, o=C.gel[rbase:rbase + 64, 448:576],
                                   s=C.ab2[:, jt * 512 + cc: jt * 512 + cc + 128][rbase:rbase + 64, :]:
                                e.activation(o, s, AF.Silu),
                            waits=[ip_done, C.zs_free], inc="act")
                t64 = b.emit("vector",
                             lambda e, o=C.tmp32[rbase:rbase + 64, 0:128],
                                    s=xbc_ap(C, jt, 3 + cc, 128)[rbase:rbase + 64, :],
                                    dd=C.bcols[rbase:rbase + 64, 64 + h:65 + h],
                                    y=bkY["t"][rbase:rbase + 64, 0:128]:
                                 e.scalar_tensor_tensor(o, s, dd, y, op0=ALU.mult, op1=ALU.add),
                             waits=[y2_, d4, C.t64_free], inc="dve")
                ygt = b.emit("vector",
                             lambda e, o=C.ab2[:, jt * 512 + cc: jt * 512 + cc + 128][rbase:rbase + 64, :],
                                    s=C.tmp32[rbase:rbase + 64, 0:128], z=C.gel[rbase:rbase + 64, 448:576]:
                                 e.tensor_tensor(o, s, z, op=ALU.mult),
                             waits=[t64, zs], inc="dve")
                C.zs_free = ygt
                C.t64_free = ygt
                C.rhsh_free = eE
                C.ps.release(bkY, t64)
                yg_toks.append(ygt)
        C.tmp32_free = b.tok("dve")
        ygn = emit_norm(b, C, src=C.ab2, dst=C.ab2, nkt=16, eps=1e-5, mean=False,
                        gamma_col=C.gcols, gamma_off=0, Tb=512, src_is_bf=True,
                        src_tok=yg_toks[-1])
        def evac_op(j, t0, tw, bank, mmtok, mw, mc=mc):
            b.emit("vector",
                   lambda e, d=fms(C.xres, j, mc * 512, 512), s=bank["t"][0:mw, 0:tw]:
                       e.tensor_tensor(d, d, s, op=ALU.add),
                   waits=[mmtok], inc="dve")
            return b.emit("vector",
                          lambda e, o=fms(C.ab3, j, mc * 512, 512), s=fms(C.xn, j, mc * 512, 512),
                                 p=bank["t"][0:mw, 0:tw]:
                              e.tensor_tensor(o, s, p, op=ALU.add),
                          waits=[mmtok], inc="dve")
        gemm(b, C, w=W["outprojbf"].ap(), Cout=1024,
             rhs=lambda kt, t0, tw: (C.ab2[:, kt * 512: kt * 512 + 512], ygn),
             evac=evac_op, jblock=256, tchunks=[(0, 512)], wwait=W.get("tok_outproj"))
        z_guard = b.tok("pe")
        ssd_guard = b.tok("pe")
    ssd_guard = ssd_guard
    x1_tok = b.tok("dve")
    xn2 = emit_norm(b, C, src=C.ab3, dst=C.xn, nkt=8, eps=1e-5, mean=False,
                    gamma_col=C.gcols, gamma_off=16, src_is_bf=True, src_tok=x1_tok)
    for tc2 in (0, 512, 1024, 1536):
        for jb0 in range(0, 4096, 512):
            dstA, dstB = C.wslot[0], C.wslot[1]
            wtokA = b.dma("sync", dstA[:, 0:4096].rearrange("p (kt j) -> p kt j", kt=8),
                          W["fc1bf"].ap()[:, jb0:jb0 + 512].rearrange("(kt p) j -> p kt j", p=128),
                          "dW", waits=[x for x in (C.wslot_tok[0], W.get("tok_fc1")) if x])
            wtokB = b.dma("sync", dstB[:, 0:4096].rearrange("p (kt j) -> p kt j", kt=8),
                          W["fc1bf"].ap()[:, 4096 + jb0: 4096 + jb0 + 512].rearrange("(kt p) j -> p kt j", p=128),
                          "dW", waits=[x for x in (C.wslot_tok[1], W.get("tok_fc1")) if x])
            lastpe = None
            for jj in range(0, 512, 128):
                j = (jb0 + jj) // 128
                for t0 in (tc2,):
                    bk1 = C.ps.acquire()
                    b.wait("tensor", wtokA)
                    b.wait("tensor", wtokB)
                    for kt in range(8):
                        b.emit("tensor",
                               lambda e, o=bk1["t"][0:128, 0:512],
                                      l=dstA[:, kt * 512 + jj: kt * 512 + jj + 128],
                                      r=fms(C.xn, kt, t0, 512), s=(kt == 0), p=(kt == 7):
                                   e.matmul(o, l, r, start=s, stop=p),
                               waits=[xn2] if kt == 0 else [], inc="pe" if kt == 7 else None)
                    bk3 = C.ps.acquire()
                    mt3 = None
                    for kt in range(8):
                        mt3 = b.emit("tensor",
                                     lambda e, o=bk3["t"][0:128, 0:512],
                                            l=dstB[:, kt * 512 + jj: kt * 512 + jj + 128],
                                            r=fms(C.xn, kt, t0, 512), s=(kt == 0), p=(kt == 7):
                                         e.matmul(o, l, r, start=s, stop=p),
                                     inc="pe" if kt == 7 else None)
                    lastpe = mt3
                    sl = b.emit("scalar",
                                lambda e, o=C.gel[:, 0:512], s=bk3["t"][0:128, 0:512]:
                                    e.activation(o, s, AF.Silu),
                                waits=[mt3, C.gel_free[0]], inc="act")
                    hv = b.emit("vector",
                                lambda e, o=C.ab2[:, j * 512: j * 512 + 512],
                                       s=C.gel[:, 0:512], p1=bk1["t"][0:128, 0:512]:
                                    e.tensor_tensor(o, p1, s, op=ALU.mult),
                                waits=[sl], inc="dve")
                    C.gel_free[0] = hv
                    bk1["free"] = hv
                    bk3["free"] = hv
                    C.ghid_tok = hv
            C.wslot_tok[0] = lastpe
            C.wslot_tok[1] = lastpe
        def evac_fc2(j, t0, tw, bank, mmtok, mw):
            return b.emit("vector",
                          lambda e, d=fms(C.xres, j, t0, tw), s=bank["t"][0:mw, 0:tw]:
                              e.tensor_tensor(d, d, s, op=ALU.add),
                          waits=[mmtok], inc="dve")
        gemm(b, C, w=W["fc2bf"].ap(), Cout=1024,
             rhs=lambda kt, t0, tw: (C.ab2[:, kt * 512: kt * 512 + tw], C.ghid_tok),
             evac=evac_fc2, jblock=128,
             tchunks=[(tc2, 512)], wwait=W.get("tok_fc2"))
    return b.tok("dve")

def setup_u1(b, C, nc, inp, newt):
    W = {}
    W["inprojbf"] = newt("u1_inproj", [1024, 4256], BF16)
    W["tok_inproj"] = convert_weight(b, C, inp["hl_u1_inproj"], W["inprojbf"])
    W["outprojbf"] = newt("u1_outproj", [2048, 1024], BF16)
    W["tok_outproj"] = convert_weight(b, C, inp["hl_u1_outproj"], W["outprojbf"])
    W["fc1bf"] = newt("u1_fc1", [1024, 8192], BF16)
    W["tok_fc1"] = convert_weight(b, C, inp["hl_u1_fc1"], W["fc1bf"])
    W["fc2bf"] = newt("u1_fc2", [4096, 1024], BF16)
    W["tok_fc2"] = convert_weight(b, C, inp["hl_u1_fc2"], W["fc2bf"])
    W["convwT_f32"] = inp["hl_u1_convwT"]
    W["conv_b_f32"] = inp["L0_attn_a_conv_b"]
    W["dt_bias_f32"] = inp["L0_attn_a_dt_bias"]
    W["A_log_f32"] = inp["L0_attn_a_A_log"]
    W["D_f32"] = inp["L0_attn_a_D"]
    W["ssm_norm_f32"] = inp["L0_attn_a_ssm_norm_w"]
    W["rms_w_f32"] = inp["L0_attn_a_rms_w"]
    return W

UNIT_INPUTS["u1"] = ["hl_u1_inproj", "hl_u1_outproj", "hl_u1_fc1", "hl_u1_fc2",
                     "hl_u1_convwT", "L0_attn_a_conv_b", "L0_attn_a_dt_bias",
                     "L0_attn_a_A_log", "L0_attn_a_D", "L0_attn_a_ssm_norm_w",
                     "L0_attn_a_rms_w"]
UNIT_SETUP["u1"] = setup_u1
UNIT_EMIT["u1"] = emit_u1
INPUT_SHAPES.update({
    "hl_u1_inproj": [1024, 4256], "hl_u1_outproj": [2048, 1024],
    "hl_u1_fc1": [1024, 8192], "hl_u1_fc2": [4096, 1024], "hl_u1_convwT": [4, 2176],
    "L0_attn_a_conv_b": [2176], "L0_attn_a_dt_bias": [32], "L0_attn_a_A_log": [32],
    "L0_attn_a_D": [32], "L0_attn_a_ssm_norm_w": [2048], "L0_attn_a_rms_w": [1024],
})

HOST_LAYOUT = {
    "hl_u1_inproj": ("L0_attn_a_in_proj_w", lambda w: np.ascontiguousarray(w.T)),
    "hl_u1_outproj": ("L0_attn_a_out_proj_w", lambda w: np.ascontiguousarray(w.T)),
    "hl_u1_fc1": ("L0_attn_a_fc1_w", lambda w: np.ascontiguousarray(w.T)),
    "hl_u1_fc2": ("L0_attn_a_fc2_w", lambda w: np.ascontiguousarray(w.T)),
    "hl_u1_convwT": ("L0_attn_a_conv_w", lambda w: np.ascontiguousarray(w.T)),
}


# version 04 (final)
# speedup vs baseline: 1.1794x; 1.1794x over previous
import math
from collections import defaultdict
from contextlib import ExitStack

import numpy as np
import ml_dtypes
import concourse.bass as bass
import concourse.mybir as mybir
from concourse.bass_utils import run_bass_kernel_spmd

F32 = mybir.dt.float32
BF16 = mybir.dt.bfloat16
AF = mybir.ActivationFunctionType
ALU = mybir.AluOpType
AX = mybir.AxisListType

T = 2048
D = 1024
NCORES = 8
NEG = -1.0e30

def _tobf(x):
    return np.asarray(x, dtype=ml_dtypes.bfloat16)

def host_constants():
    c = {}
    ar = np.arange(128)
    c["cst_id_bf"] = _tobf(np.eye(128, dtype=np.float32))
    c["cst_id_f32"] = np.eye(128, dtype=np.float32)
    c["cst_ones_col_bf"] = _tobf(np.ones((128, 1), np.float32))
    c["cst_ones_col_f32"] = np.ones((128, 1), np.float32)
    c["cst_ones_row128_f32"] = np.ones((1, 128), np.float32)
    c["cst_triu_bf"] = _tobf((ar[:, None] <= ar[None, :]).astype(np.float32))
    c["cst_slow_f32"] = (ar[:, None] > ar[None, :]).astype(np.float32)
    c["cst_amask_bf"] = _tobf(np.where(ar[None, :] <= ar[:, None], 0.0, NEG).astype(np.float32))
    c["cst_iota8_f32"] = np.broadcast_to(np.arange(8, dtype=np.float32), (128, 8)).copy()
    sw = np.zeros((128, 128), np.float32)
    sw[np.arange(128), np.arange(128) ^ 32] = 1.0
    c["cst_swap64_bf"] = _tobf(sw)
    inv = 1.0 / (1.0e5 ** (np.arange(0, 64, 2, dtype=np.float32) / 64.0))
    f = np.arange(T, dtype=np.float32)[None, :] * inv[:, None]
    sgn2 = np.where(np.arange(128)[:, None] % 64 < 32, -1.0, 1.0).astype(np.float32)
    c["tab_u2_cos"] = _tobf(np.tile(np.cos(f), (4, 1)))
    c["tab_u2_sins"] = _tobf(np.tile(np.sin(f), (4, 1)) * sgn2)
    inv7 = 1.0 / (1.0e4 ** (np.arange(32, dtype=np.float32) / 32.0))
    f7 = np.arange(T, dtype=np.float32)[None, :] * inv7[:, None]
    c["tab_u7_cos"] = _tobf(np.tile(np.cos(f7), (4, 1)))
    sgn = np.where(np.arange(128)[:, None] % 64 < 32, -1.0, 1.0).astype(np.float32)
    c["tab_u7_sins"] = _tobf(np.tile(np.sin(f7), (4, 1)) * sgn)
    inv3 = 1.0 / (1.0e4 ** (np.arange(0, 32, 2, dtype=np.float32) / 32.0))
    ang3 = np.arange(T, dtype=np.float32)[None, :] * inv3[:, None]
    r = np.arange(128)
    rr = r % 64
    cos_rows = np.ones((128, T), np.float32)
    sin_rows = np.zeros((128, T), np.float32)
    rot = rr < 32
    cos_rows[rot] = np.cos(ang3)[(rr[rot] // 2)]
    sgn3 = np.where(r % 2 == 0, -1.0, 1.0)
    sin_rows[rot] = np.sin(ang3)[(rr[rot] // 2)] * sgn3[rot, None]
    c["tab_u3_cosfm"] = _tobf(cos_rows)
    c["tab_u3_sinsfm"] = _tobf(sin_rows)
    swx = np.zeros((128, 128), np.float32)
    dst = np.where(rr < 32, r ^ 1, r)
    swx[r, dst] = 1.0
    c["cst_swapx_bf"] = _tobf(swx)
    delta = np.arange(-2047, 2048)
    n = -delta
    ret = (n < 0).astype(np.int64) * 16
    na = np.abs(n)
    vil = 8 + (np.log(np.maximum(na, 1).astype(np.float32) / 8.0) / math.log(16.0) * 8.0).astype(np.int32)
    vil = np.minimum(vil, 15)
    bucket = ret + np.where(na < 8, na, vil)
    R = np.zeros((32, 4095), np.float32)
    R[bucket, np.arange(4095)] = 1.0
    c["tab_u5_R"] = _tobf(R)
    return c

NORM_DBG = None
_HOSTC = None
def hostc():
    global _HOSTC
    if _HOSTC is None:
        _HOSTC = host_constants()
    return _HOSTC

ENGINES = ("tensor", "vector", "scalar", "gpsimd", "sync")

class B:
    def __init__(self, nc):
        self.nc = nc
        self.q = {e: [] for e in ENGINES}
        self.semh = {}
        self.cnt = defaultdict(int)
        self.lastwait = {e: defaultdict(int) for e in ENGINES}

    def on(self, eng, fn):
        self.q[eng].append(fn)

    def tok(self, sem):
        return (sem, self.cnt[sem])

    def wait(self, eng, tok):
        if tok is None:
            return
        sem, val = tok
        if val <= 0 or self.lastwait[eng][sem] >= val:
            return
        assert val <= self.cnt[sem], f"wait on future token {sem}={val}>{self.cnt[sem]}"
        self.lastwait[eng][sem] = val
        self.on(eng, lambda e, s=sem, v=val: e.wait_ge(self.semh[s], v))

    def emit(self, eng, fn, waits=(), inc=None, inc_n=1):
        for t in waits:
            self.wait(eng, t)
        if inc is None:
            self.on(eng, fn)
            return None
        self.cnt[inc] += inc_n
        tok = (inc, self.cnt[inc])
        self.on(eng, lambda e, f=fn, s=inc, n=inc_n: f(e).then_inc(self.semh[s], n))
        return tok

    def dma(self, queue, out_ap, in_ap, sem=None, waits=(), **kw):
        sem = sem or ("dS" if queue == "sync" else "dG")
        return self.emit(queue, lambda e: e.dma_start(out_ap, in_ap, **kw),
                         waits=waits, inc=sem, inc_n=16)

    def finalize(self, stack):
        for name in list(self.cnt):
            self.semh[name] = stack.enter_context(self.nc.semaphore(name))
        block = stack.enter_context(self.nc.Block())
        for eng in ENGINES:
            if not self.q[eng]:
                continue
            def body(E, eng=eng):
                for f in self.q[eng]:
                    f(E)
            getattr(block, eng)(body)


class PsumPool:
    def __init__(self, b, stack, n_f32=6):
        self.b = b
        self.banks = []
        for i in range(n_f32):
            t = stack.enter_context(b.nc.psum_tensor(f"psum{i}", [128, 512], F32))
            self.banks.append({"t": t, "free": None})
        self.idx = 0
        self.stats = stack.enter_context(b.nc.psum_tensor("psum_stats", [128, 512], F32))
        self.stats_free = None
        self.tr = stack.enter_context(b.nc.psum_tensor("psum_tr", [128, 512], BF16))
        self.tr_free = None

    def acquire(self):
        bk = self.banks[self.idx]
        self.idx = (self.idx + 1) % len(self.banks)
        self.b.wait("tensor", bk["free"])
        return bk

    def hold(self):
        bk = self.banks.pop(self.idx % len(self.banks))
        self.idx = self.idx % max(1, len(self.banks))
        self.b.wait("tensor", bk["free"])
        return bk

    def release(self, bk, tok):
        bk["free"] = tok
        self.banks.insert(self.idx, bk)


class Ctx:
    pass

def fms(buf, kt, t0, tw, Tb=T):
    return buf[:, kt * Tb + t0: kt * Tb + t0 + tw]

def unit_barrier(b, C):
    toks = [b.tok(s) for s in ("pe", "dve", "act", "pool") if b.cnt[s]]
    dtoks = [b.tok(s) for s in list(b.cnt)
             if s.startswith("d") and s != "dCv" and b.cnt[s]]
    for eng in ("tensor", "vector", "scalar", "gpsimd", "sync"):
        for t in toks + dtoks:
            b.wait(eng, t)
    mk = {}
    mk["pe"] = b.emit("tensor", lambda e: e.nop(), inc="pe")
    mk["dve"] = b.emit("vector", lambda e: e.engine_nop(), inc="dve")
    mk["act"] = b.emit("scalar", lambda e: e.nop(), inc="act")
    mk["pool"] = b.emit("gpsimd", lambda e: e.engine_nop(), inc="pool")
    C.ps.stats_free = None
    C.ps.tr_free = None
    for bk in C.ps.banks:
        bk["free"] = None
    C.wslot_tok = [None, None]
    C.gel_free = [None, None]
    C.rowA_free = None
    C.rowB_free = None
    C.tmp32_free = None
    C.xres_pool_tok = None
    C.cb_guard = None
    for nm in ("convtmp_free", "dts_free", "E_free", "eL_free", "mstd_free",
               "xstok_free", "cd_free", "bw_free", "zs_free", "t64_free",
               "rhsh_free", "G_wtok", "vstage_free", "rope_sw_free", "csb_free",
               "kp_free", "vp_free", "S_free", "q_tok"):
        setattr(C, nm, None)
    return mk["dve"]

def bcast_row(b, C, row_ap, bank, waits=()):
    tw = row_ap.shape[-1]
    return b.emit("tensor",
                  lambda e: e.matmul(bank["t"][0:128, 0:tw], C.ones_row128[0:1, :],
                                     row_ap, start=True, stop=True),
                  waits=waits, inc="pe")

def gemm(b, C, *, w, Cout, rhs, evac, jblock=512, tchunks=None, wsem="dW",
         wqueue="sync", wwait=None, extra_stop=None):
    Cin = w.shape[0]
    Kt = Cin // 128
    assert Cin % 128 == 0 and Kt * jblock <= 4096
    tchunks = tchunks or [(t0, 512) for t0 in range(0, T, 512)]
    nblocks = (Cout + jblock - 1) // jblock
    for ib in range(nblocks):
        j0 = ib * jblock
        jw = min(jblock, Cout - j0)
        si = C.wslot_idx
        C.wslot_idx = 1 - C.wslot_idx
        dst = C.wslot[si]
        in_ap = w[:, j0:j0 + jw].rearrange("(kt p) j -> p kt j", p=128)
        out_ap = dst[:, 0:Kt * jw].rearrange("p (kt j) -> p kt j", kt=Kt)
        wtok = b.dma(wqueue, out_ap, in_ap, wsem,
                     waits=[x for x in (C.wslot_tok[si], wwait) if x])
        last_pe = None
        for jj in range(0, jw, 128):
            j = (j0 + jj) // 128
            mw = min(128, jw - jj)
            for (t0, tw) in tchunks:
                bk = C.ps.acquire()
                b.wait("tensor", wtok)
                for kt in range(Kt):
                    r, rtok = rhs(kt, t0, tw)
                    if rtok is not None:
                        b.wait("tensor", rtok)
                    stop = (kt == Kt - 1) and extra_stop is None
                    tokm = b.emit("tensor",
                                  lambda e, o=bk["t"][0:mw, 0:tw],
                                         l=dst[:, kt * jw + jj: kt * jw + jj + mw],
                                         r=r, s=(kt == 0), p=stop:
                                      e.matmul(o, l, r, start=s, stop=p),
                                  inc="pe" if stop else None)
                if extra_stop is not None:
                    tokm = extra_stop(j, mw, bk, t0, tw)
                last_pe = tokm
                bk["free"] = evac(j, t0, tw, bk, tokm, mw)
        C.wslot_tok[si] = last_pe


def emit_norm(b, C, *, src, dst, nkt, eps, mean, gamma_col=None, gamma_off=0,
              Tb=T, t_range=None, src_tok=None, src_is_bf=False):
    nF = nkt * 128
    cp = [src_tok] * nkt
    if not src_is_bf:
        for kt in range(nkt):
            cp[kt] = b.emit("vector",
                            lambda e, s=fms(src, kt, 0, Tb, Tb), d=fms(dst, kt, 0, Tb, Tb):
                                e.tensor_copy(d, s),
                            waits=[src_tok] if src_tok else [], inc="dve")
        stat_src = dst
    else:
        stat_src = src
    lastdve = None
    for t0, tw in (t_range or [(x, 512) for x in range(0, Tb, 512)]):
        b.wait("tensor", C.ps.stats_free)
        st_tok = None
        for kt in range(nkt):
            si = kt % 2
            sqt = b.emit("scalar",
                         lambda e, s=fms(stat_src, kt, t0, tw, Tb),
                                d=C.gel[:, si * 512: si * 512 + tw]:
                             e.activation(d, s, AF.Square),
                         waits=[cp[kt], C.gel_free[si]], inc="act")
            if mean:
                b.emit("tensor",
                       lambda e, r=fms(stat_src, kt, t0, tw, Tb),
                              o=C.ps.stats[0:1, 0:tw], s=(kt == 0), p=(kt == nkt - 1):
                           e.matmul(o, C.ones_col_bf[:, :], r, start=s, stop=p),
                       waits=[cp[kt]], inc="pe")
            st_tok = b.emit("tensor",
                            lambda e, r=C.gel[:, si * 512: si * 512 + tw],
                                   o=C.ps.stats[32:33, 0:tw], s=(kt == 0), p=(kt == nkt - 1):
                                e.matmul(o, C.ones_col_bf[:, :], r, start=s, stop=p),
                            waits=[sqt], inc="pe")
            C.gel_free[si] = st_tok
        if mean:
            t_nm = b.emit("vector",
                          lambda e, w=tw: e.tensor_scalar(C.rowA[0:1, 0:w], C.ps.stats[0:1, 0:w],
                                                          -1.0 / nF, None, op0=ALU.mult),
                          waits=[st_tok, C.rowA_free], inc="dve")
            t_ms = b.emit("scalar",
                          lambda e, w=tw: e.activation(C.rowB[0:1, 0:w], C.rowA[0:1, 0:w], AF.Square),
                          waits=[t_nm, C.rowB_free], inc="act")
            t_var = b.emit("vector",
                           lambda e, w=tw: e.scalar_tensor_tensor(C.rowB[0:1, 0:w], C.ps.stats[32:33, 0:w],
                                                                  1.0 / nF, C.rowB[0:1, 0:w],
                                                                  op0=ALU.mult, op1=ALU.subtract),
                           waits=[st_tok, t_ms], inc="dve")
        else:
            t_var = b.emit("vector",
                           lambda e, w=tw: e.tensor_scalar(C.rowB[0:1, 0:w], C.ps.stats[32:33, 0:w],
                                                           1.0 / nF, None, op0=ALU.mult),
                           waits=[st_tok, C.rowB_free], inc="dve")
        C.ps.stats_free = t_var
        eap, etok = eps_ap(b, C, eps)
        t_ln = b.emit("scalar",
                      lambda e, w=tw, a=eap[0:1, :]: e.activation(C.rowB[0:1, 0:w], C.rowB[0:1, 0:w], AF.Ln, bias=a),
                      waits=[t_var, etok], inc="act")
        t_rs = b.emit("scalar",
                      lambda e, w=tw: e.activation(C.rowB[0:1, 0:w], C.rowB[0:1, 0:w], AF.Exp, scale=-0.5),
                      waits=[t_ln], inc="act")
        bkM = None
        if mean:
            bkM = C.ps.acquire()
            tM = bcast_row(b, C, C.rowA[0:1, 0:tw], bkM, waits=[t_nm])
            C.rowA_free = tM
        bkR = C.ps.acquire()
        tR = bcast_row(b, C, C.rowB[0:1, 0:tw], bkR, waits=[t_rs])
        C.rowB_free = tR
        if NORM_DBG is not None:
            ch = t0 // 512
            if mean:
                C.rowA_free = b.dma("sync", NORM_DBG[0:1, ch * 1024: ch * 1024 + tw],
                                    C.rowA[0:1, 0:tw], "dO", waits=[t_nm, tM])
            C.rowB_free = b.dma("sync", NORM_DBG[0:1, ch * 1024 + 512: ch * 1024 + 512 + tw],
                                C.rowB[0:1, 0:tw], "dO", waits=[t_rs, tR])
        for kt in range(nkt):
            if mean:
                ta = b.emit("vector",
                            lambda e, s=fms(src, kt, t0, tw, Tb), d=fms(dst, kt, t0, tw, Tb), w=tw,
                                   bm=bkM["t"]:
                                e.tensor_tensor(d, s, bm[0:128, 0:w], op=ALU.add),
                            waits=[tM], inc="dve")
                lastdve = b.emit("vector",
                                 lambda e, d=fms(dst, kt, t0, tw, Tb), w=tw, br=bkR["t"]:
                                     e.tensor_tensor(d, d, br[0:128, 0:w], op=ALU.mult),
                                 waits=[tR, ta], inc="dve")
            elif gamma_col is not None:
                lastdve = b.emit("vector",
                                 lambda e, s=fms(src, kt, t0, tw, Tb), d=fms(dst, kt, t0, tw, Tb),
                                        g=gamma_col[:, gamma_off + kt: gamma_off + kt + 1], w=tw,
                                        br=bkR["t"]:
                                     e.scalar_tensor_tensor(d, s, g, br[0:128, 0:w],
                                                            op0=ALU.mult, op1=ALU.mult),
                                 waits=[tR], inc="dve")
            else:
                lastdve = b.emit("vector",
                                 lambda e, s=fms(src, kt, t0, tw, Tb), d=fms(dst, kt, t0, tw, Tb), w=tw,
                                        br=bkR["t"]:
                                     e.tensor_tensor(d, s, br[0:128, 0:w], op=ALU.mult),
                                 waits=[tR], inc="dve")
        if mean:
            bkM["free"] = lastdve
        bkR["free"] = lastdve
    return lastdve

def convert_weight(b, C, src_ap, dst_t, transpose=False, wtmp_t=None):
    rows, cols = src_ap.shape
    if not transpose:
        tok = None
        step = max(128, ((4 << 20) // max(cols, 1)) & ~127)
        r = 0
        while r < rows:
            rr = min(step, rows - r)
            tok = b.dma("gpsimd", dst_t.ap()[r:r + rr, :], src_ap[r:r + rr, :], "dCv")
            r += rr
        return tok
    rows128 = (rows // 128) * 128
    ctok = None
    step = max(128, ((4 << 20) // max(cols, 1)) & ~127)
    r = 0
    while r < rows:
        rr = min(step, rows - r)
        ctok = b.dma("gpsimd", wtmp_t.ap()[r:r + rr, :], src_ap[r:r + rr, :], "dCv")
        r += rr
    tok = None
    for c0 in range(0, cols, 128):
        si = C.wslot_idx
        C.wslot_idx = 1 - C.wslot_idx
        dst_sb = C.wslot[si]
        assert rows128 <= 4096
        ttok = b.dma("sync", dst_sb[:, 0:rows128], wtmp_t.ap()[0:rows128, c0:c0 + 128],
                     "dTr", waits=[x for x in (ctok, C.wslot_tok[si]) if x], transpose=True)
        tok = b.dma("sync", dst_t.ap()[c0:c0 + 128, 0:rows128], dst_sb[:, 0:rows128],
                    "dTr", waits=[ttok])
        C.wslot_tok[si] = ("dTr", tok[1])
    return tok

def make_ctx(b, stack):
    nc = b.nc
    C = Ctx()
    C.ps = PsumPool(b, stack)
    A = lambda name, shape, dt: stack.enter_context(nc.sbuf_tensor(name, shape, dt))
    C.xres = A("xres", [128, 8 * T], F32)
    C.xn = A("xn", [128, 8 * T], BF16)
    C.ab2 = A("ab2", [128, 8 * T], BF16)
    C.ab3 = A("ab3", [128, 8 * T], BF16)
    C.cb = A("cb", [128, 4608], BF16)
    C.cbf = A("cbf", [128, 2048], F32)
    C.wslot = [A("wslot0", [128, 4096], BF16), A("wslot1", [128, 4096], BF16)]
    C.wslot_idx = 0
    C.wslot_tok = [None, None]
    C.gel = A("gel", [128, 1024], BF16)
    C.gel_free = [None, None]
    C.rowA = A("rowA", [1, 512], F32)
    C.rowB = A("rowB", [1, 512], F32)
    C.rowA_free = None
    C.rowB_free = None
    C.tmp32 = A("tmp32", [128, 512], F32)
    C.tmp32_free = None
    C.colbuf = A("colbuf", [128, 64], F32)
    C.router = A("router", [128, 256], F32)
    C.gcols = A("gcols", [128, 32], F32)
    C.bcols = A("bcols", [128, 576], F32)
    C.wrsb = A("wrsb", [128, 64], BF16)
    C.xres_pool_tok = None
    C.cb_guard = None
    for nm in ("convtmp_free", "dts_free", "E_free", "eL_free", "mstd_free",
               "xstok_free", "cd_free", "bw_free", "zs_free", "t64_free",
               "rhsh_free", "G_wtok", "vstage_free", "ghid_tok", "rope_sw_free", "csb_free",
               "kp_free", "vp_free", "S_free", "q_tok"):
        setattr(C, nm, None)
    C.epsmap = {}
    C.id_bf = A("id_bf", [128, 128], BF16)
    C.id_f32 = A("id_f32", [128, 128], F32)
    C.ones_col_bf = A("ones_col_bf", [128, 1], BF16)
    C.ones_col_f32 = A("ones_col_f32", [128, 1], F32)
    C.ones_row128 = A("ones_row128", [1, 128], F32)
    C.triu = A("triu", [128, 128], BF16)
    C.slow = A("slow", [128, 128], F32)
    C.amask = A("amask", [128, 128], BF16)
    C.iota8 = A("iota8", [128, 8], F32)
    C.swap64 = A("swap64", [128, 128], BF16)
    C.swapx = A("swapx", [128, 128], BF16)
    return C

CONST_SPECS = [
    ("cst_id_bf", "id_bf"), ("cst_id_f32", "id_f32"),
    ("cst_ones_col_bf", "ones_col_bf"), ("cst_ones_col_f32", "ones_col_f32"),
    ("cst_ones_row128_f32", "ones_row128"),
    ("cst_triu_bf", "triu"), ("cst_slow_f32", "slow"),
    ("cst_amask_bf", "amask"), ("cst_iota8_f32", "iota8"),
    ("cst_swap64_bf", "swap64"), ("cst_swapx_bf", "swapx"),
]

def load_cols(b, C, src1d, njt, dst, waits=()):
    assert njt <= 64
    s1 = b.dma("sync", C.tmp32[0:njt, 0:128], src1d.rearrange("(a p) -> a p", p=128),
               "dC", waits=[C.tmp32_free] + list(waits) if C.tmp32_free else list(waits))
    bk = C.ps.acquire()
    tt = b.emit("tensor",
                lambda e, o=bk["t"][0:128, 0:njt], s=C.tmp32[0:njt, 0:128], n=njt:
                    e.transpose(o, s, C.id_f32[0:n, 0:n]),
                waits=[s1], inc="pe")
    C.tmp32_free = tt
    ev = b.emit("vector", lambda e, o=dst, s=bk["t"][0:128, 0:njt]: e.tensor_copy(o, s),
                waits=[tt], inc="dve")
    bk["free"] = ev
    return ev

def eps_ap(b, C, eps):
    if eps not in C.epsmap:
        slot = 56 + len(C.epsmap)
        tok = b.emit("gpsimd", lambda e, s=slot, v=eps: e.memset(C.colbuf[:, s:s + 1], v),
                     inc="pool")
        C.epsmap[eps] = (C.colbuf[:, slot:slot + 1], tok)
    return C.epsmap[eps]

def load_consts(b, C, inp):
    for name, attr in CONST_SPECS:
        b.dma("sync", getattr(C, attr)[:, :], inp[name], "dC")
    return b.tok("dC")

GELU_C = 0.044715
GELU_SQ = math.sqrt(GELU_C)
GELU_S = 2.0 * math.sqrt(2.0 / math.pi)

def ghid_ap(C, j, toff, tw):
    g = j * 1024 + toff
    if j < 16:
        return C.ab2[:, g: g + tw]
    return C.ab3[:, g - 16384: g - 16384 + tw]

def emit_u6(b, C, W, in_tok=None):
    ntok = emit_norm(b, C, src=C.xres, dst=C.xn, nkt=8, eps=1e-5, mean=True,
                     src_tok=in_tok)
    wrc = b.dma("gpsimd", W["wrbf"].ap()[:, :], W["wr_f32"], "dCv")
    wr_tok = b.dma("sync", C.wrsb[:, 0:64].rearrange("p (kt j) -> p kt j", kt=8),
                   W["wrbf"].ap().rearrange("(kt p) j -> p kt j", p=128), "dW",
                   waits=[wrc])
    lgt = C.router[:, 0:128]
    wf = C.router[:, 128:256]
    for tt4 in range(0, 16, 4):
        bk = C.ps.acquire()
        for ti in range(tt4, tt4 + 4):
            mt = None
            for kt in range(8):
                mt = b.emit("tensor",
                            lambda e, o=bk["t"][0:128, (ti % 4) * 8:(ti % 4) * 8 + 8],
                                   l=fms(C.xn, kt, ti * 128, 128),
                                   r=C.wrsb[:, kt * 8:(kt + 1) * 8],
                                   s=(kt == 0), p=(kt == 7):
                                e.matmul(o, l, r, start=s, stop=p),
                            waits=[wr_tok, ntok], inc="pe" if kt == 7 else None)
        ev = b.emit("vector",
                    lambda e, o=lgt[:, tt4 * 8: tt4 * 8 + 32], s=bk["t"][0:128, 0:32]:
                        e.tensor_copy(o, s),
                    waits=[mt], inc="dve")
        bk["free"] = ev
    lgt_tok = b.tok("dve")
    if DEBUG_DUMPS:
        b.emit("vector", lambda e: e.tensor_copy(C.gcols[:, 0:32], lgt[:, 0:32]),
               waits=[lgt_tok], inc="dve")
        b.emit("vector", lambda e: e.tensor_copy(C.wrsb[:, 0:64], lgt[:, 64:128]),
               inc="dve")
    for tt in range(16):
        lg = lgt[:, tt * 8:(tt + 1) * 8]
        nm = b.emit("vector", lambda e, s=lg: e.reduce_max(C.colbuf[:, 0:1], s, axis=AX.X, negate=True),
                    waits=[lgt_tok], inc="dve")
        ex = b.emit("scalar",
                    lambda e, s=lg: e.activation(s, s, AF.Exp, bias=C.colbuf[:, 0:1],
                                                 accum_out=C.colbuf[:, 1:2]),
                    waits=[nm], inc="act")
        rc = b.emit("vector", lambda e: e.reciprocal(C.colbuf[:, 2:3], C.colbuf[:, 1:2]),
                    waits=[ex], inc="dve")
        pn = b.emit("vector",
               lambda e, s=lg: e.tensor_scalar(s, s, C.colbuf[:, 2:3], None, op0=ALU.mult),
               waits=[rc], inc="dve")
        m1 = b.emit("vector", lambda e, s=lg: e.reduce_max(C.colbuf[:, 3:4], s, axis=AX.X),
                    waits=[pn], inc="dve")
        s1 = b.emit("vector",
                    lambda e, s=lg: e.tensor_scalar(C.colbuf[:, 16:24], s, C.colbuf[:, 3:4], None,
                                                    op0=ALU.is_lt),
                    waits=[m1], inc="dve")
        s2 = b.emit("vector",
                    lambda e, s=lg: e.tensor_tensor(C.colbuf[:, 24:32], C.colbuf[:, 16:24], s, op=ALU.mult),
                    waits=[s1], inc="dve")
        v2 = b.emit("vector", lambda e: e.reduce_max(C.colbuf[:, 4:5], C.colbuf[:, 24:32], axis=AX.X),
                    waits=[s2], inc="dve")
        s3 = b.emit("vector",
                    lambda e, s=lg: e.tensor_scalar(C.colbuf[:, 16:24], s, C.colbuf[:, 4:5], None,
                                                    op0=ALU.is_ge),
                    waits=[v2], inc="dve")
        b.emit("vector",
               lambda e, s=lg, o=wf[:, tt * 8:(tt + 1) * 8]:
                   e.tensor_tensor(o, C.colbuf[:, 16:24], s, op=ALU.mult),
               waits=[s3], inc="dve")
    stage_free = None
    for tt in range(16):
        bk2 = C.ps.acquire()
        tw_ = b.emit("tensor",
                     lambda e, o=bk2["t"][0:8, 0:128], s=wf[:, tt * 8:(tt + 1) * 8]:
                         e.transpose(o, s, C.id_f32[:, :]),
                     waits=[b.tok("dve")], inc="pe")
        ev = b.emit("vector",
                    lambda e, s=bk2["t"][0:8, 0:128]: e.tensor_copy(C.tmp32[32:40, 0:128], s),
                    waits=[tw_, stage_free], inc="dve")
        bk2["free"] = ev
        stage_free = b.dma("sync", W["wrows"].ap()[0:8, tt * 128:(tt + 1) * 128],
                           C.tmp32[32:40, 0:128], "dRW", waits=[ev])
    C.tmp32_free = stage_free
    t_b = None
    for e_ in range(8):
        t_b = load_cols(b, C, W["b1_f32"][e_, :], 32, C.bcols[:, e_ * 32:(e_ + 1) * 32])
        t_b = load_cols(b, C, W["b2_f32"][e_, :], 8, C.bcols[:, 512 + e_ * 8: 512 + (e_ + 1) * 8])
    b.emit("vector",
           lambda e: e.tensor_scalar(C.bcols[:, 256:512], C.bcols[:, 0:256],
                                     GELU_SQ, None, op0=ALU.mult),
           waits=[t_b], inc="dve")
    t_bias = b.tok("dve")
    wbsb = C.cb[:, 0:2048]
    wb_tok = None
    for e_ in range(8):
        for tc in range(0, T, 512):
            rtok = b.dma("gpsimd", C.rowA[0:1, 0:512], W["wrows"].ap()[e_:e_ + 1, tc:tc + 512],
                         "dRr", waits=[b.tok("dRW"), C.rowA_free])
            bkw = C.ps.acquire()
            tb = bcast_row(b, C, C.rowA[0:1, 0:512], bkw, waits=[rtok])
            C.rowA_free = tb
            wb_tok = b.emit("vector",
                            lambda e, o=wbsb[:, tc:tc + 512], s=bkw["t"][0:128, 0:512]:
                                e.tensor_copy(o, s),
                            waits=[tb, wb_tok], inc="dve")
            bkw["free"] = wb_tok
        wb_ready = wb_tok
        for tc in (0, 1024):
            def gelu_evac(j, t0, tw, bank, mmtok, mw, e_=e_, tc=tc):
                ps = bank["t"][0:128, 0:tw]
                b1c = C.bcols[:, e_ * 32 + j: e_ * 32 + j + 1]
                b1s = C.bcols[:, 256 + e_ * 32 + j: 256 + e_ * 32 + j + 1]
                gs = C.gel[:, 0:512]
                gu = C.gel[:, 512:1024]
                a_ = b.emit("scalar",
                            lambda e: e.activation(gs[:, 0:tw], ps, AF.Square, bias=b1s, scale=GELU_SQ),
                            waits=[mmtok, t_bias, C.gel_free[0]], inc="act")
                v_ = b.emit("gpsimd",
                            lambda e: e.tensor_scalar(gs[:, 0:tw], gs[:, 0:tw], 1.0, 1.0,
                                                      op0=ALU.add, op1=ALU.mult),
                            waits=[a_], inc="pool")
                u_ = b.emit("vector",
                            lambda e: e.scalar_tensor_tensor(gu[:, 0:tw], ps, b1c, gs[:, 0:tw],
                                                             op0=ALU.add, op1=ALU.mult),
                            waits=[mmtok, v_, C.gel_free[1]], inc="dve")
                C.gel_free[0] = u_
                g_ = b.emit("scalar",
                            lambda e: e.activation(gu[:, 0:tw], gu[:, 0:tw], AF.Sigmoid, scale=GELU_S),
                            waits=[u_], inc="act")
                o_ = b.emit("vector",
                            lambda e, d=ghid_ap(C, j, t0 - tc, tw):
                                e.scalar_tensor_tensor(d, ps, b1c, gu[:, 0:tw], op0=ALU.add, op1=ALU.mult),
                            waits=[g_], inc="dve")
                C.gel_free[1] = o_
                C.ghid_tok = o_
                return o_
            gemm(b, C, w=W["w1bf"].ap()[e_ * 1024:(e_ + 1) * 1024, :], Cout=4096,
                 rhs=lambda kt, t0, tw: (fms(C.xn, kt, t0, tw), None),
                 evac=gelu_evac, jblock=512,
                 tchunks=[(tc, 512), (tc + 512, 512)], wwait=W.get("tok_w1"))
            def out_evac(jo, t0, tw, bank, mmtok, mw, e_=e_):
                ps = bank["t"][0:mw, 0:tw]
                b2c = C.bcols[:, 512 + e_ * 8 + jo: 512 + e_ * 8 + jo + 1]
                tv = b.emit("vector",
                            lambda e: e.scalar_tensor_tensor(C.tmp32[0:mw, 0:tw], ps, b2c,
                                                             wbsb[0:mw, t0:t0 + tw],
                                                             op0=ALU.add, op1=ALU.mult),
                            waits=[mmtok, wb_ready, t_bias, C.tmp32_free], inc="dve")
                ax = b.emit("gpsimd",
                            lambda e, d=fms(C.xres, jo, t0, tw):
                                e.tensor_tensor(d, d, C.tmp32[0:mw, 0:tw], op=ALU.add),
                            waits=[tv, C.xres_pool_tok], inc="pool")
                C.tmp32_free = ax
                C.xres_pool_tok = ax
                return tv
            gemm(b, C, w=W["w2bf"].ap()[e_ * 4096:(e_ + 1) * 4096, :], Cout=1024,
                 rhs=lambda kt, t0, tw, tc=tc: (ghid_ap(C, kt, t0 - tc, tw), C.ghid_tok),
                 evac=out_evac, jblock=128,
                 tchunks=[(tc, 512), (tc + 512, 512)], wwait=W.get("tok_w2"))
    return b.tok("pool")

def setup_u6(b, C, nc, inp, newt):
    W = {}
    W["wr_f32"] = inp["L1_ffn_w_router"]
    W["b1_f32"] = inp["L1_ffn_b1"]
    W["b2_f32"] = inp["L1_ffn_b2"]
    W["wrbf"] = newt("u6_wrbf", [1024, 8], BF16)
    W["w1bf"] = newt("u6_w1bf", [8192, 4096], BF16)
    W["w2bf"] = newt("u6_w2bf", [32768, 1024], BF16)
    W["wrows"] = newt("u6_wrows", [8, T], F32)
    W["tok_w1"] = convert_weight(b, C, inp["L1_ffn_w1"].rearrange("e c h -> (e c) h"), W["w1bf"])
    W["tok_w2"] = convert_weight(b, C, inp["L1_ffn_w2"].rearrange("e h c -> (e h) c"), W["w2bf"])
    return W

UNIT_TABS = {}
UNIT_INPUTS = {
    "u6": ["L1_ffn_w_router", "L1_ffn_w1", "L1_ffn_b1", "L1_ffn_w2", "L1_ffn_b2"],
    "u8": ["L1_ffn_w_router", "L1_ffn_w1", "L1_ffn_b1", "L1_ffn_w2", "L1_ffn_b2"],
}
UNIT_SETUP = {"u6": setup_u6, "u8": setup_u6}
UNIT_EMIT = {"u6": emit_u6, "u8": emit_u6}

def build_program(units):
    nc = bass.Bass("TRN2", target_bir_lowering=False)
    stack = ExitStack()
    b = B(nc)
    inp = {}
    needed = sorted({n for u in units for n in UNIT_INPUTS[u]})
    x_out = nc.dram_tensor("x_out", [8 * T * 128], F32, kind="ExternalOutput").ap()
    x_in2 = nc.dram_tensor("x_fm", [128, 8 * T], F32, kind="ExternalInput").ap()
    for name in needed:
        from_spec = INPUT_SHAPES[name]
        inp[name] = nc.dram_tensor(name, list(from_spec), F32, kind="ExternalInput").ap()
    HC = hostc()
    for u in units:
        for tname in UNIT_TABS.get(u, []):
            if tname not in inp:
                inp[tname] = nc.dram_tensor(tname, list(HC[tname].shape),
                                            BF16 if HC[tname].dtype == ml_dtypes.bfloat16 else F32,
                                            kind="ExternalInput").ap()
    for cname, _ in CONST_SPECS:
        inp[cname] = nc.dram_tensor(cname, list(HC[cname].shape),
                                    BF16 if HC[cname].dtype == ml_dtypes.bfloat16 else F32,
                                    kind="ExternalInput").ap()
    with stack:
        C = make_ctx(b, stack)
        ctok = load_consts(b, C, inp)
        newt = lambda name, shape, dt: nc.dram_tensor(name, shape, dt)
        Ws = {}
        seen = {}
        for u in units:
            key = UNIT_SETUP[u]
            if key in seen:
                Ws[u] = seen[key]
            else:
                Ws[u] = UNIT_SETUP[u](b, C, nc, inp, newt)
                seen[key] = Ws[u]
        xtok = b.dma("sync", C.xres[:, :], x_in2, "dX")
        b.wait("vector", ctok)
        tok = xtok
        t0 = b.emit("vector", lambda e: e.tensor_copy(C.colbuf[:, 0:1], C.xres[:, 0:1]),
                    waits=[xtok, ctok], inc="dve")
        tok = t0
        for iu, u in enumerate(units):
            if iu > 0:
                tok = unit_barrier(b, C)
            tok = UNIT_EMIT[u](b, C, Ws[u], in_tok=tok)
        st = b.dma("sync", x_out.rearrange("(p c) -> p c", p=128), C.xres[:, :], "dO",
                   waits=[tok])
        for dname, dap in DEBUG_DUMPS:
            dt_ = nc.dram_tensor("dbg_" + dname, list(dap(C).shape), dap(C).dtype,
                                 kind="ExternalOutput").ap()
            st = b.dma("sync", dt_, dap(C), "dO", waits=[tok])
        b.emit("sync", lambda e: e.nop(), waits=[st])
        b.finalize(stack)
    return nc

DEBUG_DUMPS = []

INPUT_SHAPES = {
    "L1_ffn_w_router": [1024, 8],
    "L1_ffn_w1": [8, 1024, 4096],
    "L1_ffn_b1": [8, 4096],
    "L1_ffn_w2": [8, 4096, 1024],
    "L1_ffn_b2": [8, 1024],
}

ALL_UNITS = ["u1", "u2", "u3", "u4", "u5", "u6", "u7", "u8"]

def _fm_layout(x):
    return np.ascontiguousarray(x.reshape(T, 8, 128).transpose(2, 1, 0).reshape(128, 8 * T))

def _fm_unlayout(y):
    return np.ascontiguousarray(y.reshape(128, 8, T).transpose(2, 1, 0).reshape(T, D))

def run_units(units, x, inputs):
    nc = build_program(units)
    HC = hostc()
    needed = sorted({n for u in units for n in UNIT_INPUTS[u]})
    in_maps = []
    for core in range(NCORES):
        m = {"x_fm": _fm_layout(x[core])}
        for n in needed:
            if n in HOST_LAYOUT:
                srcn, fn = HOST_LAYOUT[n]
                m[n] = fn(np.asarray(inputs[srcn], dtype=np.float32))
            else:
                m[n] = np.ascontiguousarray(inputs[n], dtype=np.float32)
        for cname, _ in CONST_SPECS:
            m[cname] = HC[cname]
        for u in units:
            for tname in UNIT_TABS.get(u, []):
                m[tname] = HC[tname]
        in_maps.append(m)
    res = run_bass_kernel_spmd(nc, in_maps, core_ids=list(range(NCORES)))
    out = np.stack([_fm_unlayout(res.results[i]["x_out"]) for i in range(NCORES)])
    return out

def kernel(**inputs):
    x = np.ascontiguousarray(inputs["L0_attn_a_x"], dtype=np.float32)
    out = run_units(ALL_UNITS, x, inputs)
    return np.ascontiguousarray(out, dtype=np.float32)

def rope_half(b, C, buf, j, cos_sb, sins_sb, in_tok, swap=None):
    sw = C.cb[:, 4096:4608]
    last = None
    for t0 in range(0, T, 512):
        bk = C.ps.acquire()
        tm = b.emit("tensor",
                    lambda e, o=bk["t"][0:128, 0:512], r=fms(buf, j, t0, 512),
                           sm=(swap if swap is not None else C.swap64):
                        e.matmul(o, sm[:, :], r, start=True, stop=True),
                    waits=[in_tok], inc="pe")
        t1 = b.emit("vector",
                    lambda e, o=sw, s=bk["t"][0:128, 0:512],
                           sn=sins_sb[:, t0:t0 + 512]:
                        e.tensor_tensor(o, s, sn, op=ALU.mult),
                    waits=[tm, C.rope_sw_free], inc="dve")
        bk["free"] = t1
        t2 = b.emit("vector",
                    lambda e, d=fms(buf, j, t0, 512), cs=cos_sb[:, t0:t0 + 512]:
                        e.tensor_tensor(d, d, cs, op=ALU.mult),
                    waits=[tm], inc="dve")
        last = b.emit("vector",
                      lambda e, d=fms(buf, j, t0, 512), s=sw:
                          e.tensor_tensor(d, d, s, op=ALU.add),
                      waits=[t1, t2], inc="dve")
        C.rope_sw_free = last
    return last

def vpass_gemm(b, C, *, wcols, vdst_dram, rhs_src, rhs_tok, wwait=None, Cout=1024):
    Kt = wcols.shape[0] // 128
    for jb0 in range(0, Cout, 512):
        jw = min(512, Cout - jb0)
        si = C.wslot_idx
        C.wslot_idx = 1 - C.wslot_idx
        dst = C.wslot[si]
        in_ap = wcols[:, jb0:jb0 + jw].rearrange("(kt p) j -> p kt j", p=128)
        wtok = b.dma("sync", dst[:, 0:Kt * jw].rearrange("p (kt j) -> p kt j", kt=Kt), in_ap,
                     "dW", waits=[x for x in (C.wslot_tok[si], wwait) if x])
        last_pe = None
        for tt in range(16):
            bk = C.ps.acquire()
            b.wait("tensor", wtok)
            if rhs_tok is not None:
                b.wait("tensor", rhs_tok)
            for kt in range(Kt):
                stop = kt == Kt - 1
                tokm = b.emit("tensor",
                              lambda e, o=bk["t"][0:128, 0:jw],
                                     l=fms(rhs_src, kt, tt * 128, 128),
                                     r=dst[:, kt * jw: kt * jw + jw],
                                     s=(kt == 0), p=stop:
                                  e.matmul(o, l, r, start=s, stop=p),
                              inc="pe" if stop else None)
            last_pe = tokm
            ev = b.emit("vector",
                        lambda e, s=bk["t"][0:128, 0:jw], o=C.cb[:, 4096:4096 + jw]:
                            e.tensor_copy(o, s),
                        waits=[tokm, C.vstage_free], inc="dve")
            bk["free"] = ev
            C.vstage_free = b.dma("sync", vdst_dram.ap()[tt * 128:(tt + 1) * 128, jb0:jb0 + jw],
                                  C.cb[:, 4096:4096 + jw], "dS", waits=[ev])
        C.wslot_tok[si] = last_pe

def attn_core(b, C, *, ksp, vsp, scale, qbuf, nheads=16, bias_fn=None, causal=True,
              pre_qt=None, ssb2=None):
    kp = C.cb[:, 0:2048]
    vp = C.cb[:, 2048:4096]
    ssbs = [C.cbf, ssb2 if ssb2 is not None else C.cbf]
    pipelined = ssb2 is not None
    C.S_free2 = [C.S_free, C.S_free]
    for hp in range(nheads // 2):
        ktok = b.dma("sync", kp, ksp.ap()[hp * 128:(hp + 1) * 128, :], "dS",
                     waits=[C.kp_free])
        vtok = None
        for h2_ in range(2):
            vtok = b.dma("sync",
                         vp[:, h2_ * 1024:(h2_ + 1) * 1024].rearrange("p (kt d) -> p kt d", d=64),
                         vsp.ap().rearrange("(kt p) (hh d) -> p hh kt d", p=128, d=64)[:, 2 * hp + h2_, :, :],
                         "dS", waits=[C.vp_free])
        last_pv_ev = None
        state = {}
        def phase1(h, qt):
            h2 = h % 2
            jt, rbase = h // 2, (h % 2) * 64
            par = qt % 2 if pipelined else 0
            Ssb = ssbs[par]
            cb0 = par * 4
            cmx = 8 + 6 * par
            if pre_qt is not None:
                if qt == 0:
                    pre_qt(h, 0)
                if qt + 1 < 16:
                    pre_qt(h, qt + 1)
            nvis = (qt * 128 + 128) if causal else T
            kfull = qt * 128 if causal else T
            chunks = []
            c0 = 0
            while c0 < kfull:
                w = min(512, kfull - c0)
                chunks.append((c0, w))
                c0 += w
            cbanks = []
            for (c0, w) in chunks:
                bk = C.ps.acquire()
                mtk = b.emit("tensor",
                             lambda e, o=bk["t"][0:128, 0:w],
                                    l=fms(qbuf, jt, qt * 128, 128)[rbase:rbase + 64, :],
                                    r=kp[rbase:rbase + 64, c0:c0 + w]:
                                 e.matmul(o, l, r, start=True, stop=True),
                             waits=[ktok, C.q_tok], inc="pe")
                cbanks.append((bk, mtk))
            if causal:
                dbk = C.ps.acquire()
                dtok = b.emit("tensor",
                              lambda e, o=dbk["t"][0:128, 0:128],
                                     l=fms(qbuf, jt, qt * 128, 128)[rbase:rbase + 64, :],
                                     r=kp[rbase:rbase + 64, qt * 128:qt * 128 + 128]:
                                  e.matmul(o, l, r, start=True, stop=True),
                              waits=[ktok, C.q_tok], inc="pe")
            nch = len(chunks)
            evs = []
            for ci, (c0, w) in enumerate(chunks):
                bk, mtk = cbanks[ci]
                if bias_fn is None:
                    ev = b.emit("vector",
                                lambda e, o=Ssb[:, c0:c0 + w], s=bk["t"][0:128, 0:w],
                                       mcol=C.colbuf[:, cmx + ci:cmx + ci + 1]:
                                    e.tensor_scalar(o, s, scale, None, op0=ALU.mult,
                                                    op1=ALU.max, accum_out=mcol),
                                waits=[mtk, C.S_free2[par]], inc="dve")
                else:
                    ev = bias_fn(h, qt, c0, w, bk, ci, Ssb, cmx, par, mtk)
                evs.append(ev)
                bk["free"] = ev
            if causal:
                dev = b.emit("vector",
                             lambda e, o=Ssb[:, qt * 128:qt * 128 + 128],
                                    s=dbk["t"][0:128, 0:128]:
                                 e.scalar_tensor_tensor(o, s, scale, C.amask[:, :],
                                                        op0=ALU.mult, op1=ALU.add),
                             waits=[dtok, C.S_free2[par]], inc="dve")
                dbk["free"] = dev
                dmx = b.emit("vector",
                             lambda e, s=Ssb[:, qt * 128:qt * 128 + 128],
                                    mcol=C.colbuf[:, cmx + nch:cmx + nch + 1]:
                                 e.reduce_max(mcol, s, axis=AX.X),
                             waits=[dev], inc="dve")
                nmx = nch + 1
            else:
                dmx = evs[-1]
                nmx = nch
            cm = b.emit("vector",
                        lambda e, n=nmx: e.reduce_max(C.colbuf[:, cb0 + 1:cb0 + 2],
                                                      C.colbuf[:, cmx:cmx + n], axis=AX.X),
                        waits=[dmx] + evs, inc="dve")
            ng = b.emit("vector",
                        lambda e: e.tensor_scalar(C.colbuf[:, cb0:cb0 + 1],
                                                  C.colbuf[:, cb0 + 1:cb0 + 2], -1.0, None,
                                                  op0=ALU.mult),
                        waits=[cm], inc="dve")
            ex = b.emit("scalar",
                        lambda e, w=nvis: e.activation(Ssb[:, 0:w], Ssb[:, 0:w], AF.Exp,
                                                       bias=C.colbuf[:, cb0:cb0 + 1],
                                                       accum_out=C.colbuf[:, cb0 + 2:cb0 + 3]),
                        waits=[ng], inc="act")
            rc = b.emit("vector",
                        lambda e: e.reciprocal(C.colbuf[:, cb0 + 3:cb0 + 4],
                                               C.colbuf[:, cb0 + 2:cb0 + 3]),
                        waits=[ex], inc="dve")
            sc = b.emit("vector",
                        lambda e, w=nvis: e.tensor_scalar(Ssb[:, 0:w], Ssb[:, 0:w],
                                                          C.colbuf[:, cb0 + 3:cb0 + 4], None,
                                                          op0=ALU.mult),
                        waits=[rc], inc="dve")
            state[qt] = (sc, nvis, par)
        def phase2(h, qt):
            nonlocal last_pv_ev
            h2 = h % 2
            jt, rbase = h // 2, (h % 2) * 64
            sc, nvis, par = state.pop(qt)
            Ssb = ssbs[par]
            obk = C.ps.hold()
            nkt = nvis // 128
            ptok = None
            last_tr = None
            for k0 in range(0, nkt, 4):
                kn = min(4, nkt - k0)
                tbk = C.ps.acquire()
                for ki in range(kn):
                    tr = b.emit("tensor",
                                lambda e, o=tbk["t"][0:128, ki * 128:ki * 128 + 128],
                                       s=Ssb[:, (k0 + ki) * 128:(k0 + ki) * 128 + 128]:
                                    e.transpose(o, s, C.id_f32[:, :]),
                                waits=[sc], inc="pe" if ki == kn - 1 else None)
                last_tr = tr
                si = (k0 // 4) % 2
                gsl = C.gel[:, si * 512: si * 512 + kn * 128]
                pev = b.emit("vector",
                             lambda e, o=gsl, s=tbk["t"][0:128, 0:kn * 128]:
                                 e.tensor_copy(o, s),
                             waits=[tr, C.gel_free[si]], inc="dve")
                tbk["free"] = pev
                for ki in range(kn):
                    kt = k0 + ki
                    ptok = b.emit("tensor",
                                  lambda e, o=obk["t"][0:64, 0:128],
                                         l=vp[:, h2 * 1024 + kt * 64: h2 * 1024 + kt * 64 + 64],
                                         r=C.gel[:, si * 512 + ki * 128: si * 512 + ki * 128 + 128],
                                         s=(kt == 0), p=(kt == nkt - 1):
                                      e.matmul(o, l, r, start=s, stop=p),
                                  waits=[pev, vtok], inc="pe" if kt == nkt - 1 else None)
                C.gel_free[si] = ptok
            last_pv_ev = b.emit("vector",
                                lambda e, o=fms(qbuf, jt, qt * 128, 128)[rbase:rbase + 64, :],
                                       s=obk["t"][0:64, 0:128]:
                                    e.tensor_copy(o, s),
                                waits=[ptok], inc="dve")
            C.ps.release(obk, last_pv_ev)
            C.S_free2[par] = last_tr
        for h2 in range(2):
            h = 2 * hp + h2
            if pipelined:
                for qt in range(17):
                    if qt < 16:
                        phase1(h, qt)
                    if qt > 0:
                        phase2(h, qt - 1)
            else:
                for qt in range(16):
                    phase1(h, qt)
                    phase2(h, qt)
        C.kp_free = b.tok("pe")
        C.vp_free = b.tok("pe")
    C.S_free = C.S_free2[0]
    return b.tok("dve")

def emit_rope_attn(b, C, W, in_tok=None):
    ntok = emit_norm(b, C, src=C.xres, dst=C.xn, nkt=8, eps=1e-5, mean=True,
                     src_tok=in_tok)
    cos_sb = C.cb[:, 0:2048]
    sins_sb = C.cb[:, 2048:4096]
    ttok1 = b.dma("sync", cos_sb, W["tab_cos"], "dC")
    ttok2 = b.dma("sync", sins_sb, W["tab_sins"], "dC")
    def evac_qk(j, t0, tw, bank, mmtok, mw):
        dst = fms(C.ab2, j, t0, tw) if j < 8 else fms(C.ab3, j - 8, t0, tw)
        return b.emit("vector",
                      lambda e, o=dst, s=bank["t"][0:mw, 0:tw]: e.tensor_copy(o, s),
                      waits=[mmtok], inc="dve")
    gemm(b, C, w=W["qkvbf"].ap()[:, 0:2048], Cout=2048,
         rhs=lambda kt, t0, tw: (fms(C.xn, kt, t0, tw), ntok),
         evac=evac_qk, jblock=512, wwait=W.get("tok_qkv"))
    qk_done = b.tok("dve")
    b.wait("vector", ttok1)
    b.wait("vector", ttok2)
    C.rope_sw_free = None
    swp = C.swapx if W.get("swapx") else C.swap64
    for j in range(8):
        rope_half(b, C, C.ab2, j, cos_sb, sins_sb, qk_done, swap=swp)
    for j in range(8):
        rtk = rope_half(b, C, C.ab3, j, cos_sb, sins_sb, qk_done, swap=swp)
        b.dma("sync", W["ksp"].ap()[j * 128:(j + 1) * 128, :], fms(C.ab3, j, 0, T),
              "dS", waits=[rtk])
    k_spilled = b.tok("dS")
    C.vstage_free = None
    vpass_gemm(b, C, wcols=W["qkvbf"].ap()[:, 2048:3072], vdst_dram=W["vsp"],
               rhs_src=C.xn, rhs_tok=ntok, wwait=W.get("tok_qkv"))
    v_spilled = b.tok("dS")
    cb_done = b.tok("dve")
    C.kp_free = cb_done
    C.vp_free = cb_done
    C.S_free = k_spilled
    C.q_tok = b.tok("dve")
    b.wait("sync", k_spilled)
    b.wait("sync", v_spilled)
    at = attn_core(b, C, ksp=W["ksp"], vsp=W["vsp"], scale=0.125, qbuf=C.ab2,
                   ssb2=C.ab3.bitcast(F32)[:, 0:2048])
    def evac_out(j, t0, tw, bank, mmtok, mw):
        tk = b.emit("vector",
                    lambda e, d=fms(C.xres, j, t0, tw), s=bank["t"][0:mw, 0:tw]:
                        e.tensor_tensor(d, d, s, op=ALU.add),
                    waits=[mmtok], inc="dve")
        return tk
    gemm(b, C, w=W["projbf"].ap(), Cout=1024,
         rhs=lambda kt, t0, tw: (fms(C.ab2, kt, t0, tw), at),
         evac=evac_out, jblock=512, wwait=W.get("tok_proj"))
    return b.tok("dve")

def setup_u7(b, C, nc, inp, newt):
    W = {}
    W["qkvbf"] = newt("u7_qkvbf", [1024, 3072], BF16)
    W["projbf"] = newt("u7_projbf", [1024, 1024], BF16)
    W["ksp"] = newt("u7_ksp", [1024, T], BF16)
    W["vsp"] = newt("u7_vsp", [T, 1024], BF16)
    W["tok_qkv"] = convert_weight(b, C, inp["L1_attn_b_W_qkv"], W["qkvbf"])
    W["tok_proj"] = convert_weight(b, C, inp["L1_attn_b_W_proj"], W["projbf"])
    W["tab_cos"] = inp["tab_u7_cos"]
    W["tab_sins"] = inp["tab_u7_sins"]
    return W

def setup_u3(b, C, nc, inp, newt):
    W = {}
    W["qkvbf"] = newt("u3_qkvbf", [1024, 3072], BF16)
    W["projbf"] = newt("u3_projbf", [1024, 1024], BF16)
    W["ksp"] = newt("u3_ksp", [1024, T], BF16)
    W["vsp"] = newt("u3_vsp", [T, 1024], BF16)
    src3 = inp["L0_attn_b_w_qkv"]
    tok = None
    for part in range(3):
        in_ap = bass.AP(src3.tensor, src3.offset + part * 128,
                        [[3072, 1024], [384, 8], [1, 128]])
        tok = b.dma("gpsimd", W["qkvbf"].ap()[:, part * 1024:(part + 1) * 1024]
                    .rearrange("r (mp j) -> r mp j", j=128), in_ap, "dCv")
    W["tok_qkv"] = tok
    W["tok_proj"] = convert_weight(b, C, inp["L0_attn_b_w_out"], W["projbf"])
    W["tab_cos"] = inp["tab_u3_cosfm"]
    W["tab_sins"] = inp["tab_u3_sinsfm"]
    W["swapx"] = True
    return W

UNIT_INPUTS["u7"] = ["L1_attn_b_W_qkv", "L1_attn_b_W_proj"]
UNIT_SETUP["u7"] = setup_u7
UNIT_EMIT["u7"] = emit_rope_attn
UNIT_TABS["u7"] = ["tab_u7_cos", "tab_u7_sins"]
INPUT_SHAPES["L1_attn_b_W_qkv"] = [1024, 3072]
INPUT_SHAPES["L1_attn_b_W_proj"] = [1024, 1024]
UNIT_INPUTS["u3"] = ["L0_attn_b_w_qkv", "L0_attn_b_w_out"]
UNIT_SETUP["u3"] = setup_u3
UNIT_EMIT["u3"] = emit_rope_attn
UNIT_TABS["u3"] = ["tab_u3_cosfm", "tab_u3_sinsfm"]
INPUT_SHAPES["L0_attn_b_w_qkv"] = [1024, 3072]
INPUT_SHAPES["L0_attn_b_w_out"] = [1024, 1024]

def emit_u5(b, C, W, in_tok=None):
    ntok = emit_norm(b, C, src=C.xres, dst=C.xn, nkt=8, eps=1e-5, mean=True,
                     src_tok=in_tok)
    Rtok = b.dma("sync", C.cb[0:32, 0:4095], W["tab_R"], "dC")
    rbl = b.dma("sync", C.tmp32[0:32, 0:16], W["rel_bias_f32"], "dC",
                waits=[C.tmp32_free])
    rbc = b.emit("vector", lambda e: e.tensor_copy(C.gel[0:32, 0:16], C.tmp32[0:32, 0:16]),
                 waits=[rbl], inc="dve")
    C.tmp32_free = rbc
    tstage_free = None
    for c0 in range(0, 4095, 512):
        w = min(512, 4095 - c0)
        bk = C.ps.acquire()
        mt = b.emit("tensor",
                    lambda e, o=bk["t"][0:16, 0:w], r=C.cb[0:32, c0:c0 + w]:
                        e.matmul(o, C.gel[0:32, 0:16], r, start=True, stop=True),
                    waits=[Rtok, rbc], inc="pe")
        ev = b.emit("vector",
                    lambda e, o=C.gel[0:16, 512:512 + w], s=bk["t"][0:16, 0:w]:
                        e.tensor_copy(o, s),
                    waits=[mt, tstage_free], inc="dve")
        bk["free"] = ev
        tstage_free = b.dma("sync", W["tbl"].ap()[0:16, c0:c0 + w], C.gel[0:16, 512:512 + w],
                            "dS", waits=[ev])
    C.gel_free = [b.tok("dS"), b.tok("dS")]
    tbl_done = b.tok("dS")
    for h in range(16):
        b.dma("gpsimd",
              bass.AP(W["tblX"].ap().tensor, h * 128 * 4224, [[4225, 128], [1, 4095]]),
              bass.AP(W["tbl"].ap().tensor, h * 4095, [[0, 128], [1, 4095]]),
              "dG", waits=[tbl_done])
    tblX_done = b.tok("dG")
    def mkevac(dstbuf):
        def evac(j, t0, tw, bank, mmtok, mw):
            return b.emit("vector",
                          lambda e, o=fms(dstbuf, j, t0, tw), s=bank["t"][0:mw, 0:tw]:
                              e.tensor_copy(o, s),
                          waits=[mmtok], inc="dve")
        return evac
    gemm(b, C, w=W["wqbf"].ap(), Cout=1024,
         rhs=lambda kt, t0, tw: (fms(C.xn, kt, t0, tw), ntok),
         evac=mkevac(C.ab2), jblock=512, wwait=W.get("tok_wq"))
    gemm(b, C, w=W["wkbf"].ap(), Cout=1024,
         rhs=lambda kt, t0, tw: (fms(C.xn, kt, t0, tw), ntok),
         evac=mkevac(C.ab3), jblock=512, wwait=W.get("tok_wk"))
    kq_done = b.tok("dve")
    for j in range(8):
        b.dma("sync", W["ksp"].ap()[j * 128:(j + 1) * 128, :], fms(C.ab3, j, 0, T),
              "dS", waits=[kq_done])
    k_spilled = b.tok("dS")
    C.vstage_free = None
    vpass_gemm(b, C, wcols=W["wvbf"].ap(), vdst_dram=W["vsp"],
               rhs_src=C.xn, rhs_tok=ntok, wwait=W.get("tok_wv"))
    v_spilled = b.tok("dS")
    C.kp_free = b.tok("dve")
    C.vp_free = C.kp_free
    C.S_free = k_spilled
    C.q_tok = kq_done
    b.wait("sync", k_spilled)
    b.wait("sync", v_spilled)
    bias_state = {"tok": [None, None], "loaded": k_spilled}
    tblxt = W["tblX"].ap()
    def pre_qt(h, qt):
        si = qt % 2
        off = h * 128 * 4224 + 2047 - 128 * qt
        in_ap = bass.AP(tblxt.tensor, off, [[4224, 128], [1, 2048]])
        bias_state["tok"][si] = b.dma("sync", C.ab3[:, si * 2048:(si + 1) * 2048], in_ap,
                                      "dS", waits=[tblX_done, bias_state["loaded"]])
    def bias_fn(h, qt, c0, w, bk, ci, Ssb, cmx, par, mtk):
        si = qt % 2
        ev = b.emit("vector",
                    lambda e, o=Ssb[:, c0:c0 + w], s=bk["t"][0:128, 0:w],
                           bi=C.ab3[:, si * 2048 + c0: si * 2048 + c0 + w]:
                        e.tensor_tensor(o, s, bi, op=ALU.add),
                    waits=[mtk, bias_state["tok"][si], C.S_free2[par]], inc="dve")
        mx = b.emit("vector",
                    lambda e, s=Ssb[:, c0:c0 + w], mcol=C.colbuf[:, cmx + ci:cmx + ci + 1]:
                        e.reduce_max(mcol, s, axis=AX.X),
                    waits=[ev], inc="dve")
        bias_state["loaded"] = mx
        return mx
    at = attn_core(b, C, ksp=W["ksp"], vsp=W["vsp"], scale=1.0, qbuf=C.ab2,
                   causal=False, bias_fn=bias_fn, pre_qt=pre_qt,
                   ssb2=C.ab3.bitcast(F32)[:, 2048:4096])
    def evac_out(j, t0, tw, bank, mmtok, mw):
        return b.emit("vector",
                      lambda e, d=fms(C.xres, j, t0, tw), s=bank["t"][0:mw, 0:tw]:
                          e.tensor_tensor(d, d, s, op=ALU.add),
                      waits=[mmtok], inc="dve")
    gemm(b, C, w=W["wobf"].ap(), Cout=1024,
         rhs=lambda kt, t0, tw: (fms(C.ab2, kt, t0, tw), at),
         evac=evac_out, jblock=512, wwait=W.get("tok_wo"))
    return b.tok("dve")

def setup_u5(b, C, nc, inp, newt):
    W = {}
    for nm, key in (("wq", "L1_attn_a_wq"), ("wk", "L1_attn_a_wk"),
                    ("wv", "L1_attn_a_wv"), ("wo", "L1_attn_a_wo")):
        W[nm + "bf"] = newt("u5_" + nm, [1024, 1024], BF16)
        W["tok_" + nm] = convert_weight(b, C, inp[key], W[nm + "bf"])
    W["ksp"] = newt("u5_ksp", [1024, T], BF16)
    W["vsp"] = newt("u5_vsp", [T, 1024], BF16)
    W["tbl"] = newt("u5_tbl", [16, 4095], BF16)
    W["tblX"] = newt("u5_tblX", [16 * 128 * 4224], BF16)
    W["rel_bias_f32"] = inp["L1_attn_a_rel_bias"]
    W["tab_R"] = inp["tab_u5_R"]
    return W

UNIT_INPUTS["u5"] = ["L1_attn_a_wq", "L1_attn_a_wk", "L1_attn_a_wv", "L1_attn_a_wo",
                     "L1_attn_a_rel_bias"]
UNIT_SETUP["u5"] = setup_u5
UNIT_EMIT["u5"] = emit_u5
UNIT_TABS["u5"] = ["tab_u5_R"]
INPUT_SHAPES["L1_attn_a_wq"] = [1024, 1024]
INPUT_SHAPES["L1_attn_a_wk"] = [1024, 1024]
INPUT_SHAPES["L1_attn_a_wv"] = [1024, 1024]
INPUT_SHAPES["L1_attn_a_wo"] = [1024, 1024]
INPUT_SHAPES["L1_attn_a_rel_bias"] = [32, 16]

def emit_u2(b, C, W, in_tok=None):
    ntok = emit_norm(b, C, src=C.xres, dst=C.xn, nkt=8, eps=1e-5, mean=True,
                     src_tok=in_tok)
    rtoks = []
    for kt in range(8):
        rtoks.append(b.emit("vector",
                            lambda e, d=fms(C.xres, kt, 0, T), s=fms(C.xn, kt, 0, T):
                                e.tensor_tensor(d, d, s, op=ALU.add),
                            waits=[ntok], inc="dve"))
    for j in range(8):
        b.dma("sync", W["xnsp"].ap()[j * 128:(j + 1) * 128, :], fms(C.xn, j, 0, T),
              "dS", waits=[ntok])
    xn_spilled = b.tok("dS")
    g1 = load_cols(b, C, W["ln1_f32"], 8, C.gcols[:, 0:8])
    g2 = load_cols(b, C, W["ln2_f32"], 8, C.gcols[:, 8:16])
    b.wait("vector", xn_spilled)
    y1tok = emit_norm(b, C, src=C.xn, dst=C.xn, nkt=8, eps=1e-6, mean=False,
                      gamma_col=C.gcols, gamma_off=0, src_is_bf=True, src_tok=g1)
    def evac_q(j, t0, tw, bank, mmtok, mw):
        return b.emit("vector",
                      lambda e, o=fms(C.ab2, j, t0, tw), s=bank["t"][0:mw, 0:tw]:
                          e.tensor_copy(o, s),
                      waits=[mmtok], inc="dve")
    gemm(b, C, w=W["wqbf"].ap(), Cout=1024,
         rhs=lambda kt, t0, tw: (fms(C.xn, kt, t0, tw), y1tok),
         evac=evac_q, jblock=512, wwait=W.get("tok_wq"))
    def evac_kv(j, t0, tw, bank, mmtok, mw):
        return b.emit("vector",
                      lambda e, o=C.cb[:, j * T + t0: j * T + t0 + tw], s=bank["t"][0:mw, 0:tw]:
                          e.tensor_scalar(o, s, 0.25, None, op0=ALU.mult),
                      waits=[mmtok, C.cb_guard], inc="dve")
    gemm(b, C, w=W["wkvbf"].ap(), Cout=256,
         rhs=lambda kt, t0, tw: (fms(C.xn, kt, t0, tw), y1tok),
         evac=evac_kv, jblock=256, wwait=W.get("tok_wkv"))
    kv_tok = b.tok("dve")
    kup_tok = b.dma("sync", C.wslot[0][:, 0:2048].rearrange("p (lt j) -> p lt j", lt=2),
                    W["kupbf"].ap().rearrange("(lt p) j -> p lt j", p=128)[:, :, 0:1024],
                    "dW", waits=[C.wslot_tok[0], W.get("tok_kup")])
    C.wslot_tok[0] = None
    lastk = None
    for h in range(16):
        jt, rbase = h // 2, (h % 2) * 64
        for t0 in range(0, T, 512):
            bk = C.ps.acquire()
            mt = None
            for lt in range(2):
                mt = b.emit("tensor",
                            lambda e, o=bk["t"][0:64, 0:512],
                                   l=C.wslot[0][:, lt * 1024 + h * 64: lt * 1024 + h * 64 + 64],
                                   r=C.cb[:, lt * T + t0: lt * T + t0 + 512],
                                   s=(lt == 0), p=(lt == 1):
                                e.matmul(o, l, r, start=s, stop=p),
                            waits=[kup_tok, kv_tok], inc="pe" if lt else None)
            lastk = b.emit("vector",
                           lambda e, o=fms(C.ab3, jt, t0, 512)[rbase:rbase + 64, :],
                                  s=bk["t"][0:64, 0:512]: e.tensor_copy(o, s),
                           waits=[mt], inc="dve")
            bk["free"] = lastk
    C.wslot_tok[0] = b.tok("pe")
    C.vstage_free = None
    vpass_gemm(b, C, wcols=W["vupbf"].ap(), vdst_dram=W["vsp"],
               rhs_src=C.cb, rhs_tok=kv_tok, wwait=W.get("tok_vup"))
    v_spilled = b.tok("dS")
    cos_sb = C.cb[:, 0:2048]
    sins_sb = C.cb[:, 2048:4096]
    cbg = b.tok("pe")
    ttok1 = b.dma("sync", cos_sb, W["tab_cos"], "dC", waits=[cbg])
    ttok2 = b.dma("sync", sins_sb, W["tab_sins"], "dC")
    b.wait("vector", ttok1)
    b.wait("vector", ttok2)
    qk_done = b.tok("dve")
    C.rope_sw_free = None
    for j in range(8):
        rope_half(b, C, C.ab2, j, cos_sb, sins_sb, qk_done)
    for j in range(8):
        rtk = rope_half(b, C, C.ab3, j, cos_sb, sins_sb, qk_done)
        b.dma("sync", W["ksp"].ap()[j * 128:(j + 1) * 128, :], fms(C.ab3, j, 0, T),
              "dS", waits=[rtk])
    k_spilled = b.tok("dS")
    cb_done = b.tok("dve")
    C.kp_free = cb_done
    C.vp_free = cb_done
    C.S_free = k_spilled
    C.q_tok = b.tok("dve")
    b.wait("sync", k_spilled)
    b.wait("sync", v_spilled)
    at = attn_core(b, C, ksp=W["ksp"], vsp=W["vsp"], scale=0.125, qbuf=C.ab2,
                   ssb2=C.ab3.bitcast(F32)[:, 0:2048])
    xn_state = {"tok": [None, None]}
    def evac_wo(j, t0, tw, bank, mmtok, mw):
        si = (j * 4 + t0 // 512) % 2
        xt = b.dma("sync", C.cb[:, 4096 + si * 256: 4096 + si * 256 + tw // 2]
                   if False else C.cb[:, 4096:4096 + tw], W["xnsp"].ap()[j * 128:(j + 1) * 128, t0:t0 + tw],
                   "dS", waits=[xn_state["tok"][0]])
        t1 = b.emit("vector",
                    lambda e, d=fms(C.xres, j, t0, tw), s=bank["t"][0:mw, 0:tw]:
                        e.tensor_tensor(d, d, s, op=ALU.add),
                    waits=[mmtok], inc="dve")
        t2 = b.emit("vector",
                    lambda e, o=fms(C.ab3, j, t0, tw), s=bank["t"][0:mw, 0:tw],
                           xs=C.cb[:, 4096:4096 + tw]:
                        e.tensor_tensor(o, xs, s, op=ALU.add),
                    waits=[mmtok, xt], inc="dve")
        xn_state["tok"][0] = t2
        return t2
    gemm(b, C, w=W["wobf"].ap(), Cout=1024,
         rhs=lambda kt, t0, tw: (fms(C.ab2, kt, t0, tw), at),
         evac=evac_wo, jblock=512, wwait=W.get("tok_wo"))
    u_tok = b.tok("dve")
    y2tok = emit_norm(b, C, src=C.ab3, dst=C.ab3, nkt=8, eps=1e-6, mean=False,
                      gamma_col=C.gcols, gamma_off=8, src_is_bf=True, src_tok=u_tok)
    wrc = b.dma("gpsimd", W["wrbf"].ap()[:, :], W["wr_f32"], "dCv")
    wr_tok = b.dma("sync", C.wrsb[:, 0:64].rearrange("p (kt j) -> p kt j", kt=8),
                   W["wrbf"].ap().rearrange("(kt p) j -> p kt j", p=128), "dW",
                   waits=[wrc])
    rb_tok = b.dma("sync", C.colbuf[0:1, 32:40], W["rbias_f32"], "dC")
    lgt = C.router[:, 0:128]
    wf = C.router[:, 128:256]
    for tt4 in range(0, 16, 4):
        bk = C.ps.acquire()
        for ti in range(tt4, tt4 + 4):
            mt = None
            for kt in range(8):
                mt = b.emit("tensor",
                            lambda e, o=bk["t"][0:128, (ti % 4) * 8:(ti % 4) * 8 + 8],
                                   l=fms(C.ab3, kt, ti * 128, 128),
                                   r=C.wrsb[:, kt * 8:(kt + 1) * 8],
                                   s=(kt == 0), p=False:
                                e.matmul(o, l, r, start=s, stop=p),
                            waits=[wr_tok, y2tok], inc=None)
            mt = b.emit("tensor",
                        lambda e, o=bk["t"][0:128, (ti % 4) * 8:(ti % 4) * 8 + 8]:
                            e.matmul(o, C.ones_row128[0:1, :], C.colbuf[0:1, 32:40],
                                     start=False, stop=True),
                        waits=[rb_tok], inc="pe")
        ev = b.emit("vector",
                    lambda e, o=lgt[:, tt4 * 8: tt4 * 8 + 32], s=bk["t"][0:128, 0:32]:
                        e.tensor_copy(o, s),
                    waits=[mt], inc="dve")
        bk["free"] = ev
    lgt_tok = b.tok("dve")
    for tt in range(16):
        lg = lgt[:, tt * 8:(tt + 1) * 8]
        cl = b.emit("vector",
                    lambda e, s=lg: e.tensor_scalar(s, s, 50.0, -50.0, op0=ALU.min, op1=ALU.max),
                    waits=[lgt_tok], inc="dve")
        sg = b.emit("scalar", lambda e, s=lg: e.activation(s, s, AF.Sigmoid),
                    waits=[cl], inc="act")
        m1 = b.emit("vector", lambda e, s=lg: e.reduce_max(C.colbuf[:, 0:1], s, axis=AX.X),
                    waits=[sg], inc="dve")
        pc = b.emit("vector",
                    lambda e: e.tensor_scalar(C.colbuf[:, 1:2], C.colbuf[:, 0:1], 1e-8,
                                              1.0 - 1e-8, op0=ALU.max, op1=ALU.min),
                    waits=[m1], inc="dve")
        tw_ = b.emit("vector",
                     lambda e: e.tensor_scalar(C.colbuf[:, 2:3], C.colbuf[:, 1:2],
                                               0.5 + 1e-8, 2.0, op0=ALU.add, op1=ALU.min),
                     waits=[pc], inc="dve")
        tw2 = b.emit("vector",
                     lambda e: e.tensor_scalar(C.colbuf[:, 2:3], C.colbuf[:, 2:3], 0.5, None,
                                               op0=ALU.max),
                     waits=[tw_], inc="dve")
        rr = b.emit("vector", lambda e: e.reciprocal(C.colbuf[:, 3:4], C.colbuf[:, 2:3]),
                    waits=[tw2], inc="dve")
        pr_ = b.emit("vector",
                     lambda e: e.tensor_tensor(C.colbuf[:, 4:5], C.colbuf[:, 1:2],
                                               C.colbuf[:, 3:4], op=ALU.mult),
                     waits=[rr], inc="dve")
        ms = b.emit("vector",
                    lambda e, s=lg: e.tensor_scalar(C.colbuf[:, 16:24], s, C.colbuf[:, 0:1],
                                                    None, op0=ALU.is_ge),
                    waits=[pr_], inc="dve")
        b.emit("vector",
               lambda e, o=wf[:, tt * 8:(tt + 1) * 8]:
                   e.tensor_scalar(o, C.colbuf[:, 16:24], C.colbuf[:, 4:5], None, op0=ALU.mult),
               waits=[ms], inc="dve")
        hr = b.emit("vector",
                    lambda e: e.tensor_scalar(C.colbuf[:, 5:6], C.colbuf[:, 3:4], 0.5, None,
                                              op0=ALU.mult),
                    waits=[rr], inc="dve")
        b.emit("vector",
               lambda e, o=C.gcols[:, 16 + tt:17 + tt]: e.tensor_copy(o, C.colbuf[:, 5:6]),
               waits=[hr], inc="dve")
    wf_tok = b.tok("dve")
    stage_free = None
    for tt in range(16):
        bk2 = C.ps.acquire()
        tw_t = b.emit("tensor",
                      lambda e, o=bk2["t"][0:8, 0:128], s=wf[:, tt * 8:(tt + 1) * 8]:
                          e.transpose(o, s, C.id_f32[:, :]),
                      waits=[wf_tok], inc="pe")
        bk3r = C.ps.acquire()
        tw_s = b.emit("tensor",
                      lambda e, o=bk3r["t"][0:1, 0:128], s=C.gcols[:, 16 + tt:17 + tt]:
                          e.transpose(o, s, C.id_f32[:, :]),
                      waits=[wf_tok], inc="pe")
        ev = b.emit("vector",
                    lambda e, s=bk2["t"][0:8, 0:128]: e.tensor_copy(C.tmp32[32:40, 0:128], s),
                    waits=[tw_s, stage_free], inc="dve")
        ev2 = b.emit("vector",
                     lambda e, s=bk3r["t"][0:1, 0:128]: e.tensor_copy(C.tmp32[64:65, 0:128], s),
                     waits=[tw_s], inc="dve")
        bk2["free"] = ev2
        bk3r["free"] = ev2
        b.dma("sync", W["wrows"].ap()[0:8, tt * 128:(tt + 1) * 128], C.tmp32[32:40, 0:128],
              "dRW", waits=[ev, ev2])
        stage_free = b.dma("sync", W["wrows"].ap()[8:9, tt * 128:(tt + 1) * 128],
                           C.tmp32[64:65, 0:128], "dRW", waits=[ev2])
    wbsb = C.cb[:, 0:2048]
    wb_tok = b.tok("pe")
    wb_dve = None
    for e_ in range(9):
        for tci, tc in enumerate(range(0, T, 512)):
            rtok = b.dma("gpsimd", C.rowA[0:1, 0:512], W["wrows"].ap()[e_:e_ + 1, tc:tc + 512],
                         "dRr", waits=[b.tok("dRW"), C.rowA_free])
            bkw = C.ps.acquire()
            tb = bcast_row(b, C, C.rowA[0:1, 0:512], bkw, waits=[rtok])
            C.rowA_free = tb
            wb_dve = b.emit("vector",
                            lambda e, o=wbsb[:, tc:tc + 512], s=bkw["t"][0:128, 0:512]:
                                e.tensor_copy(o, s),
                            waits=[tb, wb_dve], inc="dve")
            bkw["free"] = wb_dve
        wb_ready = wb_dve
        w1ap = W["sw1bf"].ap() if e_ == 8 else W["ew1bf"].ap()[e_ * 1024:(e_ + 1) * 1024, :]
        w3ap = W["sw3bf"].ap() if e_ == 8 else W["ew3bf"].ap()[e_ * 1024:(e_ + 1) * 1024, :]
        w2ap = W["sw2bf"].ap() if e_ == 8 else W["ew2bf"].ap()[e_ * 2048:(e_ + 1) * 2048, :]
        wt1 = W.get("tok_sw") if e_ == 8 else W.get("tok_ew")
        def out_evac(jo, t0, tw, bank, mmtok, mw, wb_ready=wb_ready):
            tv = b.emit("vector",
                        lambda e: e.scalar_tensor_tensor(C.tmp32[0:mw, 0:tw],
                                                         bank["t"][0:mw, 0:tw], 1.0,
                                                         wbsb[0:mw, t0:t0 + tw],
                                                         op0=ALU.mult, op1=ALU.mult),
                        waits=[mmtok, wb_ready, C.tmp32_free], inc="dve")
            ax = b.emit("gpsimd",
                        lambda e, d=fms(C.xres, jo, t0, tw):
                            e.tensor_tensor(d, d, C.tmp32[0:mw, 0:tw], op=ALU.add),
                        waits=[tv, C.xres_pool_tok], inc="pool")
            C.tmp32_free = ax
            C.xres_pool_tok = ax
            return tv
        for tc2 in (0, 1024):
            for jb0 in range(0, 2048, 512):
                dstA, dstB = C.wslot[0], C.wslot[1]
                wtokA = b.dma("sync", dstA[:, 0:4096].rearrange("p (kt j) -> p kt j", kt=8),
                              w1ap[:, jb0:jb0 + 512].rearrange("(kt p) j -> p kt j", p=128),
                              "dW", waits=[x for x in (C.wslot_tok[0], wt1) if x])
                wtokB = b.dma("sync", dstB[:, 0:4096].rearrange("p (kt j) -> p kt j", kt=8),
                              w3ap[:, jb0:jb0 + 512].rearrange("(kt p) j -> p kt j", p=128),
                              "dW", waits=[x for x in (C.wslot_tok[1], wt1) if x])
                lastpe = None
                for jj in range(0, 512, 128):
                    j = (jb0 + jj) // 128
                    for t0 in (tc2, tc2 + 512):
                        bk1 = C.ps.acquire()
                        b.wait("tensor", wtokA)
                        b.wait("tensor", wtokB)
                        for kt in range(8):
                            b.emit("tensor",
                                   lambda e, o=bk1["t"][0:128, 0:512],
                                          l=dstA[:, kt * 512 + jj: kt * 512 + jj + 128],
                                          r=fms(C.ab3, kt, t0, 512), s=(kt == 0), p=(kt == 7):
                                       e.matmul(o, l, r, start=s, stop=p),
                                   inc="pe" if kt == 7 else None)
                        bk3 = C.ps.acquire()
                        mt3 = None
                        for kt in range(8):
                            mt3 = b.emit("tensor",
                                         lambda e, o=bk3["t"][0:128, 0:512],
                                                l=dstB[:, kt * 512 + jj: kt * 512 + jj + 128],
                                                r=fms(C.ab3, kt, t0, 512), s=(kt == 0), p=(kt == 7):
                                             e.matmul(o, l, r, start=s, stop=p),
                                         inc="pe" if kt == 7 else None)
                        lastpe = mt3
                        sl = b.emit("scalar",
                                    lambda e, o=C.gel[:, 0:512], s=bk1["t"][0:128, 0:512]:
                                        e.activation(o, s, AF.Silu),
                                    waits=[mt3, C.gel_free[0]], inc="act")
                        hv = b.emit("vector",
                                    lambda e, o=C.ab2[:, j * 1024 + (t0 - tc2): j * 1024 + (t0 - tc2) + 512],
                                           s=C.gel[:, 0:512], p3=bk3["t"][0:128, 0:512]:
                                        e.tensor_tensor(o, s, p3, op=ALU.mult),
                                    waits=[sl], inc="dve")
                        C.gel_free[0] = hv
                        bk1["free"] = hv
                        bk3["free"] = hv
                        C.ghid_tok = hv
                C.wslot_tok[0] = lastpe
                C.wslot_tok[1] = lastpe
            gemm(b, C, w=w2ap, Cout=1024,
                 rhs=lambda kt, t0, tw, tc2=tc2: (C.ab2[:, kt * 1024 + (t0 - tc2): kt * 1024 + (t0 - tc2) + tw], C.ghid_tok),
                 evac=out_evac, jblock=128,
                 tchunks=[(tc2, 512), (tc2 + 512, 512)],
                 wwait=W.get("tok_sw2") if e_ == 8 else W.get("tok_ew2"))
    return b.tok("pool")

def setup_u2(b, C, nc, inp, newt):
    W = {}
    W["wqbf"] = newt("u2_wq", [1024, 1024], BF16)
    W["tok_wq"] = convert_weight(b, C, inp["L0_ffn_Wq"], W["wqbf"])
    W["wkvbf"] = newt("u2_wkv", [1024, 256], BF16)
    W["tok_wkv"] = convert_weight(b, C, inp["L0_ffn_Wkv"], W["wkvbf"])
    W["wobf"] = newt("u2_wo", [1024, 1024], BF16)
    W["tok_wo"] = convert_weight(b, C, inp["L0_ffn_Wo"], W["wobf"])
    for nm, key in (("kup", "L0_ffn_Wk_up"), ("vup", "L0_ffn_Wv_up")):
        W[nm + "bf"] = newt("u2_" + nm, [256, 1024], BF16)
        src = inp[key]
        tok = b.dma("gpsimd",
                    W[nm + "bf"].ap().rearrange("l (h d) -> l h d", d=64),
                    bass.AP(src.tensor, src.offset, [[64, 256], [256 * 64, 16], [1, 64]]),
                    "dCv")
        W["tok_" + nm] = tok
    W["wrbf"] = newt("u2_wr", [1024, 8], BF16)
    wrc0 = b.dma("gpsimd", W["wrbf"].ap()[:, :], inp["L0_ffn_Wr"], "dCv")
    W["wr_f32"] = inp["L0_ffn_Wr"]
    W["rbias_f32"] = inp["L0_ffn_r_bias"][None, :]
    W["ln1_f32"] = inp["L0_ffn_ln1_w"]
    W["ln2_f32"] = inp["L0_ffn_ln2_w"]
    W["sw1bf"] = newt("u2_sw1", [1024, 2048], BF16)
    W["sw3bf"] = newt("u2_sw3", [1024, 2048], BF16)
    W["sw2bf"] = newt("u2_sw2", [2048, 1024], BF16)
    t1 = convert_weight(b, C, inp["L0_ffn_sw1"], W["sw1bf"])
    t2 = convert_weight(b, C, inp["L0_ffn_sw3"], W["sw3bf"])
    W["tok_sw"] = t2
    W["tok_sw2"] = convert_weight(b, C, inp["L0_ffn_sw2"], W["sw2bf"])
    W["ew1bf"] = newt("u2_ew1", [8192, 2048], BF16)
    W["ew3bf"] = newt("u2_ew3", [8192, 2048], BF16)
    W["ew2bf"] = newt("u2_ew2", [16384, 1024], BF16)
    t3 = convert_weight(b, C, inp["L0_ffn_ew1"].rearrange("e c h -> (e c) h"), W["ew1bf"])
    t4 = convert_weight(b, C, inp["L0_ffn_ew3"].rearrange("e c h -> (e c) h"), W["ew3bf"])
    W["tok_ew"] = t4
    W["tok_ew2"] = convert_weight(b, C, inp["L0_ffn_ew2"].rearrange("e h c -> (e h) c"), W["ew2bf"])
    W["ksp"] = newt("u2_ksp", [1024, T], BF16)
    W["vsp"] = newt("u2_vsp", [T, 1024], BF16)
    W["xnsp"] = newt("u2_xnsp", [1024, T], BF16)
    W["wrows"] = newt("u2_wrows", [9, T], F32)
    W["tab_cos"] = inp["tab_u2_cos"]
    W["tab_sins"] = inp["tab_u2_sins"]
    return W

UNIT_INPUTS["u2"] = ["L0_ffn_ln1_w", "L0_ffn_ln2_w", "L0_ffn_Wq", "L0_ffn_Wkv",
                     "L0_ffn_Wk_up", "L0_ffn_Wv_up", "L0_ffn_Wo", "L0_ffn_Wr",
                     "L0_ffn_r_bias", "L0_ffn_sw1", "L0_ffn_sw2", "L0_ffn_sw3",
                     "L0_ffn_ew1", "L0_ffn_ew2", "L0_ffn_ew3"]
UNIT_INPUTS["u4"] = UNIT_INPUTS["u2"]
UNIT_SETUP["u2"] = setup_u2
UNIT_SETUP["u4"] = setup_u2
UNIT_EMIT["u2"] = emit_u2
UNIT_EMIT["u4"] = emit_u2
UNIT_TABS["u2"] = ["tab_u2_cos", "tab_u2_sins"]
UNIT_TABS["u4"] = UNIT_TABS["u2"]
INPUT_SHAPES.update({
    "L0_ffn_ln1_w": [1024], "L0_ffn_ln2_w": [1024], "L0_ffn_Wq": [1024, 1024],
    "L0_ffn_Wkv": [1024, 256], "L0_ffn_Wk_up": [16, 256, 64], "L0_ffn_Wv_up": [16, 256, 64],
    "L0_ffn_Wo": [1024, 1024], "L0_ffn_Wr": [1024, 8], "L0_ffn_r_bias": [8],
    "L0_ffn_sw1": [1024, 2048], "L0_ffn_sw2": [2048, 1024], "L0_ffn_sw3": [1024, 2048],
    "L0_ffn_ew1": [8, 1024, 2048], "L0_ffn_ew2": [8, 2048, 1024], "L0_ffn_ew3": [8, 1024, 2048],
})

def xbc_ap(C, t, c0, cw):
    if t < 15:
        return C.ab2[:, 8192 + t * 515 + c0: 8192 + t * 515 + c0 + cw]
    return C.cb[:, (t - 15) * 515 + c0: (t - 15) * 515 + c0 + cw]

def emit_u1(b, C, W, in_tok=None):
    ntok = emit_norm(b, C, src=C.xres, dst=C.xn, nkt=8, eps=1e-5, mean=True,
                     src_tok=in_tok)
    for kt in range(8):
        b.emit("vector",
               lambda e, d=fms(C.xres, kt, 0, T), s=fms(C.xn, kt, 0, T):
                   e.tensor_tensor(d, d, s, op=ALU.add),
               waits=[ntok], inc="dve")
    al = b.dma("sync", C.tmp32[0:1, 0:32], W["A_log_f32"][None, :], "dC",
               waits=[C.tmp32_free])
    a1 = b.emit("scalar", lambda e: e.activation(C.tmp32[0:1, 0:32], C.tmp32[0:1, 0:32], AF.Exp),
                waits=[al], inc="act")
    a2 = b.emit("vector",
                lambda e: e.tensor_scalar(C.tmp32[0:1, 32:64], C.tmp32[0:1, 0:32], -1.0, None,
                                          op0=ALU.mult),
                waits=[a1], inc="dve")
    bkA = C.ps.acquire()
    a3 = bcast_row(b, C, C.tmp32[0:1, 32:64], bkA, waits=[a2])
    a4 = b.emit("vector", lambda e, s=bkA["t"][0:128, 0:32]: e.tensor_copy(C.bcols[:, 0:32], s),
                waits=[a3], inc="dve")
    bkA["free"] = a4
    db = b.dma("sync", C.bcols[0:32, 32:33], W["dt_bias_f32"][:, None], "dC")
    dl = b.dma("sync", C.tmp32[0:1, 0:32], W["D_f32"][None, :], "dC", waits=[a2])
    bkD = C.ps.acquire()
    d3 = b.emit("tensor",
                lambda e: e.matmul(bkD["t"][0:128, 0:32], C.ones_row128[0:1, :],
                                   C.tmp32[0:1, 0:32], start=True, stop=True),
                waits=[dl], inc="pe")
    d4 = b.emit("vector", lambda e, s=bkD["t"][0:128, 0:32]: e.tensor_copy(C.bcols[:, 64:96], s),
                waits=[d3], inc="dve")
    bkD["free"] = d4
    C.tmp32_free = d3
    ctoks = []
    for r in range(4):
        ctoks.append(load_cols(b, C, W["convwT_f32"][r, :], 17, C.bcols[:, 96 + r * 17: 113 + r * 17]))
    ctoks.append(load_cols(b, C, W["conv_b_f32"], 17, C.bcols[:, 164:181]))
    g_ssm = load_cols(b, C, W["ssm_norm_f32"], 16, C.gcols[:, 0:16])
    g_rms = load_cols(b, C, W["rms_w_f32"], 8, C.gcols[:, 16:24])
    cst_tok = b.tok("dve")
    G = C.cb[0:64, 2048:4096]
    gz = b.emit("gpsimd", lambda e: e.memset(G, 0.0), inc="pool")
    hz = b.emit("gpsimd", lambda e: e.memset(C.cb[:, 1030:1081], 0.0), inc="pool")
    z_guard = None
    ssd_guard = None
    for mc in range(4):
        def evac_ip(j, t0, tw, bank, mmtok, mw, mc=mc):
            if j < 16:
                return b.emit("vector",
                              lambda e, o=C.ab2[:, j * 512: j * 512 + 512], s=bank["t"][0:mw, 0:tw]:
                                  e.tensor_copy(o, s),
                              waits=[mmtok, z_guard], inc="dve")
            if j < 33:
                return b.emit("vector",
                              lambda e, o=xbc_ap(C, j - 16, 3, 512), s=bank["t"][0:mw, 0:tw]:
                                  e.tensor_copy(o, s),
                              waits=[mmtok, z_guard], inc="dve")
            return b.emit("vector",
                          lambda e, s=bank["t"][0:mw, 0:tw]:
                              e.tensor_copy(C.tmp32[0:32, 0:512], s),
                          waits=[mmtok, C.tmp32_free], inc="dve")
        gemm(b, C, w=W["inprojbf"].ap(), Cout=4256,
             rhs=lambda kt, t0, tw, mc=mc: (fms(C.xn, kt, mc * 512, 512), ntok),
             evac=evac_ip, jblock=512, tchunks=[(0, 512)], wwait=W.get("tok_inproj"))
        ip_done = b.tok("dve")
        conv_done = []
        for ct in range(17):
            hcp = b.emit("vector",
                         lambda e, o=xbc_ap(C, ct, 0, 3), s=C.cb[:, 1030 + ct * 3: 1033 + ct * 3]:
                             e.tensor_copy(o, s),
                         waits=[ip_done, b.tok("pool") if mc == 0 else None], inc="dve")
            t0_ = b.emit("vector",
                         lambda e, o=C.cbf[:, 0:512], s=xbc_ap(C, ct, 0, 512),
                                w0=C.bcols[:, 96 + ct:97 + ct]:
                             e.tensor_scalar(o, s, w0, None, op0=ALU.mult),
                         waits=[hcp, cst_tok, C.convtmp_free], inc="dve")
            for r in range(1, 4):
                t0_ = b.emit("vector",
                             lambda e, o=C.cbf[:, 0:512], s=xbc_ap(C, ct, r, 512),
                                    wr=C.bcols[:, 96 + r * 17 + ct: 97 + r * 17 + ct]:
                                 e.scalar_tensor_tensor(o, s, wr, C.cbf[:, 0:512],
                                                        op0=ALU.mult, op1=ALU.add),
                             waits=[t0_], inc="dve")
            hsv = b.emit("vector",
                         lambda e, o=C.cb[:, 1030 + ct * 3: 1033 + ct * 3], s=xbc_ap(C, ct, 512, 3):
                             e.tensor_copy(o, s),
                         waits=[t0_], inc="dve")
            sv = b.emit("scalar",
                        lambda e, o=xbc_ap(C, ct, 3, 512), s=C.cbf[:, 0:512],
                               cb_=C.bcols[:, 164 + ct:165 + ct]:
                            e.activation(o, s, AF.Silu, bias=cb_),
                        waits=[t0_, hsv], inc="act")
            C.convtmp_free = sv
            conv_done.append(sv)
        xbc_tok = b.tok("act")
        dts = C.cbf[0:32, 512:1024]
        s1 = b.emit("vector",
                    lambda e: e.tensor_scalar(dts, C.tmp32[0:32, 0:512], C.bcols[0:32, 32:33],
                                              None, op0=ALU.add),
                    waits=[ip_done, db, C.dts_free], inc="dve")
        s2 = b.emit("vector",
                    lambda e: e.scalar_tensor_tensor(C.cbf[0:32, 1024:1536], dts, -1.0, dts,
                                                     op0=ALU.mult, op1=ALU.max),
                    waits=[s1], inc="dve")
        s3 = b.emit("scalar",
                    lambda e: e.activation(C.cbf[0:32, 1024:1536], C.cbf[0:32, 1024:1536],
                                           AF.Exp, scale=-1.0),
                    waits=[s2], inc="act")
        ep1, etok1 = eps_ap(b, C, 1.0)
        s4 = b.emit("scalar",
                    lambda e: e.activation(C.cbf[0:32, 1024:1536], C.cbf[0:32, 1024:1536],
                                           AF.Ln, bias=ep1[0:32, :]),
                    waits=[s3, etok1], inc="act")
        s5 = b.emit("vector",
                    lambda e: e.tensor_scalar(dts, dts, 0.0, None, op0=ALU.max),
                    waits=[s4], inc="dve")
        dt_tok = b.emit("vector",
                        lambda e: e.tensor_tensor(dts, dts, C.cbf[0:32, 1024:1536], op=ALU.add),
                        waits=[s5], inc="dve")
        C.tmp32_free = dt_tok
        C.dts_free = None
        yg_toks = []
        for c in range(4):
            cc = c * 128
            bk = C.ps.acquire()
            tt_ = b.emit("tensor",
                         lambda e, o=bk["t"][0:128, 0:32], s=dts[:, cc:cc + 128]:
                             e.transpose(o, s, C.id_f32[0:32, 0:32]),
                         waits=[dt_tok], inc="pe")
            dtk = b.emit("vector",
                         lambda e, s=bk["t"][0:128, 0:32]: e.tensor_copy(C.cbf[:, 1536:1568], s),
                         waits=[tt_, ssd_guard], inc="dve")
            bk["free"] = dtk
            b.wait("tensor", C.ps.tr_free)
            tb_ = b.emit("tensor",
                         lambda e, o=C.ps.tr[0:128, 0:64], s=xbc_ap(C, 16, 3 + cc, 128)[0:64, :]:
                             e.transpose(o, s, C.id_bf[0:64, 0:64]),
                         waits=[xbc_tok], inc="pe")
            btk = b.emit("vector",
                         lambda e: e.tensor_copy(C.gel[:, 384:448], C.ps.tr[0:128, 0:64]),
                         waits=[tb_, ssd_guard], inc="dve")
            C.ps.tr_free = btk
            b.wait("tensor", C.ps.tr_free)
            tc1_ = b.emit("tensor",
                          lambda e, o=C.ps.tr[0:128, 128:192], s=xbc_ap(C, 16, 3 + cc, 128)[64:128, :]:
                              e.transpose(o, s, C.id_bf[64:128, 64:128]),
                          waits=[xbc_tok], inc="pe")
            ce1 = b.emit("vector",
                         lambda e: e.tensor_copy(C.cb[:, 1081:1145], C.ps.tr[0:128, 128:192]),
                         waits=[tc1_, C.csb_free], inc="dve")
            tc2_ = b.emit("tensor",
                          lambda e, o=C.ps.tr[0:64, 192:320], s=C.cb[:, 1081:1145]:
                              e.transpose(o, s, C.id_bf[:, :]),
                          waits=[ce1], inc="pe")
            ce2 = b.emit("vector",
                         lambda e: e.tensor_copy(C.cb[0:64, 1145:1273], C.ps.tr[0:64, 192:320]),
                         waits=[tc2_], inc="dve")
            C.ps.tr_free = ce2
            Csb = C.cb[0:64, 1145:1273]
            bkS = C.ps.acquire()
            ts_ = b.emit("tensor",
                         lambda e, o=bkS["t"][0:128, 0:128], l=xbc_ap(C, 16, 3 + cc, 128)[0:64, :],
                                r=Csb:
                             e.matmul(o, l, r, start=True, stop=True),
                         waits=[xbc_tok, ce2], inc="pe")
            smk = b.emit("vector",
                         lambda e, s=bkS["t"][0:128, 0:128]:
                             e.tensor_tensor(C.cbf[:, 1280:1408], s, C.triu[:, :], op=ALU.mult),
                         waits=[ts_, ssd_guard], inc="dve")
            bkS["free"] = smk
            C.csb_free = ts_
            for h in range(32):
                jt, rbase = h // 4, (h % 4) * 32
                jt = h // 2
                rbase = (h % 2) * 64
                dtcol = C.cbf[:, 1536 + h: 1537 + h]
                r1 = b.emit("vector",
                            lambda e, dc=dtcol: e.tensor_scalar(C.cbf[:, 1568:1696], C.triu[:, :],
                                                                dc, None, op0=ALU.mult),
                            waits=[dtk, C.rhsh_free], inc="dve")
                bkDm = C.ps.acquire()
                md = b.emit("tensor",
                            lambda e, o=bkDm["t"][0:128, 0:128]:
                                e.matmul(o, C.slow[:, :], C.cbf[:, 1568:1696], start=True, stop=True),
                            waits=[r1], inc=None)
                b.wait("tensor", C.ps.stats_free)
                mc_ = b.emit("tensor",
                             lambda e: e.matmul(C.ps.stats[0:1, 0:128], C.ones_col_f32[:, :],
                                                C.cbf[:, 1568:1696], start=True, stop=True),
                             inc="pe")
                eE = b.emit("scalar",
                            lambda e, o=C.cbf[:, 1024:1152], s=bkDm["t"][0:128, 0:128],
                                   sc=C.bcols[:, h:h + 1]:
                                e.activation(o, s, AF.Exp, scale=sc),
                            waits=[md if False else mc_, C.E_free], inc="act")
                bkDm["free"] = eE
                eL = b.emit("scalar",
                            lambda e, o=C.cbf[0:1, 1696:1824], s=C.ps.stats[0:1, 0:128],
                                   sc=C.bcols[0:1, h:h + 1]:
                                e.activation(o, s, AF.Exp, scale=sc),
                            waits=[mc_, C.eL_free], inc="act")
                C.ps.stats_free = eL
                m1 = b.emit("vector",
                            lambda e: e.tensor_tensor(C.cbf[:, 1152:1280], C.cbf[:, 1024:1152],
                                                      C.cbf[:, 1280:1408], op=ALU.mult),
                            waits=[eE, smk], inc="dve")
                m2 = b.emit("vector",
                            lambda e, dc=dtcol: e.tensor_scalar(C.gel[:, 0:128], C.cbf[:, 1152:1280],
                                                                dc, None, op0=ALU.mult),
                            waits=[m1, C.mstd_free], inc="dve")
                C.E_free = m1
                b.wait("tensor", C.ps.tr_free)
                tx_ = b.emit("tensor",
                             lambda e, o=C.ps.tr[0:128, 64:128],
                                    s=xbc_ap(C, jt, 3 + cc, 128)[rbase:rbase + 64, :],
                                    idn=C.id_bf[rbase:rbase + 64, rbase:rbase + 64]:
                                 e.transpose(o, s, idn),
                             waits=[xbc_tok], inc="pe")
                xtk = b.emit("vector",
                             lambda e: e.tensor_copy(C.gel[:, 128:192], C.ps.tr[0:128, 64:128]),
                             waits=[tx_, C.xstok_free], inc="dve")
                C.ps.tr_free = xtk
                bkEb = C.ps.acquire()
                eb = b.emit("tensor",
                            lambda e, o=bkEb["t"][0:64, 0:128]:
                                e.matmul(o, C.ones_row128[0:1, 0:64], C.cbf[0:1, 1696:1824],
                                         start=True, stop=True),
                            waits=[eL], inc="pe")
                C.eL_free = eb
                cd = b.emit("vector",
                            lambda e, o=C.gel[0:64, 192:320], s=C.cb[0:64, 1145:1273],
                                   bb=bkEb["t"][0:64, 0:128]:
                                e.tensor_tensor(o, s, bb, op=ALU.mult),
                            waits=[eb, ce2, C.cd_free], inc="dve")
                bkEb["free"] = cd
                bkY = C.ps.hold()
                y1_ = b.emit("tensor",
                             lambda e, o=bkY["t"][rbase:rbase + 64, 0:128]:
                                 e.matmul(o, C.gel[:, 128:192], C.gel[:, 0:128], start=True, stop=False),
                             waits=[m2, xtk], inc=None)
                y2_ = b.emit("tensor",
                             lambda e, o=bkY["t"][rbase:rbase + 64, 0:128], l=G[0:64, h * 64:(h + 1) * 64]:
                                 e.matmul(o, l, C.gel[0:64, 192:320], start=False, stop=True),
                             waits=[cd, C.G_wtok], inc="pe")
                C.mstd_free = y2_
                C.cd_free = y2_
                w1_ = b.emit("vector",
                             lambda e, dc=dtcol: e.scalar_tensor_tensor(
                                 C.colbuf[:, 40:41], C.cbf[:, 1151:1152], 1.0, C.cbf[:, 1536 + h:1537 + h],
                                 op0=ALU.mult, op1=ALU.mult) if False else
                                 e.tensor_tensor(C.colbuf[:, 40:41], C.cbf[:, 1151:1152], dc, op=ALU.mult),
                             waits=[eE, dtk], inc="dve")
                bw = b.emit("vector",
                            lambda e: e.tensor_scalar(C.gel[:, 320:384], C.gel[:, 384:448],
                                                      C.colbuf[:, 40:41], None, op0=ALU.mult),
                            waits=[w1_, btk, C.bw_free], inc="dve")
                bkU = C.ps.acquire()
                up = b.emit("tensor",
                            lambda e, o=bkU["t"][0:64, 0:64]:
                                e.matmul(o, C.gel[:, 320:384], C.gel[:, 128:192], start=True, stop=True),
                            waits=[bw, xtk], inc="pe")
                C.bw_free = up
                C.xstok_free = up
                bkSc = C.ps.acquire()
                sb_ = b.emit("tensor",
                             lambda e, o=bkSc["t"][0:64, 0:1]:
                                 e.matmul(o, C.ones_row128[0:1, 0:64], C.cbf[0:1, 1823:1824],
                                          start=True, stop=True),
                             waits=[eL], inc="pe")
                sc_ = b.emit("vector",
                             lambda e, s=bkSc["t"][0:64, 0:1]: e.tensor_copy(C.colbuf[0:64, 41:42], s),
                             waits=[sb_], inc="dve")
                bkSc["free"] = sc_
                gu = b.emit("vector",
                            lambda e, g=G[0:64, h * 64:(h + 1) * 64], u=bkU["t"][0:64, 0:64]:
                                e.scalar_tensor_tensor(g, g, C.colbuf[0:64, 41:42], u,
                                                       op0=ALU.mult, op1=ALU.add),
                            waits=[sc_, up, y2_], inc="dve")
                bkU["free"] = gu
                C.G_wtok = gu
                zs = b.emit("scalar",
                            lambda e, o=C.gel[rbase:rbase + 64, 448:576],
                                   s=C.ab2[:, jt * 512 + cc: jt * 512 + cc + 128][rbase:rbase + 64, :]:
                                e.activation(o, s, AF.Silu),
                            waits=[ip_done, C.zs_free], inc="act")
                t64 = b.emit("vector",
                             lambda e, o=C.tmp32[rbase:rbase + 64, 0:128],
                                    s=xbc_ap(C, jt, 3 + cc, 128)[rbase:rbase + 64, :],
                                    dd=C.bcols[rbase:rbase + 64, 64 + h:65 + h],
                                    y=bkY["t"][rbase:rbase + 64, 0:128]:
                                 e.scalar_tensor_tensor(o, s, dd, y, op0=ALU.mult, op1=ALU.add),
                             waits=[y2_, d4, C.t64_free], inc="dve")
                ygt = b.emit("vector",
                             lambda e, o=C.ab2[:, jt * 512 + cc: jt * 512 + cc + 128][rbase:rbase + 64, :],
                                    s=C.tmp32[rbase:rbase + 64, 0:128], z=C.gel[rbase:rbase + 64, 448:576]:
                                 e.tensor_tensor(o, s, z, op=ALU.mult),
                             waits=[t64, zs], inc="dve")
                C.zs_free = ygt
                C.t64_free = ygt
                C.rhsh_free = eE
                C.ps.release(bkY, t64)
                yg_toks.append(ygt)
        C.tmp32_free = b.tok("dve")
        ygn = emit_norm(b, C, src=C.ab2, dst=C.ab2, nkt=16, eps=1e-5, mean=False,
                        gamma_col=C.gcols, gamma_off=0, Tb=512, src_is_bf=True,
                        src_tok=yg_toks[-1])
        def evac_op(j, t0, tw, bank, mmtok, mw, mc=mc):
            b.emit("vector",
                   lambda e, d=fms(C.xres, j, mc * 512, 512), s=bank["t"][0:mw, 0:tw]:
                       e.tensor_tensor(d, d, s, op=ALU.add),
                   waits=[mmtok], inc="dve")
            return b.emit("vector",
                          lambda e, o=fms(C.ab3, j, mc * 512, 512), s=fms(C.xn, j, mc * 512, 512),
                                 p=bank["t"][0:mw, 0:tw]:
                              e.tensor_tensor(o, s, p, op=ALU.add),
                          waits=[mmtok], inc="dve")
        gemm(b, C, w=W["outprojbf"].ap(), Cout=1024,
             rhs=lambda kt, t0, tw: (C.ab2[:, kt * 512: kt * 512 + 512], ygn),
             evac=evac_op, jblock=256, tchunks=[(0, 512)], wwait=W.get("tok_outproj"))
        z_guard = b.tok("pe")
        ssd_guard = b.tok("pe")
    ssd_guard = ssd_guard
    x1_tok = b.tok("dve")
    xn2 = emit_norm(b, C, src=C.ab3, dst=C.xn, nkt=8, eps=1e-5, mean=False,
                    gamma_col=C.gcols, gamma_off=16, src_is_bf=True, src_tok=x1_tok)
    for tc2 in (0, 512, 1024, 1536):
        for jb0 in range(0, 4096, 512):
            dstA, dstB = C.wslot[0], C.wslot[1]
            wtokA = b.dma("sync", dstA[:, 0:4096].rearrange("p (kt j) -> p kt j", kt=8),
                          W["fc1bf"].ap()[:, jb0:jb0 + 512].rearrange("(kt p) j -> p kt j", p=128),
                          "dW", waits=[x for x in (C.wslot_tok[0], W.get("tok_fc1")) if x])
            wtokB = b.dma("sync", dstB[:, 0:4096].rearrange("p (kt j) -> p kt j", kt=8),
                          W["fc1bf"].ap()[:, 4096 + jb0: 4096 + jb0 + 512].rearrange("(kt p) j -> p kt j", p=128),
                          "dW", waits=[x for x in (C.wslot_tok[1], W.get("tok_fc1")) if x])
            lastpe = None
            for jj in range(0, 512, 128):
                j = (jb0 + jj) // 128
                for t0 in (tc2,):
                    bk1 = C.ps.acquire()
                    b.wait("tensor", wtokA)
                    b.wait("tensor", wtokB)
                    for kt in range(8):
                        b.emit("tensor",
                               lambda e, o=bk1["t"][0:128, 0:512],
                                      l=dstA[:, kt * 512 + jj: kt * 512 + jj + 128],
                                      r=fms(C.xn, kt, t0, 512), s=(kt == 0), p=(kt == 7):
                                   e.matmul(o, l, r, start=s, stop=p),
                               waits=[xn2] if kt == 0 else [], inc="pe" if kt == 7 else None)
                    bk3 = C.ps.acquire()
                    mt3 = None
                    for kt in range(8):
                        mt3 = b.emit("tensor",
                                     lambda e, o=bk3["t"][0:128, 0:512],
                                            l=dstB[:, kt * 512 + jj: kt * 512 + jj + 128],
                                            r=fms(C.xn, kt, t0, 512), s=(kt == 0), p=(kt == 7):
                                         e.matmul(o, l, r, start=s, stop=p),
                                     inc="pe" if kt == 7 else None)
                    lastpe = mt3
                    sl = b.emit("scalar",
                                lambda e, o=C.gel[:, 0:512], s=bk3["t"][0:128, 0:512]:
                                    e.activation(o, s, AF.Silu),
                                waits=[mt3, C.gel_free[0]], inc="act")
                    hv = b.emit("vector",
                                lambda e, o=C.ab2[:, j * 512: j * 512 + 512],
                                       s=C.gel[:, 0:512], p1=bk1["t"][0:128, 0:512]:
                                    e.tensor_tensor(o, p1, s, op=ALU.mult),
                                waits=[sl], inc="dve")
                    C.gel_free[0] = hv
                    bk1["free"] = hv
                    bk3["free"] = hv
                    C.ghid_tok = hv
            C.wslot_tok[0] = lastpe
            C.wslot_tok[1] = lastpe
        def evac_fc2(j, t0, tw, bank, mmtok, mw):
            return b.emit("vector",
                          lambda e, d=fms(C.xres, j, t0, tw), s=bank["t"][0:mw, 0:tw]:
                              e.tensor_tensor(d, d, s, op=ALU.add),
                          waits=[mmtok], inc="dve")
        gemm(b, C, w=W["fc2bf"].ap(), Cout=1024,
             rhs=lambda kt, t0, tw: (C.ab2[:, kt * 512: kt * 512 + tw], C.ghid_tok),
             evac=evac_fc2, jblock=128,
             tchunks=[(tc2, 512)], wwait=W.get("tok_fc2"))
    return b.tok("dve")

def setup_u1(b, C, nc, inp, newt):
    W = {}
    W["inprojbf"] = newt("u1_inproj", [1024, 4256], BF16)
    W["tok_inproj"] = convert_weight(b, C, inp["hl_u1_inproj"], W["inprojbf"])
    W["outprojbf"] = newt("u1_outproj", [2048, 1024], BF16)
    W["tok_outproj"] = convert_weight(b, C, inp["hl_u1_outproj"], W["outprojbf"])
    W["fc1bf"] = newt("u1_fc1", [1024, 8192], BF16)
    W["tok_fc1"] = convert_weight(b, C, inp["hl_u1_fc1"], W["fc1bf"])
    W["fc2bf"] = newt("u1_fc2", [4096, 1024], BF16)
    W["tok_fc2"] = convert_weight(b, C, inp["hl_u1_fc2"], W["fc2bf"])
    W["convwT_f32"] = inp["hl_u1_convwT"]
    W["conv_b_f32"] = inp["L0_attn_a_conv_b"]
    W["dt_bias_f32"] = inp["L0_attn_a_dt_bias"]
    W["A_log_f32"] = inp["L0_attn_a_A_log"]
    W["D_f32"] = inp["L0_attn_a_D"]
    W["ssm_norm_f32"] = inp["L0_attn_a_ssm_norm_w"]
    W["rms_w_f32"] = inp["L0_attn_a_rms_w"]
    return W

UNIT_INPUTS["u1"] = ["hl_u1_inproj", "hl_u1_outproj", "hl_u1_fc1", "hl_u1_fc2",
                     "hl_u1_convwT", "L0_attn_a_conv_b", "L0_attn_a_dt_bias",
                     "L0_attn_a_A_log", "L0_attn_a_D", "L0_attn_a_ssm_norm_w",
                     "L0_attn_a_rms_w"]
UNIT_SETUP["u1"] = setup_u1
UNIT_EMIT["u1"] = emit_u1
INPUT_SHAPES.update({
    "hl_u1_inproj": [1024, 4256], "hl_u1_outproj": [2048, 1024],
    "hl_u1_fc1": [1024, 8192], "hl_u1_fc2": [4096, 1024], "hl_u1_convwT": [4, 2176],
    "L0_attn_a_conv_b": [2176], "L0_attn_a_dt_bias": [32], "L0_attn_a_A_log": [32],
    "L0_attn_a_D": [32], "L0_attn_a_ssm_norm_w": [2048], "L0_attn_a_rms_w": [1024],
})

HOST_LAYOUT = {
    "hl_u1_inproj": ("L0_attn_a_in_proj_w", lambda w: np.ascontiguousarray(w.T)),
    "hl_u1_outproj": ("L0_attn_a_out_proj_w", lambda w: np.ascontiguousarray(w.T)),
    "hl_u1_fc1": ("L0_attn_a_fc1_w", lambda w: np.ascontiguousarray(w.T)),
    "hl_u1_fc2": ("L0_attn_a_fc2_w", lambda w: np.ascontiguousarray(w.T)),
    "hl_u1_convwT": ("L0_attn_a_conv_w", lambda w: np.ascontiguousarray(w.T)),
}
